# version 05
# baseline (speedup 1.0000x reference)
import numpy as np
import ml_dtypes
from contextlib import ExitStack
import concourse.bass as bass
import concourse.mybir as mybir
from concourse.bass_utils import run_bass_kernel_spmd

F32 = mybir.dt.float32
BF16 = mybir.dt.bfloat16
AF = mybir.ActivationFunctionType
ALU = mybir.AluOpType
AX = mybir.AxisListType
BFDT = ml_dtypes.bfloat16

MAGIC = 12582912.0

_DEBUG_B = False
_DEBUG_A2 = False
_DEBUG_C = False
_A1_STRIPES = 4
_A1_CHUNKS = 16
_A1_LEVEL = 9
_A1_L6SUB = 9
_DEBUG_D1 = False
_A1_SERIAL = False
_SERIAL = False
_SERIAL_LN = False
_FLUSH_N = 1

ENGINES = ("sync", "gpsimd", "tensor", "vector", "scalar")


class Sched:

    HAZ = 4

    def __init__(self, nc, stack):
        self.nc = nc
        self.stack = stack
        self.ops = {e: [] for e in ENGINES}
        self.sems = {}
        self.cnt = {}
        self.serial = False
        self.last = None
        for e in ENGINES:
            self._mksem(e)

    def _mksem(self, name):
        if name not in self.sems:
            self.sems[name] = self.stack.enter_context(self.nc.semaphore("S" + name))
            self.cnt[name] = 0

    def emit(self, engine, fn, waits=(), sem=None, amt=None):
        if sem is None:
            sem = engine
        self._mksem(sem)
        if amt is None:
            amt = 16 if engine in ("sync", "gpsimd") else 1
        self.cnt[sem] += amt
        tok = (sem, self.cnt[sem])
        w = [x for x in waits if x is not None]
        if self.serial and self.last is not None:
            w = w + [self.last]
        self.last = tok
        self.ops[engine].append((fn, w, sem, amt, tok))
        return tok

    def dma(self, engine, out, in_, waits=(), transpose=False, sem=None):
        def fn(eng, out=out, in_=in_, transpose=transpose):
            if transpose:
                return eng.dma_start(out=out, in_=in_, transpose=True)
            return eng.dma_start(out=out, in_=in_)
        return self.emit(engine, fn, waits, sem=sem, amt=16)

    def set_flush(self, lhsT_row, rhs_row, scratch_bank):
        self._flush = (lhsT_row, rhs_row, scratch_bank)

    def flush(self, waits=()):
        lhsT, rhs, bank = self._flush
        return self.emit("tensor", lambda e: e.matmul(
            bank[:, 0:128], lhsT=lhsT, rhs=rhs, start=True, stop=True),
            waits=list(waits))

    def barrier(self, engine, toks):
        def fn(eng):
            return None
        return self.emit(engine, fn, waits=list(toks))

    def build(self, block):
        for engine in ENGINES:
            if self.ops[engine]:
                self._one(block, engine)

    def _one(self, block, engine):
        ops = self.ops[engine]
        sems = self.sems

        def body(eng):
            watermark = {}
            recent = []
            for fn, waits, sem, amt, tok in ops:
                for (wsem, wval) in waits:
                    if wsem == engine and (wsem, wval) not in recent:
                        continue
                    if watermark.get(wsem, -1) >= wval:
                        continue
                    eng.wait_ge(sems[wsem], wval)
                    watermark[wsem] = wval
                inst = fn(eng)
                if inst is not None:
                    inst.then_inc(sems[sem], amt)
                recent.append(tok)
                if len(recent) > Sched.HAZ:
                    recent.pop(0)

        getattr(block, engine)(body)



def emit_layernorm(sched, r_ap, out_ap, bn6, bn2, scr, eps, deps, prev=(None, None)):
    D = r_ap.shape[-1]
    halves = [(0, min(512, D))] + ([(512, D)] if D > 512 else [])
    toks = []
    for i, (a, b) in enumerate(halves):
        toks.append(sched.emit(
            "vector", lambda e, i=i, a=a, b=b: e.bn_stats(bn6[:, i * 6:(i + 1) * 6], r_ap[:, a:b]),
            waits=list(deps)))
    t2 = sched.emit("vector", lambda e, n=len(halves): e.bn_aggr(bn2[:], bn6[:, 0:6 * n]),
                    waits=toks)
    t2b = sched.emit("vector", lambda e: e.tensor_scalar(out=scr[:, 3:4], in0=bn2[:, 1:2],
                                                         scalar1=float(eps), scalar2=None,
                                                         op0=ALU.add), waits=[t2, prev[0]])
    t3 = sched.emit("scalar", lambda e: e.activation(out=scr[:, 0:1], in_=scr[:, 3:4],
                                                     func=AF.Ln), waits=[t2b])
    t4 = sched.emit("scalar", lambda e: e.activation(out=scr[:, 1:2], in_=scr[:, 0:1],
                                                     func=AF.Exp, scale=-0.5), waits=[t3])
    t5 = sched.emit("vector", lambda e: e.tensor_scalar(out=scr[:, 2:3], in0=bn2[:, 0:1],
                                                        scalar1=scr[:, 1:2], scalar2=-1.0,
                                                        op0=ALU.mult, op1=ALU.mult),
                    waits=[t4, prev[1]])
    t6 = sched.emit("scalar", lambda e: e.activation(out=out_ap, in_=r_ap,
                                                     func=AF.Identity, bias=scr[:, 2:3],
                                                     scale=scr[:, 1:2]), waits=[t5])
    return t6, (t3, t6)



def build_B():
    nc = bass.Bass("TRN2", target_bir_lowering=False)
    NT = 16
    x1 = nc.dram_tensor("x1", [2048, 1024], F32, kind="ExternalInput").ap()
    G1r = nc.dram_tensor("G1r", [8, 16, 32], F32, kind="ExternalInput").ap()
    G1i = nc.dram_tensor("G1i", [8, 16, 32], F32, kind="ExternalInput").ap()
    G2r = nc.dram_tensor("G2r", [16, 4, 16], F32, kind="ExternalInput").ap()
    G2i = nc.dram_tensor("G2i", [16, 4, 16], F32, kind="ExternalInput").ap()
    biasri = nc.dram_tensor("biasri", [1, 1024], F32, kind="ExternalInput").ap()
    seld = nc.dram_tensor("seld", [8, 128], F32, kind="ExternalInput").ap()
    selg = nc.dram_tensor("selg", [16, 128], F32, kind="ExternalInput").ap()
    onesr = nc.dram_tensor("onesr", [1, 128], F32, kind="ExternalInput").ap()
    x2 = nc.dram_tensor("x2", [2048, 1024], F32, kind="ExternalOutput").ap()
    x2r = x2.rearrange("(t p) d -> p t d", p=128)
    dbgW = nc.dram_tensor("dbgW", [128, 8, 1024], BF16, kind="ExternalOutput").ap() if _DEBUG_B else None

    with ExitStack() as stack:
        sched = Sched(nc, stack)
        sched.serial = _SERIAL
        sb = lambda n, s, d=F32: stack.enter_context(nc.sbuf_tensor(n, s, d))
        ps = lambda n, s, d=F32: stack.enter_context(nc.psum_tensor(n, s, d))

        sX = sb("sX", [128, NT, 1024])
        sXb = sb("sXb", [128, NT, 1024], BF16)
        sFM = sb("sFM", [128, 8, 2048], BF16)
        sG1 = sb("sG1", [8, 16, 32]); sG1i = sb("sG1i", [8, 16, 32])
        sG2 = sb("sG2", [16, 4, 16]); sG2i = sb("sG2i", [16, 4, 16])
        sG1b = sb("sG1b", [8, 16, 32], BF16); sG1ib = sb("sG1ib", [8, 16, 32], BF16)
        sG2b = sb("sG2b", [16, 4, 16], BF16); sG2ib = sb("sG2ib", [16, 4, 16], BF16)
        sSelD = sb("sSelD", [8, 128]); sSelDb = sb("sSelDb", [128, 128], BF16)
        sSelG = sb("sSelG", [16, 128]); sSelGb = sb("sSelGb", [128, 128], BF16)
        sOnes = sb("sOnes", [1, 128]); sOnesB = sb("sOnesB", [1, 128], BF16)
        sBias = sb("sBias", [1, 1024])
        sBiasPad = sb("sBiasPad", [128, 1024], BF16)
        sE0 = sb("sE0", [128, 128], BF16)
        sA8e = sb("sA8e", [128, 512], BF16)
        sB16e = sb("sB16e", [128, 512], BF16)
        sP1 = sb("sP1", [128, 512])
        sWr = [sb(f"sWr{q}", [128, 512]) for q in range(4)]
        sWi = [sb(f"sWi{q}", [128, 512]) for q in range(4)]
        sTmp = sb("sTmp", [128, 512])
        sRhs = sb("sRhs", [128, 8, 1024], BF16)
        sBn6 = sb("sBn6", [128, 12]); sBn2 = sb("sBn2", [128, 2]); sScr = sb("sScr", [128, 4])
        sOut = [sb("sOutA", [128, 1024]), sb("sOutB", [128, 1024])]
        pA = ps("pA", [128, 512])
        pB = ps("pB", [128, 512])
        pC = ps("pC", [128, 512])
        pD = ps("pD", [128, 512])
        pF = ps("pF", [128, 512])

        sched.dma("sync", sX[:], x1.rearrange("(t p) d -> p t d", p=128), sem="ld")
        sched.dma("sync", sG1[:], G1r, sem="ld")
        sched.dma("sync", sG1i[:], G1i, sem="ld")
        sched.dma("sync", sG2[:], G2r, sem="ld")
        sched.dma("sync", sG2i[:], G2i, sem="ld")
        sched.dma("sync", sSelD[:], seld, sem="ld")
        sched.dma("sync", sSelG[:], selg, sem="ld")
        sched.dma("sync", sOnes[:], onesr, sem="ld")
        sched.dma("sync", sBias[:], biasri, sem="ld")
        ld_all = ("ld", sched.cnt["ld"])

        z1 = sched.emit("gpsimd", lambda e: e.memset(sSelDb[:], 0.0), sem="zz")
        sched.emit("gpsimd", lambda e: e.memset(sSelGb[:], 0.0), sem="zz")
        sched.emit("gpsimd", lambda e: e.memset(sA8e[:], 0.0), sem="zz")
        sched.emit("gpsimd", lambda e: e.memset(sB16e[:], 0.0), sem="zz")
        sched.emit("gpsimd", lambda e: e.memset(sBiasPad[:], 0.0), sem="zz")
        sched.emit("gpsimd", lambda e: e.memset(sE0[:], 0.0), sem="zz")
        zz_all = ("zz", sched.cnt["zz"])

        casts = []
        for dst, src in [(sG1b, sG1), (sG1ib, sG1i), (sG2b, sG2),
                         (sG2ib, sG2i), (sOnesB, sOnes)]:
            casts.append(sched.emit("vector",
                                    lambda e, d=dst, s=src: e.tensor_copy(d[:], s[:]),
                                    waits=[ld_all]))
        casts.append(sched.emit("vector", lambda e: e.tensor_copy(sSelDb[0:8, :], sSelD[:]),
                                waits=[ld_all, zz_all]))
        casts.append(sched.emit("vector", lambda e: e.tensor_copy(sSelGb[0:16, :], sSelG[:]),
                                waits=[ld_all, zz_all]))
        casts.append(sched.emit("vector", lambda e: e.tensor_copy(sBiasPad[0:1, :], sBias[:]),
                                waits=[ld_all, zz_all]))
        casts.append(sched.emit("vector", lambda e: e.tensor_copy(sE0[0:1, :], sOnes[:]),
                                waits=[ld_all, zz_all]))
        casts.append(zz_all)

        for t in range(NT):
            tc = sched.emit("vector", lambda e, t=t: e.tensor_copy(sXb[:, t, :], sX[:, t, :]),
                            waits=[ld_all])
            sched.dma("sync", sFM[:, :, t * 128:(t + 1) * 128], sXb[:, t, :],
                      waits=[tc], transpose=True, sem="tr")
        tFM = ("tr", sched.cnt["tr"])

        wq_done = []
        ch = None
        for which in range(2):
            g1t = sG1b if which == 0 else sG1ib
            g2t = sG2b if which == 0 else sG2ib
            wdst = sWr if which == 0 else sWi
            for q in range(4):
                for c in range(4):
                    ch = sched.emit("vector", lambda e, c=c, q=q, g1t=g1t: e.tensor_copy(
                        sA8e[0:8, :], g1t[:, q * 4 + c, :, None]
                        .broadcast_to([8, 32, 16])), waits=casts + [ch])
                    ch = sched.emit("vector", lambda e, c=c, g2t=g2t: e.tensor_copy(
                        sB16e[0:16, :], g2t[:, c, None, :]
                        .broadcast_to([16, 32, 16])), waits=casts + [ch])
                    ch = sched.emit("tensor", lambda e: e.matmul(
                        pA[:, :], lhsT=sSelDb[:], rhs=sA8e[:], start=True, stop=True),
                        waits=casts + [ch])
                    ch = sched.emit("tensor", lambda e: e.matmul(
                        pB[:, :], lhsT=sSelGb[:], rhs=sB16e[:], start=True, stop=True),
                        waits=casts + [ch])
                    ch = sched.emit("vector", lambda e: e.tensor_copy(sP1[:], pA[:, :]),
                                    waits=[ch])
                    ch = sched.emit("vector", lambda e: e.tensor_tensor(
                        out=sTmp[:], in0=sP1[:], in1=pB[:, :], op=ALU.mult), waits=[ch])
                    if c == 0:
                        ch = sched.emit("vector", lambda e, w=wdst[q]: e.tensor_copy(
                            w[:], sTmp[:]), waits=[ch])
                    else:
                        ch = sched.emit("vector", lambda e, w=wdst[q]: e.tensor_tensor(
                            out=w[:], in0=w[:], in1=sTmp[:], op=ALU.add), waits=[ch])
                wq_done.append(ch)

        asm = []
        for q in range(4):
            asm.append(sched.emit("vector", lambda e, q=q: e.tensor_copy(
                sRhs[:, q, 0:512], sWr[q][:]), waits=wq_done))
            asm.append(sched.emit("vector", lambda e, q=q: e.tensor_copy(
                sRhs[:, q, 512:1024], sWi[q][:]), waits=wq_done))
            asm.append(sched.emit("vector", lambda e, q=q: e.tensor_scalar(
                out=sRhs[:, q + 4, 0:512], in0=sWi[q][:], scalar1=-1.0, scalar2=None,
                op0=ALU.mult), waits=wq_done))
            asm.append(sched.emit("vector", lambda e, q=q: e.tensor_copy(
                sRhs[:, q + 4, 512:1024], sWr[q][:]), waits=wq_done))

        if _DEBUG_B:
            sched.dma("sync", dbgW, sRhs[:], waits=asm, sem="stA")

        bank_free = {0: None, 1: None}
        out_free = [None, None]
        prev_bn = (None, None)
        store_toks = []
        sched.set_flush(sE0[:], sBiasPad[:, 0:128], pF)
        for t in range(NT):
            buf = sOut[t % 2]
            tr_toks = []
            for half, bank in ((0, pC), (1, pD)):
                deps = asm + [tFM]
                if bank_free[half]:
                    deps = deps + [bank_free[half]]
                if out_free[t % 2]:
                    deps = deps + [out_free[t % 2]]
                for ic in range(8):
                    sched.emit("tensor",
                               lambda e, t=t, ic=ic, half=half, bank=bank: e.matmul(
                                   bank[:, :], lhsT=sFM[:, ic, t * 128:(t + 1) * 128],
                                   rhs=sRhs[:, ic, half * 512:(half + 1) * 512],
                                   start=(ic == 0), stop=False), waits=deps)
                sched.emit("tensor", lambda e, half=half, bank=bank: e.matmul(
                    bank[:, :], lhsT=sE0[:], rhs=sBiasPad[:, half * 512:(half + 1) * 512],
                    start=False, stop=True), waits=deps)
            tfl = sched.flush()
            for _ in range(_FLUSH_N - 1):
                tfl = sched.flush()
            for half, bank in ((0, pC), (1, pD)):
                tr = sched.emit("vector", lambda e, t=t, half=half, bank=bank, buf=buf:
                                e.tensor_tensor(out=buf[:, half * 512:(half + 1) * 512],
                                                in0=sX[:, t, half * 512:(half + 1) * 512],
                                                in1=bank[:, :], op=ALU.add),
                                waits=[tfl] + ([out_free[t % 2]] if out_free[t % 2] else []))
                bank_free[half] = tr
                tr_toks.append(tr)
            if _SERIAL_LN:
                sched.serial = True
            tln, tbn = emit_layernorm(sched, buf[:], buf[:], sBn6, sBn2, sScr, 1e-5,
                                      deps=tr_toks, prev=prev_bn)
            sched.serial = _SERIAL
            prev_bn = tbn
            tst = sched.dma("sync", x2r[:, t, :], buf[:], waits=[tln],
                            sem="stA" if t % 2 == 0 else "stB")
            out_free[t % 2] = tst
            store_toks.append(tst)

        sched.barrier("gpsimd", [("stA", sched.cnt["stA"]), ("stB", sched.cnt["stB"])])
        block = stack.enter_context(nc.Block())
        sched.build(block)
    return nc


def host_B_inputs(x1_shard, G1_R, G2_R, G1_I, G2_I, bias_real, bias_imag):
    p = np.arange(128)
    seld = (p[None, :] // 16 == np.arange(8)[:, None]).astype(np.float32)
    selg = (p[None, :] % 16 == np.arange(16)[:, None]).astype(np.float32)
    def packA(G1):
        A = np.transpose(np.asarray(G1)[0], (2, 1, 0))
        A = A.reshape(4, 8, 4, 32).transpose(1, 0, 2, 3)
        return np.ascontiguousarray(A.reshape(8, 16, 32), np.float32)

    def packB(G2):
        B = np.transpose(np.asarray(G2)[:, :, 0, :], (2, 0, 1))
        return np.ascontiguousarray(B, np.float32)

    return {
        "x1": np.ascontiguousarray(x1_shard, np.float32),
        "G1r": packA(G1_R), "G1i": packA(G1_I),
        "G2r": packB(G2_R), "G2i": packB(G2_I),
        "biasri": np.concatenate([np.asarray(bias_real), np.asarray(bias_imag)])[None, :].astype(np.float32),
        "seld": seld, "selg": selg,
        "onesr": np.ones((1, 128), np.float32),
    }



def build_A2():
    nc = bass.Bass("TRN2", target_bir_lowering=False)
    NT = 16
    DFF = 2736
    NS = 22
    slabw = [128] * 21 + [48]
    x = nc.dram_tensor("x", [2048, 1024], F32, kind="ExternalInput").ap()
    oT = nc.dram_tensor("oT", [1024, 2048], BF16, kind="ExternalInput").ap()
    Wo = nc.dram_tensor("Wo", [1024, 1024], F32, kind="ExternalInput").ap()
    w1 = nc.dram_tensor("w1", [1024, DFF], F32, kind="ExternalInput").ap()
    w2 = nc.dram_tensor("w2", [1024, DFF], F32, kind="ExternalInput").ap()
    w3 = nc.dram_tensor("w3", [DFF, 1024], F32, kind="ExternalInput").ap()
    gff = nc.dram_tensor("gff", [128, 8], F32, kind="ExternalInput").ap()
    onesr = nc.dram_tensor("onesr", [1, 128], F32, kind="ExternalInput").ap()
    x1o = nc.dram_tensor("x1o", [2048, 1024], F32, kind="ExternalOutput").ap()
    x1r = x1o.rearrange("(t p) d -> p t d", p=128)
    if _DEBUG_A2:
        dbg1 = nc.dram_tensor("dbg1", [128, 16, 1024], F32, kind="ExternalOutput").ap()
        dbg2 = nc.dram_tensor("dbg2", [128, 8, 2048], BF16, kind="ExternalOutput").ap()

    with ExitStack() as stack:
        sched = Sched(nc, stack)
        sched.serial = _SERIAL
        sb = lambda n, s, d=F32: stack.enter_context(nc.sbuf_tensor(n, s, d))
        ps = lambda n, s, d=F32: stack.enter_context(nc.psum_tensor(n, s, d))

        sR = sb("sR", [128, NT, 1024])
        sU = sb("sU", [128, 16384], BF16)
        sOT = sU[:, 0:16384].rearrange("p (c t) -> p c t", c=8)
        sP = sU[:, 0:11264].rearrange("p (c t) -> p c t", c=22)
        sStage = sb("sStage", [128, 8, 128])
        sStage2 = sb("sStage2", [128, 8, 128])
        sStage3 = sStage[:].rearrange("p c o -> p (c o)")
        sStage4 = sStage2[:].rearrange("p c o -> p (c o)")
        sWoB = sb("sWoB", [128, 8, 1024], BF16)
        sW3B = sb("sW3B", [128, 22, 1024], BF16)
        sW1B = [sb("sW1Ba", [128, 8, 128], BF16), sb("sW1Bb", [128, 8, 128], BF16)]
        sW2B = [sb("sW2Ba", [128, 8, 128], BF16), sb("sW2Bb", [128, 8, 128], BF16)]
        sXF = sb("sXF", [128, 8, 2048], BF16)
        sXT = sb("sXT", [128, 1024], BF16)
        sSilu = sXT[:, 0:512]
        sUp = sXT[:, 512:1024]
        sG = sb("sG", [128, 8])
        sOnes = sb("sOnes", [1, 128])
        sE0 = sb("sE0", [128, 128], BF16)
        sBn6 = sb("sBn6", [128, 12]); sBn2 = sb("sBn2", [128, 2]); sScr = sb("sScr", [128, 6])
        _wof = sWoB[:].rearrange("p c o -> p (c o)")
        sOut = [_wof[:, 0:2048].bitcast(F32), _wof[:, 2048:4096].bitcast(F32)]
        sXr = [_wof[:, 4096:6144].bitcast(F32), _wof[:, 6144:8192].bitcast(F32)]
        pA = ps("pA", [128, 512])
        pB = ps("pB", [128, 512])
        pF = ps("pF", [128, 512])

        sched.dma("sync", sR[:], x.rearrange("(t p) d -> p t d", p=128), sem="ld")
        sched.dma("sync", sOT, oT.rearrange("(c p) t -> p c t", p=128), sem="ld")
        sched.dma("sync", sG[:], gff, sem="ld")
        sched.dma("sync", sOnes[:], onesr, sem="ld")
        ld_all = ("ld", sched.cnt["ld"])
        z1 = sched.emit("gpsimd", lambda e: e.memset(sE0[:], 0.0), sem="zz")
        z2 = sched.emit("gpsimd", lambda e: e.memset(sP[:, 21, :], 0.0), sem="zz")
        z3 = sched.emit("gpsimd", lambda e: e.memset(sW3B[:, 21, :], 0.0), sem="zz")
        zz_all = ("zz", sched.cnt["zz"])
        tE0 = sched.emit("vector", lambda e: e.tensor_copy(sE0[0:1, :], sOnes[:]),
                         waits=[ld_all, zz_all])

        cast_wo = []
        for j in range(8):
            stg = sStage if j % 2 == 0 else sStage2
            sched.dma("sync", stg[:],
                      Wo.rearrange("(c p) o -> p c o", p=128)[:, :, j * 128:(j + 1) * 128],
                      sem=f"wo{j % 2}",
                      waits=[cast_wo[j - 2]] if j >= 2 else [])
            tc = sched.emit("vector", lambda e, j=j, stg=stg: e.tensor_copy(
                sWoB[:, :, j * 128:(j + 1) * 128], stg[:]),
                waits=[(f"wo{j % 2}", sched.cnt[f"wo{j % 2}"])])
            cast_wo.append(tc)
        cast_w3 = []
        for c in range(NS):
            stg = sStage3 if c % 2 == 0 else sStage4
            rows = slabw[c]
            sched.dma("sync", stg[0:rows, :], w3[c * 128:c * 128 + rows, :],
                      sem=f"w3{c % 2}",
                      waits=[cast_w3[c - 2]] if c >= 2 else [cast_wo[-2], cast_wo[-1]])
            tc = sched.emit("vector", lambda e, c=c, stg=stg, rows=rows: e.tensor_copy(
                sW3B[0:rows, c, :], stg[0:rows, :]),
                waits=[(f"w3{c % 2}", sched.cnt[f"w3{c % 2}"]), zz_all])
            cast_w3.append(tc)
        w3_ready = [cast_w3[-1]]

        sched.set_flush(sE0[:], sE0[:], pF)

        bank_free = {0: None, 1: None}
        prev_rms = None
        prev_tt = None
        for t in range(NT):
            mm_deps = [cast_wo[-1], ld_all, tE0]
            for half, bank in ((0, pA), (1, pB)):
                d = mm_deps + ([bank_free[half]] if bank_free[half] else [])
                for ic in range(8):
                    sched.emit("tensor", lambda e, t=t, ic=ic, half=half, bank=bank: e.matmul(
                        bank[:, :], lhsT=sOT[:, ic, t * 128:(t + 1) * 128],
                        rhs=sWoB[:, ic, half * 512:(half + 1) * 512],
                        start=(ic == 0), stop=(ic == 7)), waits=d)
            tfl = sched.flush()
            radd = []
            for half, bank in ((0, pA), (1, pB)):
                tr = sched.emit("vector", lambda e, t=t, half=half, bank=bank: e.tensor_tensor(
                    out=sR[:, t, half * 512:(half + 1) * 512],
                    in0=sR[:, t, half * 512:(half + 1) * 512],
                    in1=bank[:, :], op=ALU.add), waits=[tfl])
                bank_free[half] = tr
                radd.append(tr)
            s1 = sched.emit("vector", lambda e, t=t: e.bn_stats(sBn6[:, 0:6], sR[:, t, 0:512]),
                            waits=radd)
            s2 = sched.emit("vector", lambda e, t=t: e.bn_stats(sBn6[:, 6:12], sR[:, t, 512:1024]),
                            waits=radd)
            s3 = sched.emit("vector", lambda e: e.bn_aggr(sBn2[:], sBn6[:]), waits=[s1, s2])
            s4 = sched.emit("vector", lambda e: e.tensor_scalar(
                out=sScr[:, 0:1], in0=sBn2[:, 0:1], scalar1=sBn2[:, 0:1], scalar2=sBn2[:, 1:2],
                op0=ALU.mult, op1=ALU.add), waits=[s3, prev_rms])
            s5 = sched.emit("vector", lambda e: e.tensor_scalar(
                out=sScr[:, 1:2], in0=sScr[:, 0:1], scalar1=1e-6, scalar2=None,
                op0=ALU.add), waits=[s4])
            t3 = sched.emit("scalar", lambda e: e.activation(out=sScr[:, 2:3], in_=sScr[:, 1:2],
                                                             func=AF.Ln), waits=[s5])
            t4 = sched.emit("scalar", lambda e: e.activation(out=sScr[:, 3:4], in_=sScr[:, 2:3],
                                                             func=AF.Exp, scale=-0.5), waits=[t3])
            prev_rms = t4
            t5 = sched.emit("scalar", lambda e, t=t: e.activation(
                out=sXT[:], in_=sR[:, t, :], func=AF.Identity, scale=sScr[:, 3:4]),
                waits=[t4, prev_tt])
            prev_tt = sched.dma("sync", sXF[:, :, t * 128:(t + 1) * 128], sXT[:],
                                waits=[t5], transpose=True, sem="trx")
        txf = ("trx", sched.cnt["trx"])
        gf = []
        for c in range(8):
            gf.append(sched.emit("vector", lambda e, c=c: e.tensor_scalar(
                out=sXF[:, c, :], in0=sXF[:, c, :], scalar1=sG[:, c:c + 1], scalar2=None,
                op0=ALU.mult), waits=[txf, ld_all]))
        gf = [gf[-1]]

        if _DEBUG_A2:
            sched.dma("sync", dbg1, sR[:], waits=gf, sem="stA")
            sched.dma("sync", dbg2, sXF[:], waits=gf, sem="stA")

        ph1_pe = ("tensor", sched.cnt["tensor"])
        xr_free = [ph1_pe, ph1_pe]
        out_free = [None, None]
        prev_bn = (None, None)
        pfree = None
        wbuf_free = [cast_wo[-1], cast_wo[-1]]
        e1_prev = e2_prev = None
        for q4 in range(4):
            tok0 = q4 * 512
            prod_toks = []
            for sl in range(NS):
                rows = slabw[sl]
                par = sl % 2
                d0 = [wbuf_free[par]] if wbuf_free[par] else []
                sched.dma("sync", sStage[:, :, 0:rows],
                          w1[:, sl * 128:sl * 128 + rows].rearrange("(c p) o -> p c o", p=128),
                          sem=f"sw{par}", waits=d0)
                sched.dma("sync", sStage2[:, :, 0:rows],
                          w2[:, sl * 128:sl * 128 + rows].rearrange("(c p) o -> p c o", p=128),
                          sem=f"sw{par}", waits=d0)
                tld = (f"sw{par}", sched.cnt[f"sw{par}"])
                c1 = sched.emit("vector", lambda e, par=par, rows=rows: e.tensor_copy(
                    sW1B[par][:, :, 0:rows], sStage[:, :, 0:rows]), waits=[tld])
                c2 = sched.emit("vector", lambda e, par=par, rows=rows: e.tensor_copy(
                    sW2B[par][:, :, 0:rows], sStage2[:, :, 0:rows]), waits=[tld])
                wbuf_free[par] = c2
                mmdeps = gf + [c1, c2]
                for ic in range(8):
                    sched.emit("tensor", lambda e, ic=ic, tok0=tok0, par=par:
                               e.matmul(pA[:, :], lhsT=sW1B[par][:, ic, :],
                                        rhs=sXF[:, ic, tok0:tok0 + 512],
                                        start=(ic == 0), stop=(ic == 7)),
                               waits=mmdeps + ([e1_prev] if e1_prev else []))
                for ic in range(8):
                    sched.emit("tensor", lambda e, ic=ic, tok0=tok0, par=par:
                               e.matmul(pB[:, :], lhsT=sW2B[par][:, ic, :],
                                        rhs=sXF[:, ic, tok0:tok0 + 512],
                                        start=(ic == 0), stop=(ic == 7)),
                               waits=mmdeps + ([e2_prev] if e2_prev else []))
                tfl = sched.flush()
                e1 = sched.emit("scalar", lambda e, rows=rows: e.activation(
                    out=sSilu[0:rows, :], in_=pA[0:rows, :], func=AF.Silu),
                    waits=[tfl, pfree])
                e2 = sched.emit("scalar", lambda e, rows=rows: e.activation(
                    out=sUp[0:rows, :], in_=pB[0:rows, :], func=AF.Identity),
                    waits=[tfl, pfree])
                e1_prev, e2_prev = e1, e2
                tp = sched.emit("vector", lambda e, sl=sl, rows=rows: e.tensor_tensor(
                    out=sP[0:rows, sl, :], in0=sSilu[0:rows, :], in1=sUp[0:rows, :],
                    op=ALU.mult), waits=[e1, e2] + ([pfree] if pfree else []))
                prod_toks.append(tp)
            prod_toks = [prod_toks[-1]]
            for tt_ in range(4):
                t = q4 * 4 + tt_
                buf = sOut[t % 2]
                for half, bank in ((0, pA), (1, pB)):
                    dd = prod_toks + w3_ready + ([bank_free[half]] if bank_free[half] else [])
                    for c in range(NS):
                        sched.emit("tensor", lambda e, c=c, tt_=tt_, half=half, bank=bank:
                                   e.matmul(bank[:, :],
                                            lhsT=sP[:, c, tt_ * 128:(tt_ + 1) * 128],
                                            rhs=sW3B[:, c, half * 512:(half + 1) * 512],
                                            start=(c == 0), stop=(c == NS - 1)), waits=dd)
                tfl = sched.flush()
                xrb = sXr[t % 2]
                txr = sched.dma("sync", xrb, x.rearrange("(t p) d -> p t d", p=128)[:, t, :],
                                sem=f"xr{t % 2}",
                                waits=[xr_free[t % 2]] if xr_free[t % 2] else [])
                tr_toks = []
                for half, bank in ((0, pA), (1, pB)):
                    tr = sched.emit("vector", lambda e, t=t, half=half, bank=bank, buf=buf:
                                    e.tensor_tensor(out=buf[:, half * 512:(half + 1) * 512],
                                                    in0=sR[:, t, half * 512:(half + 1) * 512],
                                                    in1=bank[:, :], op=ALU.add),
                                    waits=[tfl] + ([out_free[t % 2]] if out_free[t % 2] else []))
                    bank_free[half] = tr
                    tr2 = sched.emit("vector", lambda e, half=half, buf=buf, xrb=xrb:
                                     e.tensor_tensor(out=buf[:, half * 512:(half + 1) * 512],
                                                     in0=buf[:, half * 512:(half + 1) * 512],
                                                     in1=xrb[:, half * 512:(half + 1) * 512],
                                                     op=ALU.add), waits=[tr, txr])
                    tr_toks.append(tr2)
                xr_free[t % 2] = tr_toks[-1]
                tln, tbn = emit_layernorm(sched, buf, buf, sBn6, sBn2, sScr, 1e-5,
                                          deps=tr_toks, prev=prev_bn)
                prev_bn = tbn
                tst = sched.dma("sync", x1r[:, t, :], buf, waits=[tln],
                                sem="stA" if t % 2 == 0 else "stB")
                out_free[t % 2] = tst
            pfree = bank_free[1]
            e1_prev = bank_free[0]
            e2_prev = bank_free[1]

        sched.barrier("gpsimd", [("stA", sched.cnt["stA"]), ("stB", sched.cnt["stB"])])
        block = stack.enter_context(nc.Block())
        sched.build(block)
    return nc


def host_A2_inputs(x_shard, oT_shard, Wo, w1, w2, w3, g_ffn):
    return {
        "x": np.ascontiguousarray(x_shard, np.float32),
        "oT": np.ascontiguousarray(oT_shard).astype(BFDT),
        "Wo": np.ascontiguousarray(Wo, np.float32),
        "w1": np.ascontiguousarray(w1, np.float32),
        "w2": np.ascontiguousarray(w2, np.float32),
        "w3": np.ascontiguousarray(w3, np.float32),
        "gff": np.ascontiguousarray(np.asarray(g_ffn, np.float32).reshape(8, 128).T),
        "onesr": np.ones((1, 128), np.float32),
    }



def build_C():
    nc = bass.Bass("TRN2", target_bir_lowering=False)
    NTK = 32
    NTQ = 16
    x2kv = nc.dram_tensor("x2kv", [4096, 1024], F32, kind="ExternalInput").ap()
    WqT = nc.dram_tensor("WqT", [1024, 1024], F32, kind="ExternalInput").ap()
    WkT = nc.dram_tensor("WkT", [1024, 1024], F32, kind="ExternalInput").ap()
    WvT = nc.dram_tensor("WvT", [1024, 1024], F32, kind="ExternalInput").ap()
    WoT = nc.dram_tensor("WoT", [1024, 1024], F32, kind="ExternalInput").ap()
    bqc = nc.dram_tensor("bqc", [128, 8], F32, kind="ExternalInput").ap()
    bkc = nc.dram_tensor("bkc", [128, 8], F32, kind="ExternalInput").ap()
    bvr = nc.dram_tensor("bvr", [1, 1024], F32, kind="ExternalInput").ap()
    bor = nc.dram_tensor("bor", [1, 1024], F32, kind="ExternalInput").ap()
    onesb = nc.dram_tensor("onesb", [1, 128], BF16, kind="ExternalInput").ap()
    x3o = nc.dram_tensor("x3o", [2048, 1024], F32, kind="ExternalOutput").ap()
    x3r = x3o.rearrange("(t p) d -> p t d", p=128)

    with ExitStack() as stack:
        sched = Sched(nc, stack)
        sched.serial = _SERIAL
        sb = lambda n, s, d=F32: stack.enter_context(nc.sbuf_tensor(n, s, d))
        ps = lambda n, s, d=F32: stack.enter_context(nc.psum_tensor(n, s, d))

        sXF = sb("sXF", [128, 8, 4096], BF16)
        sK = sb("sK", [128, 4, 4096], BF16)
        sQ = sb("sQ", [128, 4, 2048], BF16)
        sV = sb("sV", [128, 32, 260], BF16)
        sOTb = sb("sOTb", [128, 8, 2048], BF16)
        sWoB = sb("sWoB", [128, 8, 1024], BF16)
        sWq = sb("sWq", [128, 8, 256], BF16)
        sWk = sb("sWk", [128, 8, 256], BF16)
        sWv = sb("sWv", [128, 8, 256], BF16)
        sStage = sb("sStage", [128, 8, 256])
        sXs = [sb("sXsA", [128, 1024]), sb("sXsB", [128, 1024])]
        _otf = sOTb[:].rearrange("p c t -> p (c t)")
        sXC = [_otf[:, 0:1024], _otf[:, 1024:2048]]
        sPT = [sb("sPTa", [128, 512], BF16), sb("sPTb", [128, 512], BF16)]
        sRec = sb("sRec", [128, 512], BF16)
        _stf = sStage[:].rearrange("p c o -> p (c o)")
        sRv = _stf[0:64, 0:512]
        sSr = _stf[0:1, 512:1024]
        sBoPad = _stf[:, 1024:1536].bitcast(BF16)
        sPT = sPT + [_stf[:, 1024:1280].bitcast(BF16), _stf[:, 1280:1536].bitcast(BF16)]
        sBq = sb("sBq", [128, 8]); sBk = sb("sBk", [128, 8])
        sBvR = sXs[0][:][0:1, :]
        sBoR = sXs[1][:][0:1, :]
        sBvPad = _stf[:, 1792:1920].bitcast(BF16)
        sE0 = sb("sE0", [128, 128], BF16)
        sBn6 = _stf[:, 1920:1932]; sBn2 = _stf[:, 1932:1934]; sScr = _stf[:, 1934:1940]
        sM1 = _stf[:, 1940:1946]
        sMW = _stf[:, 1946:1978]
        _qf = sQ[:].rearrange("p c t -> p (c t)")
        sOut = [_qf[:, 0:2048].bitcast(F32), _qf[:, 2048:4096].bitcast(F32)]
        sXr = [_qf[:, 4096:6144].bitcast(F32), _qf[:, 6144:8192].bitcast(F32)]
        pS = [ps("pS0", [128, 512]), ps("pS1", [128, 512])]
        pO = ps("pO", [128, 512])
        pO1 = ps("pO1", [128, 512])
        pR = ps("pR", [128, 512])
        pA = ps("pA", [128, 512])
        pB = ps("pB", [128, 512])
        pF = ps("pF", [128, 512])

        sched.dma("sync", sBq[:], bqc, sem="ld")
        sched.dma("sync", sBk[:], bkc, sem="ld")
        ld_all = ("ld", sched.cnt["ld"])
        sched.emit("gpsimd", lambda e: e.memset(sE0[:], 0.0), sem="zz")
        sched.emit("gpsimd", lambda e: e.memset(sK[:], 0.0), sem="zz")
        sched.emit("gpsimd", lambda e: e.memset(sQ[:], 0.0), sem="zz")
        sched.emit("gpsimd", lambda e: e.memset(sRec[:], 0.0), sem="zz")
        sched.emit("gpsimd", lambda e: e.memset(sM1, 0.0), sem="zz")
        for hh in range(4):
            sched.emit("gpsimd", lambda e, hh=hh: e.memset(
                sV[:, :, hh * 65 + 64:hh * 65 + 65], 1.0), sem="zz")
        zz_all = ("zz", sched.cnt["zz"])
        tE0 = sched.dma("sync", sE0[0:1, :], onesb, sem="ld", waits=[zz_all])
        tE0 = ("ld", sched.cnt["ld"])
        xc_free = [None, None]
        for t in range(NTK):
            par = t % 2
            tl = sched.dma("sync", sXs[par][:],
                           x2kv.rearrange("(t p) d -> p t d", p=128)[:, t, :],
                           sem=f"x{par}", waits=[xc_free[par]] if xc_free[par] else [])
            tc = sched.emit("vector", lambda e, par=par: e.tensor_copy(sXC[par], sXs[par][:]),
                            waits=[tl])
            ttr = sched.dma("sync", sXF[:, :, t * 128:(t + 1) * 128], sXC[par],
                            waits=[tc], transpose=True, sem="tr")
            xc_free[par] = ttr
        tXF = ("tr", sched.cnt["tr"])
        sched.dma("sync", sBvR, bvr, sem="ld2", waits=[tXF])
        sched.dma("sync", sBoR, bor, sem="ld2", waits=[tXF])
        rows_ld = ("ld2", sched.cnt["ld2"])

        cast_wo = []
        for j in range(4):
            sched.dma("sync", sStage[:],
                      WoT.rearrange("(c p) o -> p c o", p=128)[:, :, j * 256:(j + 1) * 256],
                      sem="wo", waits=[cast_wo[-1]] if cast_wo else [])
            cast_wo.append(sched.emit("vector", lambda e, j=j: e.tensor_copy(
                sWoB[:, :, j * 256:(j + 1) * 256], sStage[:]),
                waits=[("wo", sched.cnt["wo"])]))

        sched.set_flush(sE0[:], sE0[:], pF)

        g_gate = cast_wo[-1]
        wfree_prev = None
        att_done = None
        prev_exp_last = None
        prev_c3 = [None]
        prev_c5 = [None]
        att_sidx = [0]
        pend_tail = [None]
        tail_done = {}
        tail_read = {}

        def _emit_tail(sidx, pOb, slab, rbase, qp, avtok):
            c1 = sched.emit("vector", lambda e, pOb=pOb: e.tensor_copy(sSr, pOb[64:65, :]),
                            waits=[("tensor", sched.cnt["tensor"]), prev_c3[0], avtok])
            c2 = sched.emit("scalar", lambda e: e.activation(out=sSr, in_=sSr,
                                                             func=AF.Ln), waits=[c1])
            c3 = sched.emit("scalar", lambda e: e.activation(out=sRec[0:1, :],
                                                             in_=sSr, func=AF.Exp,
                                                             scale=-1.0), waits=[c2])
            prev_c3[0] = c3
            rmm = sched.emit("tensor", lambda e: e.matmul(
                pR[:, :], lhsT=sE0[:], rhs=sRec[:], start=True, stop=True), waits=[c3])
            fl2 = sched.flush()
            c4 = sched.emit("scalar", lambda e: e.activation(out=sRv, in_=pR[0:64, :],
                                                             func=AF.Identity),
                            waits=[fl2, prev_c5[0]])
            c5 = sched.emit("vector", lambda e, pOb=pOb, slab=slab, rbase=rbase, qp=qp:
                            e.tensor_tensor(
                                out=sOTb[rbase:rbase + 64, slab, qp * 512:(qp + 1) * 512],
                                in0=pOb[0:64, :], in1=sRv, op=ALU.mult), waits=[c4])
            prev_c5[0] = c5
            tail_done[sidx] = c5
            tail_read[sidx] = c5

        for g in range(4):
            wt_all = None
            for (wsrc, wdst) in ((WqT, sWq), (WkT, sWk), (WvT, sWv)):
                sched.dma("sync", sStage[:],
                          wsrc.rearrange("(c p) o -> p c o", p=128)[:, :, g * 256:(g + 1) * 256],
                          sem="wg", waits=[g_gate, wfree_prev, att_done])
                g_gate = sched.emit("vector", lambda e, wdst=wdst: e.tensor_copy(
                    wdst[:], sStage[:]), waits=[("wg", sched.cnt["wg"])])
            wt_all = g_gate
            gd = [wt_all, tXF, zz_all, ld_all, tE0]
            if att_done:
                gd.append(att_done)

            ev_prev = None
            for j in range(2):
                for piece in range(8):
                    for ic in range(8):
                        sched.emit("tensor", lambda e, j=j, piece=piece, ic=ic: e.matmul(
                            pA[:, :], lhsT=sWk[:, ic, j * 128:(j + 1) * 128],
                            rhs=sXF[:, ic, piece * 512:(piece + 1) * 512],
                            start=(ic == 0), stop=(ic == 7)),
                            waits=gd + ([ev_prev] if ev_prev else []))
                    tfl = sched.flush()
                    e1 = sched.emit("scalar", lambda e, g=g, j=j, piece=piece: e.activation(
                        out=sK[0:64, 2 * j, piece * 512:(piece + 1) * 512],
                        in_=pA[0:64, :], func=AF.Identity,
                        bias=sBk[0:64, g * 2 + j:g * 2 + j + 1]), waits=[tfl])
                    e2 = sched.emit("scalar", lambda e, g=g, j=j, piece=piece: e.activation(
                        out=sK[0:64, 2 * j + 1, piece * 512:(piece + 1) * 512],
                        in_=pA[64:128, :], func=AF.Identity,
                        bias=sBk[64:128, g * 2 + j:g * 2 + j + 1]), waits=[tfl])
                    ev_prev = e2
            for j in range(2):
                for piece in range(4):
                    for ic in range(8):
                        sched.emit("tensor", lambda e, j=j, piece=piece, ic=ic: e.matmul(
                            pA[:, :], lhsT=sWq[:, ic, j * 128:(j + 1) * 128],
                            rhs=sXF[:, ic, piece * 512:(piece + 1) * 512],
                            start=(ic == 0), stop=(ic == 7)),
                            waits=gd + ([ev_prev] if ev_prev else []))
                    tfl = sched.flush()
                    e1 = sched.emit("scalar", lambda e, g=g, j=j, piece=piece: e.activation(
                        out=sQ[0:64, 2 * j, piece * 512:(piece + 1) * 512],
                        in_=pA[0:64, :], func=AF.Identity,
                        bias=sBq[0:64, g * 2 + j:g * 2 + j + 1]), waits=[tfl])
                    e2 = sched.emit("scalar", lambda e, g=g, j=j, piece=piece: e.activation(
                        out=sQ[0:64, 2 * j + 1, piece * 512:(piece + 1) * 512],
                        in_=pA[64:128, :], func=AF.Identity,
                        bias=sBq[64:128, g * 2 + j:g * 2 + j + 1]), waits=[tfl])
                    ev_prev = e2

            zbv = sched.emit("gpsimd", lambda e: e.memset(sBvPad, 0.0),
                             waits=[wt_all], sem="zv")
            tbv = sched.emit("vector", lambda e, g=g: e.tensor_copy(
                sBvPad[0:1, :], sBvR[:, g * 256:(g + 1) * 256]),
                waits=[rows_ld, ("zv", sched.cnt["zv"]), att_done])
            for t in range(NTK):
                for ic in range(8):
                    sched.emit("tensor", lambda e, t=t, ic=ic: e.matmul(
                        pB[:, 0:256], lhsT=sXF[:, ic, t * 128:(t + 1) * 128],
                        rhs=sWv[:, ic, :], start=(ic == 0), stop=False),
                        waits=gd + ([ev_prev] if ev_prev else []))
                sched.emit("tensor", lambda e: e.matmul(
                    pB[:, 0:256], lhsT=sE0[:], rhs=sBvPad, start=False, stop=True),
                    waits=[tbv])
                tfl = sched.flush()
                ev_prev = sched.emit("vector", lambda e, t=t: e.tensor_copy(
                    sV[:, t, 0:260].rearrange("p (h v) -> p h v", h=4)[:, :, 0:64],
                    pB[:, 0:256].rearrange("p (h v) -> p h v", h=4)), waits=[tfl])
            qkv_done = ev_prev
            wfree_prev = qkv_done

            for hh in range(4):
                H = g * 4 + hh
                slab = H // 2
                rbase = (H % 2) * 64
                for qpair in range(2):
                    qpA, qpB = 2 * qpair, 2 * qpair + 1
                    sidx = att_sidx[0]
                    att_sidx[0] += 2
                    banksB = [pA, pB]
                    ptA = [sPT[0], sPT[1]]
                    ptB = [sPT[2], sPT[3]]
                    expA = {}; expB = {}
                    mmA = {}; mmB = {}
                    avA = avB = None
                    for sl in range(NTK):
                        par = sl % 2
                        wA = [qkv_done]
                        wB = [qkv_done]
                        if sl >= 2:
                            wA.append(expA[sl - 2]); wB.append(expB[sl - 2])
                        if sl < 2 and prev_exp_last is not None:
                            wA.append(prev_exp_last); wB.append(prev_exp_last)
                        mmA[sl] = sched.emit("tensor", lambda e, hh=hh, sl=sl, qpA=qpA,
                                             par=par: e.matmul(
                            pS[par][:, :],
                            lhsT=sK[:, hh, sl * 128:(sl + 1) * 128],
                            rhs=sQ[:, hh, qpA * 512:(qpA + 1) * 512],
                            start=True, stop=True), waits=wA)
                        mmB[sl] = sched.emit("tensor", lambda e, hh=hh, sl=sl, qpB=qpB,
                                             par=par, banksB=banksB: e.matmul(
                            banksB[par][:, :], lhsT=sK[:, hh, sl * 128:(sl + 1) * 128],
                            rhs=sQ[:, hh, qpB * 512:(qpB + 1) * 512],
                            start=True, stop=True), waits=wB)
                        if sl == 0 and pend_tail[0] is not None:
                            for tt in pend_tail[0]:
                                _emit_tail(*tt)
                            pend_tail[0] = None
                        if sl >= 1:
                            ep = (sl - 1) % 2
                            expA[sl - 1] = sched.emit("scalar", lambda e, ep=ep, ptA=ptA:
                                e.activation(out=ptA[ep][:, :], in_=pS[ep][:, :], func=AF.Exp,
                                             scale=1.0 / 32.0), waits=[mmA[sl]])
                            expB[sl - 1] = sched.emit("scalar", lambda e, ep=ep, ptB=ptB,
                                                      banksB=banksB:
                                e.activation(out=ptB[ep], in_=banksB[ep][:, :],
                                             func=AF.Exp, scale=1.0 / 32.0), waits=[mmB[sl]])
                            awA = [expA[sl - 1]]
                            awB = [expB[sl - 1]]
                            if sl - 1 == 0:
                                awA.append(tail_done.get(sidx - 2))
                                awB.append(tail_done.get(sidx - 1))
                            avA = sched.emit("tensor", lambda e, hh=hh, sl=sl, ep=ep, ptA=ptA:
                                e.matmul(pO[0:65, :],
                                         lhsT=sV[:, sl - 1, hh * 65:hh * 65 + 65],
                                         rhs=ptA[ep][:, :], start=(sl - 1 == 0), stop=False),
                                waits=awA)
                            avB = sched.emit("tensor", lambda e, hh=hh, sl=sl, ep=ep, ptB=ptB:
                                e.matmul(pO1[0:65, :],
                                         lhsT=sV[:, sl - 1, hh * 65:hh * 65 + 65],
                                         rhs=ptB[ep], start=(sl - 1 == 0), stop=False),
                                waits=awB)
                    tfl0 = sched.flush()
                    ep = (NTK - 1) % 2
                    eA = sched.emit("scalar", lambda e, ep=ep, ptA=ptA: e.activation(
                        out=ptA[ep][:, :], in_=pS[ep][:, :], func=AF.Exp,
                        scale=1.0 / 32.0), waits=[tfl0])
                    eB = sched.emit("scalar", lambda e, ep=ep, ptB=ptB, banksB=banksB:
                        e.activation(out=ptB[ep], in_=banksB[ep][:, :], func=AF.Exp,
                                     scale=1.0 / 32.0), waits=[tfl0, eA])
                    prev_exp_last = eB
                    avA = sched.emit("tensor", lambda e, hh=hh, ep=ep, ptA=ptA: e.matmul(
                        pO[0:65, :], lhsT=sV[:, NTK - 1, hh * 65:hh * 65 + 65],
                        rhs=ptA[ep][:, :], start=False, stop=True), waits=[eA])
                    avB = sched.emit("tensor", lambda e, hh=hh, ep=ep, ptB=ptB: e.matmul(
                        pO1[0:65, :], lhsT=sV[:, NTK - 1, hh * 65:hh * 65 + 65],
                        rhs=ptB[ep], start=False, stop=True), waits=[eB])
                    pend_tail[0] = [(sidx, pO, slab, rbase, qpA, avA),
                                    (sidx + 1, pO1, slab, rbase, qpB, avB)]
            if pend_tail[0] is not None:
                for tt in pend_tail[0]:
                    _emit_tail(*tt)
                pend_tail[0] = None
            att_done = tail_done[att_sidx[0] - 1]

        zbo = sched.emit("gpsimd", lambda e: e.memset(sBoPad, 0.0),
                         waits=[g_gate, att_done, prev_c5[0]], sem="zb")
        tbo = sched.emit("vector", lambda e: e.tensor_copy(sBoPad[0:1, :], sBoR),
                         waits=[("zb", sched.cnt["zb"]), rows_ld])
        ot_done = att_done
        bank_free = {0: None, 1: None}
        out_free = [None, None]
        xr_free = [("tensor", sched.cnt["tensor"]), ("tensor", sched.cnt["tensor"])]
        prev_bn = (None, None)
        mprev = None
        mtoks = {}
        for t in range(NTQ):
            buf = sOut[t % 2]
            for half, bank in ((0, pA), (1, pB)):
                dd = [ot_done, cast_wo[-1]] + ([bank_free[half]] if bank_free[half] else [])
                for ic in range(8):
                    sched.emit("tensor", lambda e, t=t, ic=ic, half=half, bank=bank: e.matmul(
                        bank[:, :], lhsT=sOTb[:, ic, t * 128:(t + 1) * 128],
                        rhs=sWoB[:, ic, half * 512:(half + 1) * 512],
                        start=(ic == 0), stop=False), waits=dd)
                sched.emit("tensor", lambda e, half=half, bank=bank: e.matmul(
                    bank[:, :], lhsT=sE0[:], rhs=sBoPad[:, half * 512:(half + 1) * 512],
                    start=False, stop=True), waits=[tbo])
            tfl = sched.flush()
            xrb = sXr[t % 2]
            txr = sched.dma("sync", xrb,
                            x2kv.rearrange("(t p) d -> p t d", p=128)[:, t, :],
                            sem=f"xr{t % 2}",
                            waits=[xr_free[t % 2]])
            tr_toks = []
            for half, bank in ((0, pA), (1, pB)):
                tr = sched.emit("vector", lambda e, half=half, bank=bank, buf=buf, xrb=xrb:
                                e.tensor_tensor(out=buf[:, half * 512:(half + 1) * 512],
                                                in0=xrb[:, half * 512:(half + 1) * 512],
                                                in1=bank[:, :], op=ALU.add),
                                waits=[tfl, txr] + ([out_free[t % 2]] if out_free[t % 2] else []))
                bank_free[half] = tr
                tr_toks.append(tr)
            xr_free[t % 2] = tr_toks[-1]
            tln, tbn = emit_layernorm(sched, buf, buf, sBn6, sBn2, sScr, 1e-5,
                                      deps=tr_toks, prev=prev_bn)
            prev_bn = tbn
            tst = sched.dma("sync", x3r[:, t, :], buf, waits=[tln],
                            sem="stA" if t % 2 == 0 else "stB")
            out_free[t % 2] = tst
        sched.barrier("gpsimd", [("stA", sched.cnt["stA"]), ("stB", sched.cnt["stB"])])
        block = stack.enter_context(nc.Block())
        sched.build(block)
    return nc


def build_Cm():
    nc = bass.Bass("TRN2", target_bir_lowering=False)
    x3 = nc.dram_tensor("x3", [2048, 1024], F32, kind="ExternalInput").ap()
    idf = nc.dram_tensor("idf", [128, 128], F32, kind="ExternalInput").ap()
    m1o = nc.dram_tensor("m1o", [1, 1], F32, kind="ExternalOutput").ap()
    with ExitStack() as stack:
        sched = Sched(nc, stack)
        sb = lambda n, s, d=F32: stack.enter_context(nc.sbuf_tensor(n, s, d))
        ps = lambda n, s, d=F32: stack.enter_context(nc.psum_tensor(n, s, d))
        sX = sb("sX", [128, 16, 1024])
        sMW = sb("sMW", [128, 16])
        sM1 = sb("sM1", [128, 1])
        sIdf = sb("sIdf", [128, 128])
        sM1r = sb("sM1r", [1, 128])
        pT = ps("pT", [128, 512])
        tld = sched.dma("sync", sX[:], x3.rearrange("(t p) d -> p t d", p=128), sem="ld")
        sched.dma("sync", sIdf[:], idf, sem="ld")
        ld_all = ("ld", sched.cnt["ld"])
        toks = []
        for t in range(16):
            toks.append(sched.emit("vector", lambda e, t=t: e.tensor_reduce(
                out=sMW[:, t:t + 1], in_=sX[:, t, :], axis=AX.X, op=ALU.max,
                apply_absolute_value=True), waits=[ld_all]))
        tg = sched.emit("scalar", lambda e: e.activation(out=sM1[:], in_=sMW[:, 0:1],
                                                         func=AF.Identity), waits=toks)
        tf = sched.emit("vector", lambda e: e.tensor_reduce(
            out=sM1[:, 0:1], in_=sMW[:], axis=AX.X, op=ALU.max), waits=[tg])
        tg2 = sched.emit("scalar", lambda e: e.activation(out=sMW[:, 0:1], in_=sM1[:],
                                                          func=AF.Identity), waits=[tf])
        ttp = sched.emit("tensor", lambda e: e.transpose(pT[0:1, 0:128], sM1[:, 0:1], sIdf[:]),
                         waits=[tg2, ld_all])
        tdr = sched.emit("tensor", lambda e: e.drain(), waits=[])
        tm2 = sched.emit("vector", lambda e: e.tensor_reduce(
            out=sM1r[0:1, 0:1], in_=pT[0:1, 0:128], axis=AX.X, op=ALU.max), waits=[tdr])
        sched.dma("sync", m1o, sM1r[0:1, 0:1], waits=[tm2], sem="stM")
        sched.barrier("gpsimd", [("stM", sched.cnt["stM"])])
        block = stack.enter_context(nc.Block())
        sched.build(block)
    return nc


def host_C_inputs(x2_full_rot, Wq, bq, Wk, bk, Wv, bv, Wo, bo):
    return {
        "x2kv": np.ascontiguousarray(x2_full_rot, np.float32),
        "WqT": np.ascontiguousarray(np.asarray(Wq, np.float32).T),
        "WkT": np.ascontiguousarray(np.asarray(Wk, np.float32).T),
        "WvT": np.ascontiguousarray(np.asarray(Wv, np.float32).T),
        "WoT": np.ascontiguousarray(np.asarray(Wo, np.float32).T),
        "bqc": np.ascontiguousarray(np.asarray(bq, np.float32).reshape(8, 128).T),
        "bkc": np.ascontiguousarray(np.asarray(bk, np.float32).reshape(8, 128).T),
        "bvr": np.asarray(bv, np.float32).reshape(1, 1024),
        "bor": np.asarray(bo, np.float32).reshape(1, 1024),
        "onesb": np.ones((1, 128), BFDT),
        "idf": np.eye(128, dtype=np.float32),
    }



def build_A1():
    nc = bass.Bass("TRN2", target_bir_lowering=False)
    NSTR = _A1_STRIPES
    _ser = _SERIAL or _A1_SERIAL
    x = nc.dram_tensor("x", [4096, 1024], F32, kind="ExternalInput").ap()
    Wq = nc.dram_tensor("Wq", [1024, 512], F32, kind="ExternalInput").ap()
    Wk = nc.dram_tensor("Wk", [1024, 512], F32, kind="ExternalInput").ap()
    Wv = nc.dram_tensor("Wv", [1024, 512], F32, kind="ExternalInput").ap()
    Wg = nc.dram_tensor("Wg", [1024, 512], F32, kind="ExternalInput").ap()
    gat = nc.dram_tensor("gat", [128, 8], F32, kind="ExternalInput").ap()
    triu = nc.dram_tensor("triu", [64, 64], F32, kind="ExternalInput").ap()
    blkp = nc.dram_tensor("blkp", [128, 128], F32, kind="ExternalInput").ap()
    selp = nc.dram_tensor("selp", [128, 128], F32, kind="ExternalInput").ap()
    oTo = nc.dram_tensor("oTo", [512, 4096], BF16, kind="ExternalOutput").ap()
    oTr = oTo.rearrange("(s p) t -> p s t", p=128)

    with ExitStack() as stack:
        sched = Sched(nc, stack)
        sched.serial = _ser
        sb = lambda n, s, d=F32: stack.enter_context(nc.sbuf_tensor(n, s, d))
        ps = lambda n, s, d=F32: stack.enter_context(nc.psum_tensor(n, s, d))

        sX = sb("sX", [128, 8, 1024])
        sWqB = sb("sWqB", [128, 8, 512], BF16)
        sWkB = sb("sWkB", [128, 8, 512], BF16)
        sWvB = sb("sWvB", [128, 8, 512], BF16)
        sWgB = sb("sWgB", [128, 8, 512], BF16)
        sStage = sb("sStage", [128, 8, 512])
        sXFM = sb("sXFM", [128, 8, 1024], BF16)
        sXT = sb("sXT", [128, 1024], BF16)
        sQT = sb("sQT", [64, 8, 1024], BF16)
        sKT = sb("sKT", [64, 8, 1024], BF16)
        sKTM = sb("sKTM", [128, 8, 512], BF16)
        sVTM = sb("sVTM", [128, 8, 512], BF16)
        sLnr = sX[:, 0:4, :]
        sLc = sX[:, 4:8, :]
        sE1 = sb("sE1", [128, 4, 1024], BF16)
        sE2 = sb("sE2", [128, 4, 1024], BF16)
        sEc = sb("sEc", [128, 4, 16])
        sS = sb("sS", [64, 8, 64], BF16)
        sOT = sb("sOT", [128, 4, 1024], BF16)
        sF32 = sb("sF32", [128, 1024])
        sSq = sb("sSq", [128, 1024], BF16)
        sQh = sb("sQh", [128, 1024], BF16)
        sEbf = sb("sEbf", [128, 1024])
        sRn = sb("sRn", [128, 512], BF16)
        sRsc = sb("sRsc", [128, 512])
        sTri = sb("sTri", [64, 64], BF16)
        sBlk = sb("sBlk", [128, 128], BF16)
        sSel = sb("sSel", [128, 128], BF16)
        sG = sb("sG", [128, 8])
        sSm = [sb("sSmA", [128, 64], BF16), sb("sSmB", [128, 64], BF16)]
        sT1 = sb("sT1", [64, 8, 64], BF16)
        sOev = sb("sOev", [64, 64], BF16)
        sBn6 = sb("sBn6", [128, 12]); sBn2 = sb("sBn2", [128, 2]); sScr = sb("sScr", [128, 6])
        pA = ps("pA", [128, 512])
        pB = ps("pB", [128, 512])
        pS2 = [ps("pS0", [128, 512]), ps("pS1", [128, 512])]
        pO2 = [ps("pO0", [128, 512]), ps("pO1", [128, 512])]
        pSt = ps("pSt", [128, 512])
        pF = ps("pF", [128, 512])

        sched.dma("sync", sG[:], gat, sem="ld")
        sTriF = sb("sTriF", [64, 64]); sBlkF = sb("sBlkF", [128, 128]); sSelF = sb("sSelF", [128, 128])
        sched.dma("sync", sTriF[:], triu, sem="ld")
        sched.dma("sync", sBlkF[:], blkp, sem="ld")
        sched.dma("sync", sSelF[:], selp, sem="ld")
        ld_all = ("ld", sched.cnt["ld"])
        sched.emit("gpsimd", lambda e: e.memset(sS[:], 0.0), sem="zz")
        sched.emit("gpsimd", lambda e: e.memset(sRn[:], 0.0), sem="zz")
        zz_all = ("zz", sched.cnt["zz"])
        cst = []
        cst.append(sched.emit("vector", lambda e: e.tensor_copy(sTri[:], sTriF[:]), waits=[ld_all]))
        cst.append(sched.emit("vector", lambda e: e.tensor_copy(sBlk[:], sBlkF[:]), waits=[ld_all]))
        cst.append(sched.emit("vector", lambda e: e.tensor_copy(sSel[:], sSelF[:]), waits=[ld_all]))
        wprev = None
        for (wsrc, wdst, fold) in ((Wq, sWqB, True), (Wk, sWkB, True), (Wg, sWgB, True),
                                   (Wv, sWvB, True)):
            sched.dma("sync", sStage[:], wsrc.rearrange("(c p) o -> p c o", p=128),
                      sem="wl", waits=[wprev])
            tw = ("wl", sched.cnt["wl"])
            for c in range(8):
                wprev = sched.emit("vector", lambda e, wdst=wdst, c=c: e.tensor_scalar(
                    out=wdst[:, c, :], in0=sStage[:, c, :], scalar1=sG[:, c:c + 1],
                    scalar2=None, op0=ALU.mult), waits=[tw, ld_all])
        w_all = wprev

        sched.set_flush(sBlk[:], sBlk[:], pF)

        def flush64(waits=()):
            return sched.emit("tensor", lambda e: e.matmul(
                pF[0:64, 0:64], lhsT=sTri[:], rhs=sTri[:], start=True, stop=True),
                waits=list(waits))

        st_free = None
        prev_chunk_done = None
        prev_e_done = None
        for st in range(NSTR):
            tld = sched.dma("sync", sX[:], x.rearrange("(t p) d -> p t d", p=128)
                            [:, st * 8:(st + 1) * 8, :], sem=f"xs{st % 2}",
                            waits=[st_free, prev_e_done])
            tprev_tt = None
            prev_rms = None
            tr_toks = []
            for t in range(8):
                s1 = sched.emit("vector", lambda e, t=t: e.bn_stats(sBn6[:, 0:6],
                                sX[:, t, 0:512]), waits=[tld, prev_rms])
                s2 = sched.emit("vector", lambda e, t=t: e.bn_stats(sBn6[:, 6:12],
                                sX[:, t, 512:1024]), waits=[tld, s1])
                s3 = sched.emit("vector", lambda e: e.bn_aggr(sBn2[:], sBn6[:]), waits=[s2])
                s4 = sched.emit("vector", lambda e: e.tensor_scalar(
                    out=sScr[:, 0:1], in0=sBn2[:, 0:1], scalar1=sBn2[:, 0:1],
                    scalar2=sBn2[:, 1:2], op0=ALU.mult, op1=ALU.add), waits=[s3])
                s5 = sched.emit("vector", lambda e: e.tensor_scalar(
                    out=sScr[:, 1:2], in0=sScr[:, 0:1], scalar1=1e-6, scalar2=None,
                    op0=ALU.add), waits=[s4])
                t3 = sched.emit("scalar", lambda e: e.activation(out=sScr[:, 2:3],
                                in_=sScr[:, 1:2], func=AF.Ln), waits=[s5])
                t4 = sched.emit("scalar", lambda e: e.activation(out=sScr[:, 3:4],
                                in_=sScr[:, 2:3], func=AF.Exp, scale=-0.5), waits=[t3])
                prev_rms = t4
                t5 = sched.emit("scalar", lambda e, t=t: e.activation(
                    out=sXT[:], in_=sX[:, t, :], func=AF.Identity, scale=sScr[:, 3:4]),
                    waits=[t4, tprev_tt])
                tprev_tt = sched.dma("sync", sXFM[:, :, t * 128:(t + 1) * 128], sXT[:],
                                     waits=[t5], transpose=True, sem=f"tf{st % 2}")
            tXF = (f"tf{st % 2}", sched.cnt[f"tf{st % 2}"])
            st_free = tXF

            pd = [tXF, w_all] + cst + [zz_all, prev_chunk_done]
            ev = None
            for t in range(8 if _A1_LEVEL >= 2 else 0):
                for ic in range(8):
                    sched.emit("tensor", lambda e, t=t, ic=ic: e.matmul(
                        pA[:, :], lhsT=sXFM[:, ic, t * 128:(t + 1) * 128],
                        rhs=sWvB[:, ic, :], start=(ic == 0), stop=(ic == 7)),
                        waits=pd + ([ev] if ev else []))
                tf = sched.flush()
                ev = sched.emit("vector", lambda e, t=t: e.tensor_copy(
                    sVTM[:, t, :], pA[:, :]), waits=[tf])
            v_done = ev
            for sl in range(4 if _A1_LEVEL >= 3 else 0):
                for piece in range(2):
                    for ic in range(8):
                        sched.emit("tensor", lambda e, sl=sl, piece=piece, ic=ic: e.matmul(
                            pA[:, :], lhsT=sWgB[:, ic, sl * 128:(sl + 1) * 128],
                            rhs=sXFM[:, ic, piece * 512:(piece + 1) * 512],
                            start=(ic == 0), stop=(ic == 7)),
                            waits=pd + ([ev] if ev else []))
                    tf = sched.flush()
                    e1 = sched.emit("scalar", lambda e, piece=piece: e.activation(
                        out=sEbf[:, piece * 512:(piece + 1) * 512], in_=pA[:, :],
                        func=AF.Exp, scale=-1.0), waits=[tf])
                    ev = sched.emit("scalar", lambda e, sl=sl, piece=piece: e.activation(
                        out=sLnr[:, sl, piece * 512:(piece + 1) * 512],
                        in_=sEbf[:, piece * 512:(piece + 1) * 512],
                        func=AF.Ln, bias=1.0), waits=[e1])
            lnr_done = ev
            sc = []
            for sl in range(4 if _A1_LEVEL >= 4 else 0):
                for wdw in range(16):
                    sc.append(sched.emit("vector", lambda e, sl=sl, wdw=wdw: e.tensor_tensor_scan(
                        out=sLc[:, sl, wdw * 64:(wdw + 1) * 64],
                        data0=sLnr[:, sl, wdw * 64:(wdw + 1) * 64],
                        data1=sLnr[:, sl, wdw * 64:(wdw + 1) * 64],
                        initial=0.0, op0=ALU.add, op1=ALU.bypass), waits=[lnr_done]))
            scan_done = sc[-1] if sc else lnr_done
            ee = []
            for sl in range(4 if _A1_LEVEL >= 5 else 0):
                ee.append(sched.emit("scalar", lambda e, sl=sl: e.activation(
                    out=sE1[:, sl, :], in_=sLc[:, sl, :], func=AF.Exp, scale=-1.0),
                    waits=[scan_done]))
                ee.append(sched.emit("scalar", lambda e, sl=sl: e.activation(
                    out=sE2[:, sl, :], in_=sLc[:, sl, :], func=AF.Exp, scale=1.0),
                    waits=[scan_done]))
                for cch in range(16):
                    ee.append(sched.emit("scalar", lambda e, sl=sl, cch=cch: e.activation(
                        out=sEc[:, sl, cch:cch + 1], in_=sLc[:, sl, cch * 64 + 63:cch * 64 + 64],
                        func=AF.Exp, scale=-1.0), waits=[scan_done]))
            e_done = ee[-1] if ee else scan_done
            prev_e_done = e_done
            prev_h1 = prev_n3 = None
            for (wb, dst, ebuf) in (((sWqB, sQT, sE1), (sWkB, sKT, sE2)) if _A1_LEVEL >= 6 else ()):
                for sl in range(4):
                    mm_t = None
                    for piece in range(2):
                        for ic in range(8):
                            mm_t = sched.emit("tensor", lambda e, sl=sl, piece=piece, ic=ic,
                                              wb=wb: e.matmul(
                                pB[:, :] if piece == 0 else pSt[:, :],
                                lhsT=wb[:, ic, sl * 128:(sl + 1) * 128],
                                rhs=sXFM[:, ic, piece * 512:(piece + 1) * 512],
                                start=(ic == 0), stop=(ic == 7)),
                                waits=pd + ([ev] if ev else []))
                    tf = sched.flush()
                    c1 = sched.emit("vector", lambda e: e.tensor_copy(sF32[:, 0:512],
                                    pB[:, :]), waits=[tf])
                    c2 = sched.emit("vector", lambda e: e.tensor_copy(sF32[:, 512:1024],
                                    pSt[:, :]), waits=[tf, c1])
                    q1 = sched.emit("vector", lambda e: e.tensor_tensor(
                        out=sSq[:, 0:512], in0=sF32[:, 0:512], in1=sF32[:, 0:512],
                        op=ALU.mult), waits=[c1])
                    q2 = sched.emit("vector", lambda e: e.tensor_tensor(
                        out=sSq[:, 512:1024], in0=sF32[:, 512:1024], in1=sF32[:, 512:1024],
                        op=ALU.mult), waits=[c2, q1])
                    if _A1_L6SUB < 2:
                        ev = q2
                    for piece in range(2 if _A1_L6SUB >= 2 else 0):
                        m1 = sched.emit("tensor", lambda e, piece=piece: e.matmul(
                            pA[:, :], lhsT=sBlk[:], rhs=sSq[:, piece * 512:(piece + 1) * 512],
                            start=True, stop=True), waits=[q1, q2, c1, c2, prev_h1])
                        tf2 = sched.flush()
                        n1 = sched.emit("vector", lambda e: e.tensor_scalar(
                            out=sRsc[:], in0=pA[:, :], scalar1=1e-18, scalar2=None,
                            op0=ALU.add), waits=[tf2, prev_n3])
                        if _A1_L6SUB < 3:
                            prev_h1, prev_n3 = m1, n1
                            ev = n1
                            continue
                        n2 = sched.emit("scalar", lambda e: e.activation(out=sRsc[:],
                                        in_=sRsc[:], func=AF.Ln), waits=[n1])
                        n3 = sched.emit("scalar", lambda e: e.activation(out=sRn[:],
                                        in_=sRsc[:], func=AF.Exp, scale=-0.5), waits=[n2])
                        if _A1_L6SUB < 4:
                            prev_h1, prev_n3 = m1, n3
                            ev = n3
                            continue
                        m2 = sched.emit("tensor", lambda e: e.matmul(
                            pA[:, :], lhsT=sSel[:], rhs=sRn[:], start=True, stop=True),
                            waits=[n3, zz_all])
                        tf3 = sched.flush()
                        h1 = sched.emit("vector", lambda e, piece=piece: e.tensor_tensor(
                            out=sQh[:, piece * 512:(piece + 1) * 512],
                            in0=sF32[:, piece * 512:(piece + 1) * 512], in1=pA[:, :],
                            op=ALU.mult), waits=[tf3])
                        prev_h1, prev_n3 = h1, n3
                        if _A1_L6SUB < 5:
                            prev_h1, prev_n3 = h1, n3
                            ev = h1
                            continue
                        for hloc in range(2):
                            off2 = hloc * 64
                            ev = sched.emit("vector", lambda e, sl=sl, piece=piece, dst=dst,
                                            ebuf=ebuf, off2=off2, hloc=hloc: e.tensor_tensor(
                                out=dst[0:64, 2 * sl + hloc, piece * 512:(piece + 1) * 512],
                                in0=sQh[off2:off2 + 64, piece * 512:(piece + 1) * 512],
                                in1=ebuf[off2:off2 + 64, sl, piece * 512:(piece + 1) * 512],
                                op=ALU.mult), waits=[h1, e_done])
            qk_done = ev if ev is not None else e_done
            ktm_t = []
            kcp = qk_done
            for sl in range(4 if _A1_LEVEL >= 7 else 0):
                kc1 = sched.emit("vector", lambda e, sl=sl: e.tensor_copy(
                    sSq[0:64, :], sKT[0:64, 2 * sl, :]), waits=[qk_done, kcp])
                kc2 = sched.emit("vector", lambda e, sl=sl: e.tensor_copy(
                    sSq[64:128, :], sKT[0:64, 2 * sl + 1, :]), waits=[kc1])
                tkt = sched.dma("sync", sKTM[:, :, sl * 128:(sl + 1) * 128],
                                sSq[:], waits=[kc2], transpose=True, sem=f"kt{st % 2}")
                kcp = tkt
                ktm_t.append(tkt)
            ktm_done = (f"kt{st % 2}", sched.cnt[f"kt{st % 2}"]) if sched.cnt.get(f"kt{st % 2}") else qk_done

            tdr = sched.emit("tensor", lambda e: e.drain(), waits=[])
            cdep = [ktm_done, v_done, qk_done, e_done, tdr] + cst
            prev_sm = [None, None]
            prev_oe1 = {}
            prev_oe2 = None
            prev_supd = {}
            prev_su1 = None
            oev = None
            for cch in range(_A1_CHUNKS):
                tt_ = cch // 2
                toff = (cch % 2) * 64
                tc0 = cch * 64
                stbank = [pSt, pB][cch % 2]
                pint = pO2[cch % 2]
                for hh in range(8):
                    sl = hh // 2
                    par = hh % 2
                    sc_t = sched.emit("tensor", lambda e, hh=hh, tc0=tc0, par=par: e.matmul(
                        pS2[par][0:64, 0:64], lhsT=sKT[0:64, hh, tc0:tc0 + 64],
                        rhs=sQT[0:64, hh, tc0:tc0 + 64], start=True, stop=True),
                        waits=cdep + [prev_sm[par]])
                    st_t = sched.emit("tensor", lambda e, hh=hh, tt_=tt_, toff=toff,
                                      stbank=stbank: e.matmul(
                        stbank[0:64, 0:64], lhsT=sKTM[toff:toff + 64, tt_, hh * 64:hh * 64 + 64],
                        rhs=sVTM[toff:toff + 64, tt_, hh * 64:hh * 64 + 64],
                        start=True, stop=True), waits=cdep + [prev_supd.get(hh), prev_su1])
                    f1 = flush64()
                    sm_t = sched.emit("vector", lambda e, par=par, toff=toff: e.tensor_tensor(
                        out=sSm[par][toff:toff + 64, :], in0=pS2[par][0:64, 0:64], in1=sTri[:],
                        op=ALU.mult), waits=[f1, prev_oe1.get(par)])
                    prev_sm[par] = sm_t
                    su1 = sched.emit("vector", lambda e, hh=hh, stbank=stbank: e.tensor_tensor(
                        out=sT1[0:64, hh, :], in0=sS[0:64, hh, :],
                        in1=stbank[0:64, 0:64], op=ALU.add),
                        waits=[f1, prev_supd.get(hh)])
                    prev_su1 = su1
                    oi1 = sched.emit("tensor", lambda e, hh=hh, tt_=tt_, toff=toff, par=par,
                                     pint=pint: e.matmul(
                        pint[0:64, 0:64], lhsT=sVTM[toff:toff + 64, tt_, hh * 64:hh * 64 + 64],
                        rhs=sSm[par][toff:toff + 64, :], start=True, stop=True),
                        waits=[sm_t, prev_oe1.get(par)])
                    oi2 = sched.emit("tensor", lambda e, hh=hh, tc0=tc0: e.matmul(
                        pA[0:64, 0:64], lhsT=sS[0:64, hh, :],
                        rhs=sQT[0:64, hh, tc0:tc0 + 64], start=True, stop=True),
                        waits=[prev_supd.get(hh), prev_oe2, cdep[0] if cch == 0 else None,
                               cdep[1] if cch == 0 else None])
                    f2 = flush64()
                    su2 = sched.emit("vector", lambda e, hh=hh, sl=sl, par=par, cch=cch:
                                     e.tensor_scalar(
                        out=sS[0:64, hh, :], in0=sT1[0:64, hh, :],
                        scalar1=sEc[par * 64:par * 64 + 64, sl, cch:cch + 1], scalar2=None,
                        op0=ALU.mult), waits=[su1, f2])
                    prev_supd[hh] = su2
                    oe1 = sched.emit("scalar", lambda e, pint=pint: e.activation(
                        out=sOev[:], in_=pint[0:64, 0:64], func=AF.Identity),
                        waits=[f2, prev_oe2])
                    prev_oe1[cch % 2] = oe1
                    oev = sched.emit("vector", lambda e, hh=hh, sl=sl, par=par, tc0=tc0:
                                     e.tensor_tensor(
                        out=sOT[par * 64:par * 64 + 64, sl, tc0:tc0 + 64], in0=sOev[:],
                        in1=pA[0:64, 0:64], op=ALU.add), waits=[oe1])
                    prev_oe2 = oev
            tdr2 = sched.emit("tensor", lambda e: e.drain(), waits=[])
            sched.dma("sync", oTr[:, :, st * 1024:(st + 1) * 1024], sOT[:],
                      waits=[oev, prev_supd.get(7), prev_supd.get(6), qk_done], sem="stO")
            prev_chunk_done = ("stO", sched.cnt["stO"])
            for kpar in (0, 1):
                prev_oe1[kpar] = prev_chunk_done
            prev_sm = [prev_chunk_done, prev_chunk_done]
            prev_oe2 = prev_chunk_done

        sched.barrier("gpsimd", [("stO", sched.cnt["stO"])])
        block = stack.enter_context(nc.Block())
        sched.build(block)
    return nc


def host_A1_inputs(x_b, Wq_s, Wk_s, Wv_s, Wg_s, g_attn):
    p = np.arange(128)
    tri = (np.arange(64)[:, None] <= np.arange(64)[None, :]).astype(np.float32)
    blkp = np.zeros((128, 128), np.float32)
    for j in range(2):
        blkp[:, j] = (p // 64 == j)
    selp = np.zeros((128, 128), np.float32)
    for j in range(2):
        selp[j, :] = (p // 64 == j)
    return {
        "x": np.ascontiguousarray(x_b, np.float32),
        "Wq": np.ascontiguousarray(Wq_s, np.float32),
        "Wk": np.ascontiguousarray(Wk_s, np.float32),
        "Wv": np.ascontiguousarray(Wv_s, np.float32),
        "Wg": np.ascontiguousarray(Wg_s, np.float32),
        "gat": np.ascontiguousarray(np.asarray(g_attn, np.float32).reshape(8, 128).T),
        "triu": tri, "blkp": blkp, "selp": selp,
    }



def _emit_scalar_prep(nc, sched, sb, ps, mrow_ap, clip_ap, sE0, pR, n_parts):
    sSc = sb("sScal", [1, 8])
    sPad = sb("sPad", [128, 2], BF16)
    sCols = sb("sCols", [128, 2])
    z = sched.emit("gpsimd", lambda e: e.memset(sPad[:], 0.0), sem="zz")
    t1 = sched.emit("vector", lambda e: e.tensor_reduce(
        out=sSc[0:1, 0:1], in_=mrow_ap, axis=AX.X, op=ALU.max), waits=[("ld", sched.cnt["ld"])])
    t2 = sched.emit("scalar", lambda e: e.activation(out=sSc[0:1, 1:2], in_=clip_ap,
                                                     func=AF.Sigmoid),
                    waits=[("ld", sched.cnt["ld"]), t1])
    t3 = sched.emit("vector", lambda e: e.tensor_tensor(out=sSc[0:1, 2:3], in0=sSc[0:1, 0:1],
                                                        in1=sSc[0:1, 1:2], op=ALU.mult),
                    waits=[t2, t1])
    t3 = sched.emit("vector", lambda e: e.tensor_scalar(
        out=sSc[0:1, 2:3], in0=sSc[0:1, 2:3], scalar1=1.0 / 127.0, scalar2=None,
        op0=ALU.mult), waits=[t3])
    t4 = sched.emit("vector", lambda e: e.reciprocal(sSc[0:1, 3:4], sSc[0:1, 2:3]), waits=[t3])
    t5 = sched.emit("vector", lambda e: e.tensor_scalar(
        out=sSc[0:1, 4:5], in0=sSc[0:1, 3:4], scalar1=1.0, scalar2=None, op0=ALU.mult),
        waits=[t4])
    t6 = sched.emit("vector", lambda e: e.tensor_copy(sPad[0:1, 0:1], sSc[0:1, 4:5]),
                    waits=[t5, z])
    t7 = sched.emit("vector", lambda e: e.tensor_copy(sPad[0:1, 1:2], sSc[0:1, 2:3]),
                    waits=[t6])
    m1 = sched.emit("tensor", lambda e: e.matmul(pR[:, 0:2], lhsT=sE0[:], rhs=sPad[:],
                                                 start=True, stop=True), waits=[t7])
    fl = sched.flush()
    t8 = sched.emit("vector", lambda e: e.tensor_copy(sCols[:], pR[:, 0:2]), waits=[fl])
    return sCols[:, 0:1], sCols[:, 1:2], t8


def build_D1():
    nc = bass.Bass("TRN2", target_bir_lowering=False)
    x3 = nc.dram_tensor("x3", [2048, 1024], F32, kind="ExternalInput").ap()
    m1row = nc.dram_tensor("m1row", [1, 8], F32, kind="ExternalInput").ap()
    clipx = nc.dram_tensor("clipx", [1, 1], F32, kind="ExternalInput").ap()
    wg = nc.dram_tensor("wg", [2816, 1024], F32, kind="ExternalInput").ap()
    wu = nc.dram_tensor("wu", [2816, 1024], F32, kind="ExternalInput").ap()
    onesb = nc.dram_tensor("onesb", [1, 128], BF16, kind="ExternalInput").ap()
    po = nc.dram_tensor("po", [2816, 2048], BF16, kind="ExternalOutput").ap()
    m2o = nc.dram_tensor("m2o", [1, 1], F32, kind="ExternalOutput").ap()
    idf = nc.dram_tensor("idf", [128, 128], F32, kind="ExternalInput").ap()
    dbgS = nc.dram_tensor("dbgS", [128, 6], F32, kind="ExternalOutput").ap() if _DEBUG_D1 else None
    dbgG = nc.dram_tensor("dbgG", [128, 64], BF16, kind="ExternalOutput").ap() if _DEBUG_D1 else None
    dbgU = nc.dram_tensor("dbgU", [128, 64], BF16, kind="ExternalOutput").ap() if _DEBUG_D1 else None
    dbgX = nc.dram_tensor("dbgX", [128, 64], BF16, kind="ExternalOutput").ap() if _DEBUG_D1 else None

    with ExitStack() as stack:
        sched = Sched(nc, stack)
        sched.serial = _SERIAL
        sb = lambda n, s, d=F32: stack.enter_context(nc.sbuf_tensor(n, s, d))
        ps = lambda n, s, d=F32: stack.enter_context(nc.psum_tensor(n, s, d))
        sXs = [sb("sXsA", [128, 1024]), sb("sXsB", [128, 1024])]
        sXq = sb("sXq", [128, 1024], BF16)
        sXF = sb("sXF", [128, 8, 2048], BF16)
        sWs = [sb("sWsA", [128, 1024]), sb("sWsB", [128, 1024])]
        sWq = [sb("sWqA", [128, 1024], BF16), sb("sWqB2", [128, 1024], BF16)]
        sWgF = [sb("sWgFa", [128, 8, 128], BF16), sb("sWgFb", [128, 8, 128], BF16)]
        sWuF = [sb("sWuFa", [128, 8, 128], BF16), sb("sWuFb", [128, 8, 128], BF16)]
        sQt = sb("sQt", [128, 1024])
        sQi = sb("sQi", [128, 1024], mybir.dt.int32)
        sCol = sb("sCol", [128, 6])
        sSilu = sb("sSilu", [128, 2048], BF16)
        sUp = sb("sUp", [128, 2048], BF16)
        sPp = [sb("sPpA", [128, 2048], BF16), sb("sPpB", [128, 2048], BF16)]
        sMW = sb("sMW", [128, 44])
        sM1 = sb("sM1", [128, 1])
        sM1r = sb("sM1r", [1, 128])
        sIdf = sb("sIdf", [128, 128])
        sE0 = sb("sE0", [128, 128], BF16)
        pB4 = [ps(f"p{i}", [128, 512]) for i in range(4)]
        pR = ps("pR", [128, 512])
        pF = ps("pF", [128, 512])

        sched.dma("sync", sIdf[:], idf, sem="ld")
        z0 = sched.emit("gpsimd", lambda e: e.memset(sE0[:], 0.0), sem="zz")
        te = sched.dma("sync", sE0[0:1, :], onesb, sem="ld", waits=[z0])
        zM = sched.emit("gpsimd", lambda e: e.memset(sCol[:, 0:1], MAGIC), sem="zz")
        m1l = sched.dma("sync", sM1r[0:1, 0:8].bitcast(F32) if False else sM1r[0:1, 0:8],
                        m1row, sem="ld")
        cxl = sched.dma("sync", sM1[0:1, 0:1], clipx, sem="ld")
        sched.set_flush(sE0[:], sE0[:], pF)
        rs_col, s_col, tcols = _emit_scalar_prep(nc, sched, sb, ps, sM1r[0:1, 0:8],
                                                 sM1[0:1, 0:1], sE0, pR, 128)

        xfree = [None, None]
        prev_xq = None
        for t in range(16):
            par = t % 2
            tl = sched.dma("sync", sXs[par][:], x3.rearrange("(t p) d -> p t d", p=128)[:, t, :],
                           sem=f"x{par}", waits=[xfree[par]])
            a1 = sched.emit("scalar", lambda e, par=par: e.activation(
                out=sQt[:], in_=sXs[par][:], func=AF.Identity, scale=rs_col),
                waits=[tl, tcols, prev_xq if t else None])
            a2 = sched.emit("vector", lambda e: e.tensor_copy(sQi[:], sQt[:]), waits=[a1])
            a3 = sched.emit("vector", lambda e: e.tensor_scalar(
                out=sXq[:], in0=sQi[:], scalar1=-128.0, scalar2=127.0,
                op0=ALU.max, op1=ALU.min), waits=[a2])
            prev_xq = sched.dma("sync", sXF[:, :, t * 128:(t + 1) * 128], sXq[:],
                                waits=[a3], transpose=True, sem="tq")
            xfree[par] = a1
        txq = ("tq", sched.cnt["tq"])

        wfree = [None, None]
        mm_free = None
        ev_free = None
        ppfree = [None, None]
        mtoks = []
        for sl in range(22):
            par = sl % 2
            tlg = sched.dma("sync", sWs[0][:], wg[sl * 128:(sl + 1) * 128, :],
                            sem=f"wg{0}", waits=[wfree[0]])
            am = sched.emit("vector", lambda e: e.tensor_reduce(
                out=sCol[:, 1:2], in_=sWs[0][:], axis=AX.X, op=ALU.max,
                apply_absolute_value=True), waits=[tlg])
            rc = sched.emit("vector", lambda e: e.reciprocal(sCol[:, 2:3], sCol[:, 1:2]),
                            waits=[am])
            rc2 = sched.emit("vector", lambda e: e.tensor_scalar(
                out=sCol[:, 2:3], in0=sCol[:, 2:3], scalar1=127.0, scalar2=None,
                op0=ALU.mult), waits=[rc])
            scg = sched.emit("vector", lambda e: e.tensor_scalar(
                out=sCol[:, 3:4], in0=sCol[:, 1:2], scalar1=1.0 / 127.0, scalar2=s_col,
                op0=ALU.mult, op1=ALU.mult), waits=[rc2, tcols])
            q1 = sched.emit("scalar", lambda e: e.activation(
                out=sQt[:], in_=sWs[0][:], func=AF.Identity, scale=sCol[:, 2:3]),
                waits=[rc2, ev_free])
            q2 = sched.emit("vector", lambda e: e.tensor_copy(sQi[:], sQt[:]), waits=[q1])
            q3 = sched.emit("vector", lambda e, par=par: e.tensor_scalar(
                out=sWq[0][:], in0=sQi[:], scalar1=-128.0, scalar2=127.0,
                op0=ALU.max, op1=ALU.min), waits=[q2, wfree[0]])
            tq1 = sched.dma("sync", sWgF[par][:], sWq[0][:], waits=[q3, mm_free],
                            transpose=True, sem=f"tw{par}")
            wfree[0] = tq1
            tlu = sched.dma("sync", sWs[1][:], wu[sl * 128:(sl + 1) * 128, :],
                            sem=f"wu{0}", waits=[wfree[1]])
            am2 = sched.emit("vector", lambda e: e.tensor_reduce(
                out=sCol[:, 1:2], in_=sWs[1][:], axis=AX.X, op=ALU.max,
                apply_absolute_value=True), waits=[tlu, scg])
            rcB = sched.emit("vector", lambda e: e.reciprocal(sCol[:, 2:3], sCol[:, 1:2]),
                             waits=[am2])
            rcB2 = sched.emit("vector", lambda e: e.tensor_scalar(
                out=sCol[:, 2:3], in0=sCol[:, 2:3], scalar1=127.0, scalar2=None,
                op0=ALU.mult), waits=[rcB])
            scu = sched.emit("vector", lambda e: e.tensor_scalar(
                out=sCol[:, 4:5], in0=sCol[:, 1:2], scalar1=1.0 / 127.0, scalar2=s_col,
                op0=ALU.mult, op1=ALU.mult), waits=[rcB2])
            u1 = sched.emit("scalar", lambda e: e.activation(
                out=sQt[:], in_=sWs[1][:], func=AF.Identity, scale=sCol[:, 2:3]),
                waits=[rcB2, q3])
            u2 = sched.emit("vector", lambda e: e.tensor_copy(sQi[:], sQt[:]), waits=[u1])
            u3 = sched.emit("vector", lambda e, par=par: e.tensor_scalar(
                out=sWq[1][:], in0=sQi[:], scalar1=-128.0, scalar2=127.0,
                op0=ALU.max, op1=ALU.min), waits=[u2, wfree[1]])
            tq2 = sched.dma("sync", sWuF[par][:], sWq[1][:], waits=[u3, mm_free],
                            transpose=True, sem=f"tw{par}")
            wfree[1] = tq2
            twq = (f"tw{par}", sched.cnt[f"tw{par}"])

            gmm = []
            for piece in range(4):
                for ic in range(8):
                    gmm.append(sched.emit("tensor", lambda e, piece=piece, ic=ic, par=par:
                                          e.matmul(
                        pB4[piece][:, :], lhsT=sWgF[par][:, ic, :],
                        rhs=sXF[:, ic, piece * 512:(piece + 1) * 512],
                        start=(ic == 0), stop=(ic == 7)),
                        waits=[twq, txq, te, ev_free]))
            fl = sched.flush()
            evg = []
            for piece in range(4):
                evg.append(sched.emit("scalar", lambda e, piece=piece: e.activation(
                    out=sSilu[:, piece * 512:(piece + 1) * 512], in_=pB4[piece][:, :],
                    func=AF.Silu, scale=sCol[:, 3:4]), waits=[fl, scg]))
            ummh = []
            for piece in range(4):
                for ic in range(8):
                    ummh.append(sched.emit("tensor", lambda e, piece=piece, ic=ic, par=par:
                                           e.matmul(
                        pB4[piece][:, :], lhsT=sWuF[par][:, ic, :],
                        rhs=sXF[:, ic, piece * 512:(piece + 1) * 512],
                        start=(ic == 0), stop=(ic == 7)),
                        waits=[twq, evg[piece]]))
            fl2 = sched.flush()
            mm_free = fl2
            evu = []
            for piece in range(4):
                evu.append(sched.emit("scalar", lambda e, piece=piece: e.activation(
                    out=sUp[:, piece * 512:(piece + 1) * 512], in_=pB4[piece][:, :],
                    func=AF.Identity, scale=sCol[:, 4:5]), waits=[fl2, scu]))
            if _DEBUG_D1 and sl == 0:
                sched.dma("sync", dbgS, sCol[:, 0:8].rearrange("p c -> p c")
                          if False else sCol[:], waits=[scg, scu], sem="stM")
                sched.dma("sync", dbgG, sSilu[:, 0:64], waits=evg + evu, sem="stM")
                sched.dma("sync", dbgU, sUp[:, 0:64], waits=evg + evu, sem="stM")
                sched.dma("sync", dbgX, sXF[:, 0, 0:64], waits=[txq], sem="stM")
            pp = sPp[par]
            tp = sched.emit("vector", lambda e, pp=pp: e.tensor_tensor(
                out=pp[:], in0=sSilu[:], in1=sUp[:], op=ALU.mult),
                waits=evu + evg + [ppfree[par]])
            ev_free = tp
            tmx = sched.emit("vector", lambda e, sl=sl, pp=pp: e.tensor_reduce(
                out=sMW[:, sl:sl + 1], in_=pp[:], axis=AX.X, op=ALU.max), waits=[tp])
            tmn = sched.emit("vector", lambda e, sl=sl, pp=pp: e.tensor_reduce(
                out=sMW[:, 22 + sl:23 + sl], in_=pp[:], axis=AX.X, op=ALU.min,
                negate=True), waits=[tp, tmx])
            mtoks.append(tmn)
            tst = sched.dma("sync", po[sl * 128:(sl + 1) * 128, :], pp[:],
                            waits=[tp], sem=f"sp{par}")
            ppfree[par] = tst

        tg = sched.emit("scalar", lambda e: e.activation(out=sM1[:], in_=sMW[:, 0:1],
                                                         func=AF.Identity), waits=mtoks)
        tf_ = sched.emit("vector", lambda e: e.tensor_reduce(
            out=sM1[:, 0:1], in_=sMW[:], axis=AX.X, op=ALU.max), waits=[tg])
        tg2 = sched.emit("scalar", lambda e: e.activation(out=sMW[:, 0:1], in_=sM1[:],
                                                          func=AF.Identity), waits=[tf_])
        tdr0 = sched.emit("tensor", lambda e: e.drain(), waits=[])
        ttp = sched.emit("tensor", lambda e: e.transpose(pR[0:1, 0:128], sM1[:, 0:1], sIdf[:]),
                         waits=[tg2, ("ld", sched.cnt["ld"]), tdr0])
        tdr = sched.emit("tensor", lambda e: e.drain(), waits=[])
        tm2 = sched.emit("vector", lambda e: e.tensor_reduce(
            out=sM1r[0:1, 0:1], in_=pR[0:1, 0:128], axis=AX.X, op=ALU.max), waits=[tdr])
        sched.dma("sync", m2o, sM1r[0:1, 0:1], waits=[tm2], sem="stM")
        sched.barrier("gpsimd", [("stM", sched.cnt["stM"]), ("sp0", sched.cnt["sp0"]),
                                 ("sp1", sched.cnt["sp1"])])
        block = stack.enter_context(nc.Block())
        sched.build(block)
    return nc


def build_D2():
    nc = bass.Bass("TRN2", target_bir_lowering=False)
    NT = 16
    po = nc.dram_tensor("po", [2816, 2048], BF16, kind="ExternalInput").ap()
    m2row = nc.dram_tensor("m2row", [1, 8], F32, kind="ExternalInput").ap()
    clipd = nc.dram_tensor("clipd", [1, 1], F32, kind="ExternalInput").ap()
    wd = nc.dram_tensor("wd", [1024, 2816], F32, kind="ExternalInput").ap()
    x3 = nc.dram_tensor("x3", [2048, 1024], F32, kind="ExternalInput").ap()
    onesb = nc.dram_tensor("onesb", [1, 128], BF16, kind="ExternalInput").ap()
    idf = nc.dram_tensor("idf", [128, 128], F32, kind="ExternalInput").ap()
    x4o = nc.dram_tensor("x4o", [2048, 1024], F32, kind="ExternalOutput").ap()
    x4r = x4o.rearrange("(t p) d -> p t d", p=128)

    with ExitStack() as stack:
        sched = Sched(nc, stack)
        sched.serial = _SERIAL
        sb = lambda n, s, d=F32: stack.enter_context(nc.sbuf_tensor(n, s, d))
        ps = lambda n, s, d=F32: stack.enter_context(nc.psum_tensor(n, s, d))
        sWdF = sb("sWdF", [128, 22, 1024], BF16)
        sHq = sb("sHq", [128, 22, 1024], BF16)
        sWs = [sb("sWsA", [128, 2816]), sb("sWsB", [128, 2816])]
        sWqT = sb("sWqT", [128, 2816], BF16)
        sPs = [sb("sPsA", [128, 1024], BF16), sb("sPsB", [128, 1024], BF16)]
        sQt = sb("sQt", [128, 2816])
        sQi = sb("sQi", [128, 2816], mybir.dt.int32)
        sCol = sb("sCol", [128, 6])
        sSd = sb("sSd", [128, 8])
        sRow = sb("sRow", [1, 1024])
        sRowP = sb("sRowP", [128, 1024], BF16)
        sRsb = sb("sRsb", [128, 1024])
        sXr = [sb("sXrA", [128, 1024]), sb("sXrB", [128, 1024])]
        sOut = [sb("sOutA", [128, 1024]), sb("sOutB", [128, 1024])]
        sM1 = sb("sM1", [128, 1]); sM1r = sb("sM1r", [1, 128])
        sIdf = sb("sIdf", [128, 128])
        sE0 = sb("sE0", [128, 128], BF16)
        sBn6 = sb("sBn6", [128, 12]); sBn2 = sb("sBn2", [128, 2]); sScr = sb("sScr", [128, 6])
        pA = ps("pA", [128, 512]); pB = ps("pB", [128, 512])
        pR = ps("pR", [128, 512]); pF = ps("pF", [128, 512])

        sched.dma("sync", sIdf[:], idf, sem="ld")
        z0 = sched.emit("gpsimd", lambda e: e.memset(sE0[:], 0.0), sem="zz")
        te = sched.dma("sync", sE0[0:1, :], onesb, sem="ld", waits=[z0])
        zM = sched.emit("gpsimd", lambda e: e.memset(sCol[:, 0:1], MAGIC), sem="zz")
        zR = sched.emit("gpsimd", lambda e: e.memset(sRowP[:], 0.0), sem="zz")
        sched.dma("sync", sM1r[0:1, 0:8], m2row, sem="ld")
        sched.dma("sync", sM1[0:1, 0:1], clipd, sem="ld")
        sched.set_flush(sE0[:], sE0[:], pF)
        rs_col, s_col, tcols = _emit_scalar_prep(nc, sched, sb, ps, sM1r[0:1, 0:8],
                                                 sM1[0:1, 0:1], sE0, pR, 128)

        wfree = [None, None]
        wq_toks = []
        trp_toks = []
        for j in range(8):
            par = j % 2
            tl = sched.dma("sync", sWs[par][:], wd[j * 128:(j + 1) * 128, :],
                           sem=f"w{par}", waits=[wfree[par]])
            am = sched.emit("vector", lambda e, par=par: e.tensor_reduce(
                out=sCol[:, 1:2], in_=sWs[par][:], axis=AX.X, op=ALU.max,
                apply_absolute_value=True), waits=[tl, wq_toks[-1] if wq_toks else None])
            rc = sched.emit("vector", lambda e: e.reciprocal(sCol[:, 2:3], sCol[:, 1:2]),
                            waits=[am])
            rc2 = sched.emit("vector", lambda e: e.tensor_scalar(
                out=sCol[:, 2:3], in0=sCol[:, 2:3], scalar1=127.0, scalar2=None,
                op0=ALU.mult), waits=[rc])
            sw = sched.emit("vector", lambda e, j=j: e.tensor_scalar(
                out=sSd[:, j:j + 1], in0=sCol[:, 1:2], scalar1=1.0 / 127.0, scalar2=None,
                op0=ALU.mult), waits=[rc2])
            q1 = sched.emit("scalar", lambda e, par=par: e.activation(
                out=sQt[:], in_=sWs[par][:], func=AF.Identity, scale=sCol[:, 2:3]),
                waits=[rc2, wq_toks[-1] if wq_toks else None])
            q2 = sched.emit("vector", lambda e: e.tensor_copy(sQi[:], sQt[:]), waits=[q1])
            q3 = sched.emit("vector", lambda e: e.tensor_scalar(
                out=sWqT[:], in0=sQi[:], scalar1=-128.0, scalar2=127.0,
                op0=ALU.max, op1=ALU.min), waits=[q2, trp_toks[-1] if trp_toks else None])
            wq_toks.append(q3)
            ttr = sched.dma("sync", sWdF[:, :, j * 128:(j + 1) * 128], sWqT[:],
                            waits=[q3], transpose=True, sem="tw")
            trp_toks.append(ttr)
            wfree[par] = ttr
        tw_all = ("tw", sched.cnt["tw"])

        rowp = []
        sched.emit("tensor", lambda e: e.drain(), waits=[])
        for j in range(8):
            ttp = sched.emit("tensor", lambda e, j=j: e.transpose(
                pR[0:1, 0:128], sSd[:, j:j + 1], sIdf[:]),
                waits=[("ld", sched.cnt["ld"]),
                       wq_toks[-1] if j == 0 else None, rowp[-1] if rowp else None])
            td = sched.emit("tensor", lambda e: e.drain(), waits=[])
            cp = sched.emit("vector", lambda e, j=j: e.tensor_copy(
                sRow[0:1, j * 128:(j + 1) * 128], pR[0:1, 0:128]), waits=[td])
            rowp.append(cp)
        trow = sched.emit("vector", lambda e: e.tensor_scalar(
            out=sRow[:], in0=sRow[:], scalar1=s_col[0:1, 0:1], scalar2=None,
            op0=ALU.mult), waits=rowp + [tcols])
        trp = sched.emit("vector", lambda e: e.tensor_copy(sRowP[0:1, :], sRow[:]),
                         waits=[trow, zR])
        rmm = sched.emit("tensor", lambda e: e.matmul(pR[:, 0:512], lhsT=sE0[:],
                                                      rhs=sRowP[:, 0:512], start=True,
                                                      stop=True), waits=[trp, te])
        rmm2 = sched.emit("tensor", lambda e: e.matmul(pA[:, 0:512], lhsT=sE0[:],
                                                       rhs=sRowP[:, 512:1024], start=True,
                                                       stop=True), waits=[trp, te])
        flr = sched.flush()
        tre = sched.emit("vector", lambda e: e.tensor_copy(sRsb[:, 0:512], pR[:, 0:512]),
                         waits=[flr])
        tre2 = sched.emit("vector", lambda e: e.tensor_copy(sRsb[:, 512:1024], pA[:, 0:512]),
                          waits=[flr, tre])

        out_free = [None, None]
        xr_free = [None, None]
        prev_bn = (None, None)
        hq_free = None
        bank_free = {0: None, 1: None}
        for h2 in range(2):
            pfree = [None, None]
            hq_toks = []
            for c in range(22):
                par = c % 2
                tl = sched.dma("sync", sPs[par][:],
                               po[c * 128:(c + 1) * 128, h2 * 1024:(h2 + 1) * 1024],
                               sem=f"p{par}", waits=[pfree[par]])
                a1 = sched.emit("scalar", lambda e, par=par: e.activation(
                    out=sQt[:, 0:1024], in_=sPs[par][:], func=AF.Identity, scale=rs_col),
                    waits=[tl, tcols, hq_toks[-1] if hq_toks else hq_free])
                a2 = sched.emit("vector", lambda e: e.tensor_copy(
                    sQi[:, 0:1024], sQt[:, 0:1024]), waits=[a1])
                a3 = sched.emit("vector", lambda e, c=c: e.tensor_scalar(
                    out=sHq[:, c, :], in0=sQi[:, 0:1024], scalar1=-128.0, scalar2=127.0,
                    op0=ALU.max, op1=ALU.min), waits=[a2])
                pfree[par] = a3
                hq_toks.append(a3)
            hq_all = hq_toks[-1]
            for tt_ in range(8):
                t = h2 * 8 + tt_
                buf = sOut[t % 2]
                for half, bank in ((0, pA), (1, pB)):
                    dd = [hq_all, tw_all, tre2] + ([bank_free[half]] if bank_free[half]
                                                   else [])
                    for c in range(22):
                        sched.emit("tensor", lambda e, c=c, tt_=tt_, half=half, bank=bank:
                                   e.matmul(bank[:, :],
                                            lhsT=sHq[:, c, tt_ * 128:(tt_ + 1) * 128],
                                            rhs=sWdF[:, c, half * 512:(half + 1) * 512],
                                            start=(c == 0), stop=(c == 21)), waits=dd)
                tfl = sched.flush()
                xrb = sXr[t % 2]
                txr = sched.dma("sync", xrb[:],
                                x3.rearrange("(t p) d -> p t d", p=128)[:, t, :],
                                sem=f"xr{t % 2}", waits=[xr_free[t % 2]])
                tr_toks = []
                for half, bank in ((0, pA), (1, pB)):
                    tr = sched.emit("vector", lambda e, half=half, bank=bank, buf=buf:
                                    e.tensor_tensor(out=buf[:, half * 512:(half + 1) * 512],
                                                    in0=bank[:, :],
                                                    in1=sRsb[:, half * 512:(half + 1) * 512],
                                                    op=ALU.mult),
                                    waits=[tfl] + ([out_free[t % 2]] if out_free[t % 2]
                                                   else []))
                    bank_free[half] = tr
                    tr2 = sched.emit("vector", lambda e, half=half, buf=buf, xrb=xrb:
                                     e.tensor_tensor(out=buf[:, half * 512:(half + 1) * 512],
                                                     in0=buf[:, half * 512:(half + 1) * 512],
                                                     in1=xrb[:, half * 512:(half + 1) * 512],
                                                     op=ALU.add), waits=[tr, txr])
                    tr_toks.append(tr2)
                xr_free[t % 2] = tr_toks[-1]
                tln, tbn = emit_layernorm(sched, buf[:], buf[:], sBn6, sBn2, sScr, 1e-5,
                                          deps=tr_toks, prev=prev_bn)
                prev_bn = tbn
                tst = sched.dma("sync", x4r[:, t, :], buf[:], waits=[tln],
                                sem="stA" if t % 2 == 0 else "stB")
                out_free[t % 2] = tst
            hq_free = ("tensor", sched.cnt["tensor"])

        sched.barrier("gpsimd", [("stA", sched.cnt["stA"]), ("stB", sched.cnt["stB"])])
        block = stack.enter_context(nc.Block())
        sched.build(block)
    return nc



_NCS = {}


def _get(name, builder):
    if name not in _NCS:
        _NCS[name] = builder()
    return _NCS[name]


def kernel(**inputs):
    inp = {k: np.asarray(v) for k, v in inputs.items()}
    x = np.ascontiguousarray(inp["L0_attn_x"], np.float32)
    cores = list(range(8))
    idf = np.eye(128, dtype=np.float32)
    onesb = np.ones((1, 128), BFDT)
    onesr = np.ones((1, 128), np.float32)

    nc = _get("A1", build_A1)
    im = []
    for c in cores:
        b, hg = c // 2, c % 2
        cols = slice(hg * 512, (hg + 1) * 512)
        im.append(host_A1_inputs(x[b], inp["L0_attn_Wq"][:, cols], inp["L0_attn_Wk"][:, cols],
                                 inp["L0_attn_Wv"][:, cols], inp["L0_attn_Wg"][:, cols],
                                 inp["L0_attn_g_attn"]))
    rA1 = run_bass_kernel_spmd(nc, im, core_ids=cores).results

    nc = _get("A2", build_A2)
    im = []
    for c in cores:
        b, hf = c // 2, c % 2
        oT_shard = np.concatenate([rA1[2 * b]["oTo"], rA1[2 * b + 1]["oTo"]], axis=0)
        oT_shard = oT_shard[:, hf * 2048:(hf + 1) * 2048]
        im.append(host_A2_inputs(x[b, hf * 2048:(hf + 1) * 2048], oT_shard,
                                 inp["L0_attn_Wo"], inp["L0_attn_w1"], inp["L0_attn_w2"],
                                 inp["L0_attn_w3"], inp["L0_attn_g_ffn"]))
    rA2 = run_bass_kernel_spmd(nc, im, core_ids=cores).results

    nc = _get("B", build_B)
    im = [host_B_inputs(rA2[c]["x1o"], inp["L0_ffn_G1_R"], inp["L0_ffn_G2_R"],
                        inp["L0_ffn_G1_I"], inp["L0_ffn_G2_I"],
                        inp["L0_ffn_bias_real"], inp["L0_ffn_bias_imag"]) for c in cores]
    rB = run_bass_kernel_spmd(nc, im, core_ids=cores).results

    nc = _get("C", build_C)
    im = []
    for c in cores:
        b, hf = c // 2, c % 2
        own = rB[c]["x2"]
        other = rB[2 * b + (1 - hf)]["x2"]
        rot = np.concatenate([own, other], axis=0)
        im.append(host_C_inputs(rot, inp["L1_attn_Wq"], inp["L1_attn_bq"],
                                inp["L1_attn_Wk"], inp["L1_attn_bk"],
                                inp["L1_attn_Wv"], inp["L1_attn_bv"],
                                inp["L1_attn_Wo"], inp["L1_attn_bo"]))
    rC = run_bass_kernel_spmd(nc, im, core_ids=cores).results

    nc = _get("Cm", build_Cm)
    im = [{"x3": rC[c]["x3o"], "idf": idf} for c in cores]
    rCm = run_bass_kernel_spmd(nc, im, core_ids=cores).results
    m1row = np.concatenate([rCm[c]["m1o"] for c in cores], axis=1)

    nc = _get("D1", build_D1)
    im = [{"x3": rC[c]["x3o"], "m1row": m1row,
           "clipx": np.asarray(inp["L1_ffn_clip_x"], np.float32).reshape(1, 1),
           "wg": np.ascontiguousarray(inp["L1_ffn_w_gate"], np.float32),
           "wu": np.ascontiguousarray(inp["L1_ffn_w_up"], np.float32),
           "onesb": onesb, "idf": idf} for c in cores]
    rD1 = run_bass_kernel_spmd(nc, im, core_ids=cores).results
    m2row = np.concatenate([rD1[c]["m2o"] for c in cores], axis=1)

    nc = _get("D2", build_D2)
    im = [{"po": rD1[c]["po"], "m2row": m2row,
           "clipd": np.asarray(inp["L1_ffn_clip_down"], np.float32).reshape(1, 1),
           "wd": np.ascontiguousarray(inp["L1_ffn_w_down"], np.float32),
           "x3": rC[c]["x3o"], "onesb": onesb, "idf": idf} for c in cores]
    rD2 = run_bass_kernel_spmd(nc, im, core_ids=cores).results

    out = np.zeros((4, 4096, 1024), np.float32)
    for c in cores:
        b, hf = c // 2, c % 2
        out[b, hf * 2048:(hf + 1) * 2048, :] = rD2[c]["x4o"]
    return out


# version 06
# speedup vs baseline: 1.0188x; 1.0188x over previous
import numpy as np
import ml_dtypes
from contextlib import ExitStack
import concourse.bass as bass
import concourse.mybir as mybir
from concourse.bass_utils import run_bass_kernel_spmd

F32 = mybir.dt.float32
BF16 = mybir.dt.bfloat16
AF = mybir.ActivationFunctionType
ALU = mybir.AluOpType
AX = mybir.AxisListType
BFDT = ml_dtypes.bfloat16

MAGIC = 12582912.0

_DEBUG_B = False
_DEBUG_A2 = False
_DEBUG_C = False
_A1_STRIPES = 4
_A1_CHUNKS = 16
_A1_LEVEL = 9
_A1_L6SUB = 9
_DEBUG_D1 = False
_A1_SERIAL = False
_SERIAL = False
_SERIAL_LN = False
_FLUSH_N = 1

ENGINES = ("sync", "gpsimd", "tensor", "vector", "scalar")


class Sched:

    HAZ = 4

    def __init__(self, nc, stack):
        self.nc = nc
        self.stack = stack
        self.ops = {e: [] for e in ENGINES}
        self.sems = {}
        self.cnt = {}
        self.serial = False
        self.last = None
        for e in ENGINES:
            self._mksem(e)

    def _mksem(self, name):
        if name not in self.sems:
            self.sems[name] = self.stack.enter_context(self.nc.semaphore("S" + name))
            self.cnt[name] = 0

    def emit(self, engine, fn, waits=(), sem=None, amt=None):
        if sem is None:
            sem = engine
        self._mksem(sem)
        if amt is None:
            amt = 16 if engine in ("sync", "gpsimd") else 1
        self.cnt[sem] += amt
        tok = (sem, self.cnt[sem])
        w = [x for x in waits if x is not None]
        if self.serial and self.last is not None:
            w = w + [self.last]
        self.last = tok
        self.ops[engine].append((fn, w, sem, amt, tok))
        return tok

    def dma(self, engine, out, in_, waits=(), transpose=False, sem=None):
        def fn(eng, out=out, in_=in_, transpose=transpose):
            if transpose:
                return eng.dma_start(out=out, in_=in_, transpose=True)
            return eng.dma_start(out=out, in_=in_)
        return self.emit(engine, fn, waits, sem=sem, amt=16)

    def set_flush(self, lhsT_row, rhs_row, scratch_bank):
        self._flush = (lhsT_row, rhs_row, scratch_bank)

    def flush(self, waits=()):
        lhsT, rhs, bank = self._flush
        return self.emit("tensor", lambda e: e.matmul(
            bank[:, 0:128], lhsT=lhsT, rhs=rhs, start=True, stop=True),
            waits=list(waits))

    def barrier(self, engine, toks):
        def fn(eng):
            return None
        return self.emit(engine, fn, waits=list(toks))

    def build(self, block):
        for engine in ENGINES:
            if self.ops[engine]:
                self._one(block, engine)

    def _one(self, block, engine):
        ops = self.ops[engine]
        sems = self.sems

        def body(eng):
            watermark = {}
            recent = []
            for fn, waits, sem, amt, tok in ops:
                for (wsem, wval) in waits:
                    if wsem == engine and (wsem, wval) not in recent:
                        continue
                    if watermark.get(wsem, -1) >= wval:
                        continue
                    eng.wait_ge(sems[wsem], wval)
                    watermark[wsem] = wval
                inst = fn(eng)
                if inst is not None:
                    inst.then_inc(sems[sem], amt)
                recent.append(tok)
                if len(recent) > Sched.HAZ:
                    recent.pop(0)

        getattr(block, engine)(body)



def emit_layernorm(sched, r_ap, out_ap, bn6, bn2, scr, eps, deps, prev=(None, None)):
    D = r_ap.shape[-1]
    halves = [(0, min(512, D))] + ([(512, D)] if D > 512 else [])
    toks = []
    for i, (a, b) in enumerate(halves):
        toks.append(sched.emit(
            "vector", lambda e, i=i, a=a, b=b: e.bn_stats(bn6[:, i * 6:(i + 1) * 6], r_ap[:, a:b]),
            waits=list(deps)))
    t2 = sched.emit("vector", lambda e, n=len(halves): e.bn_aggr(bn2[:], bn6[:, 0:6 * n]),
                    waits=toks)
    t2b = sched.emit("vector", lambda e: e.tensor_scalar(out=scr[:, 3:4], in0=bn2[:, 1:2],
                                                         scalar1=float(eps), scalar2=None,
                                                         op0=ALU.add), waits=[t2, prev[0]])
    t3 = sched.emit("scalar", lambda e: e.activation(out=scr[:, 0:1], in_=scr[:, 3:4],
                                                     func=AF.Ln), waits=[t2b])
    t4 = sched.emit("scalar", lambda e: e.activation(out=scr[:, 1:2], in_=scr[:, 0:1],
                                                     func=AF.Exp, scale=-0.5), waits=[t3])
    t5 = sched.emit("vector", lambda e: e.tensor_scalar(out=scr[:, 2:3], in0=bn2[:, 0:1],
                                                        scalar1=scr[:, 1:2], scalar2=-1.0,
                                                        op0=ALU.mult, op1=ALU.mult),
                    waits=[t4, prev[1]])
    t6 = sched.emit("scalar", lambda e: e.activation(out=out_ap, in_=r_ap,
                                                     func=AF.Identity, bias=scr[:, 2:3],
                                                     scale=scr[:, 1:2]), waits=[t5])
    return t6, (t3, t6)



def build_B():
    nc = bass.Bass("TRN2", target_bir_lowering=False)
    NT = 16
    x1 = nc.dram_tensor("x1", [2048, 1024], F32, kind="ExternalInput").ap()
    G1r = nc.dram_tensor("G1r", [8, 16, 32], F32, kind="ExternalInput").ap()
    G1i = nc.dram_tensor("G1i", [8, 16, 32], F32, kind="ExternalInput").ap()
    G2r = nc.dram_tensor("G2r", [16, 4, 16], F32, kind="ExternalInput").ap()
    G2i = nc.dram_tensor("G2i", [16, 4, 16], F32, kind="ExternalInput").ap()
    biasri = nc.dram_tensor("biasri", [1, 1024], F32, kind="ExternalInput").ap()
    seld = nc.dram_tensor("seld", [8, 128], F32, kind="ExternalInput").ap()
    selg = nc.dram_tensor("selg", [16, 128], F32, kind="ExternalInput").ap()
    onesr = nc.dram_tensor("onesr", [1, 128], F32, kind="ExternalInput").ap()
    x2 = nc.dram_tensor("x2", [2048, 1024], F32, kind="ExternalOutput").ap()
    x2r = x2.rearrange("(t p) d -> p t d", p=128)
    dbgW = nc.dram_tensor("dbgW", [128, 8, 1024], BF16, kind="ExternalOutput").ap() if _DEBUG_B else None

    with ExitStack() as stack:
        sched = Sched(nc, stack)
        sched.serial = _SERIAL
        sb = lambda n, s, d=F32: stack.enter_context(nc.sbuf_tensor(n, s, d))
        ps = lambda n, s, d=F32: stack.enter_context(nc.psum_tensor(n, s, d))

        sX = sb("sX", [128, NT, 1024])
        sXb = sb("sXb", [128, NT, 1024], BF16)
        sFM = sb("sFM", [128, 8, 2048], BF16)
        sG1 = sb("sG1", [8, 16, 32]); sG1i = sb("sG1i", [8, 16, 32])
        sG2 = sb("sG2", [16, 4, 16]); sG2i = sb("sG2i", [16, 4, 16])
        sG1b = sb("sG1b", [8, 16, 32], BF16); sG1ib = sb("sG1ib", [8, 16, 32], BF16)
        sG2b = sb("sG2b", [16, 4, 16], BF16); sG2ib = sb("sG2ib", [16, 4, 16], BF16)
        sSelD = sb("sSelD", [8, 128]); sSelDb = sb("sSelDb", [128, 128], BF16)
        sSelG = sb("sSelG", [16, 128]); sSelGb = sb("sSelGb", [128, 128], BF16)
        sOnes = sb("sOnes", [1, 128]); sOnesB = sb("sOnesB", [1, 128], BF16)
        sBias = sb("sBias", [1, 1024])
        sBiasPad = sb("sBiasPad", [128, 1024], BF16)
        sE0 = sb("sE0", [128, 128], BF16)
        sA8e = sb("sA8e", [128, 512], BF16)
        sB16e = sb("sB16e", [128, 512], BF16)
        sP1 = sb("sP1", [128, 512])
        sWr = [sb(f"sWr{q}", [128, 512]) for q in range(4)]
        sWi = [sb(f"sWi{q}", [128, 512]) for q in range(4)]
        sTmp = sb("sTmp", [128, 512])
        sRhs = sb("sRhs", [128, 8, 1024], BF16)
        sBn6 = sb("sBn6", [128, 12]); sBn2 = sb("sBn2", [128, 2]); sScr = sb("sScr", [128, 4])
        sOut = [sb("sOutA", [128, 1024]), sb("sOutB", [128, 1024])]
        pA = ps("pA", [128, 512])
        pB = ps("pB", [128, 512])
        pC = ps("pC", [128, 512])
        pD = ps("pD", [128, 512])
        pF = ps("pF", [128, 512])

        sched.dma("sync", sX[:], x1.rearrange("(t p) d -> p t d", p=128), sem="ld")
        sched.dma("sync", sG1[:], G1r, sem="ld")
        sched.dma("sync", sG1i[:], G1i, sem="ld")
        sched.dma("sync", sG2[:], G2r, sem="ld")
        sched.dma("sync", sG2i[:], G2i, sem="ld")
        sched.dma("sync", sSelD[:], seld, sem="ld")
        sched.dma("sync", sSelG[:], selg, sem="ld")
        sched.dma("sync", sOnes[:], onesr, sem="ld")
        sched.dma("sync", sBias[:], biasri, sem="ld")
        ld_all = ("ld", sched.cnt["ld"])

        z1 = sched.emit("gpsimd", lambda e: e.memset(sSelDb[:], 0.0), sem="zz")
        sched.emit("gpsimd", lambda e: e.memset(sSelGb[:], 0.0), sem="zz")
        sched.emit("gpsimd", lambda e: e.memset(sA8e[:], 0.0), sem="zz")
        sched.emit("gpsimd", lambda e: e.memset(sB16e[:], 0.0), sem="zz")
        sched.emit("gpsimd", lambda e: e.memset(sBiasPad[:], 0.0), sem="zz")
        sched.emit("gpsimd", lambda e: e.memset(sE0[:], 0.0), sem="zz")
        zz_all = ("zz", sched.cnt["zz"])

        casts = []
        for dst, src in [(sG1b, sG1), (sG1ib, sG1i), (sG2b, sG2),
                         (sG2ib, sG2i), (sOnesB, sOnes)]:
            casts.append(sched.emit("vector",
                                    lambda e, d=dst, s=src: e.tensor_copy(d[:], s[:]),
                                    waits=[ld_all]))
        casts.append(sched.emit("vector", lambda e: e.tensor_copy(sSelDb[0:8, :], sSelD[:]),
                                waits=[ld_all, zz_all]))
        casts.append(sched.emit("vector", lambda e: e.tensor_copy(sSelGb[0:16, :], sSelG[:]),
                                waits=[ld_all, zz_all]))
        casts.append(sched.emit("vector", lambda e: e.tensor_copy(sBiasPad[0:1, :], sBias[:]),
                                waits=[ld_all, zz_all]))
        casts.append(sched.emit("vector", lambda e: e.tensor_copy(sE0[0:1, :], sOnes[:]),
                                waits=[ld_all, zz_all]))
        casts.append(zz_all)

        for t in range(NT):
            tc = sched.emit("vector", lambda e, t=t: e.tensor_copy(sXb[:, t, :], sX[:, t, :]),
                            waits=[ld_all])
            sched.dma("sync", sFM[:, :, t * 128:(t + 1) * 128], sXb[:, t, :],
                      waits=[tc], transpose=True, sem="tr")
        tFM = ("tr", sched.cnt["tr"])

        wq_done = []
        ch = None
        for which in range(2):
            g1t = sG1b if which == 0 else sG1ib
            g2t = sG2b if which == 0 else sG2ib
            wdst = sWr if which == 0 else sWi
            for q in range(4):
                for c in range(4):
                    ch = sched.emit("vector", lambda e, c=c, q=q, g1t=g1t: e.tensor_copy(
                        sA8e[0:8, :], g1t[:, q * 4 + c, :, None]
                        .broadcast_to([8, 32, 16])), waits=casts + [ch])
                    ch = sched.emit("vector", lambda e, c=c, g2t=g2t: e.tensor_copy(
                        sB16e[0:16, :], g2t[:, c, None, :]
                        .broadcast_to([16, 32, 16])), waits=casts + [ch])
                    ch = sched.emit("tensor", lambda e: e.matmul(
                        pA[:, :], lhsT=sSelDb[:], rhs=sA8e[:], start=True, stop=True),
                        waits=casts + [ch])
                    ch = sched.emit("tensor", lambda e: e.matmul(
                        pB[:, :], lhsT=sSelGb[:], rhs=sB16e[:], start=True, stop=True),
                        waits=casts + [ch])
                    ch = sched.emit("vector", lambda e: e.tensor_copy(sP1[:], pA[:, :]),
                                    waits=[ch])
                    ch = sched.emit("vector", lambda e: e.tensor_tensor(
                        out=sTmp[:], in0=sP1[:], in1=pB[:, :], op=ALU.mult), waits=[ch])
                    if c == 0:
                        ch = sched.emit("vector", lambda e, w=wdst[q]: e.tensor_copy(
                            w[:], sTmp[:]), waits=[ch])
                    else:
                        ch = sched.emit("vector", lambda e, w=wdst[q]: e.tensor_tensor(
                            out=w[:], in0=w[:], in1=sTmp[:], op=ALU.add), waits=[ch])
                wq_done.append(ch)

        asm = []
        for q in range(4):
            asm.append(sched.emit("vector", lambda e, q=q: e.tensor_copy(
                sRhs[:, q, 0:512], sWr[q][:]), waits=wq_done))
            asm.append(sched.emit("vector", lambda e, q=q: e.tensor_copy(
                sRhs[:, q, 512:1024], sWi[q][:]), waits=wq_done))
            asm.append(sched.emit("vector", lambda e, q=q: e.tensor_scalar(
                out=sRhs[:, q + 4, 0:512], in0=sWi[q][:], scalar1=-1.0, scalar2=None,
                op0=ALU.mult), waits=wq_done))
            asm.append(sched.emit("vector", lambda e, q=q: e.tensor_copy(
                sRhs[:, q + 4, 512:1024], sWr[q][:]), waits=wq_done))

        if _DEBUG_B:
            sched.dma("sync", dbgW, sRhs[:], waits=asm, sem="stA")

        bank_free = {0: None, 1: None}
        out_free = [None, None]
        prev_bn = (None, None)
        store_toks = []
        sched.set_flush(sE0[:], sBiasPad[:, 0:128], pF)
        for t in range(NT):
            buf = sOut[t % 2]
            tr_toks = []
            for half, bank in ((0, pC), (1, pD)):
                deps = asm + [tFM]
                if bank_free[half]:
                    deps = deps + [bank_free[half]]
                if out_free[t % 2]:
                    deps = deps + [out_free[t % 2]]
                for ic in range(8):
                    sched.emit("tensor",
                               lambda e, t=t, ic=ic, half=half, bank=bank: e.matmul(
                                   bank[:, :], lhsT=sFM[:, ic, t * 128:(t + 1) * 128],
                                   rhs=sRhs[:, ic, half * 512:(half + 1) * 512],
                                   start=(ic == 0), stop=False), waits=deps)
                sched.emit("tensor", lambda e, half=half, bank=bank: e.matmul(
                    bank[:, :], lhsT=sE0[:], rhs=sBiasPad[:, half * 512:(half + 1) * 512],
                    start=False, stop=True), waits=deps)
            tfl = sched.flush()
            for _ in range(_FLUSH_N - 1):
                tfl = sched.flush()
            for half, bank in ((0, pC), (1, pD)):
                tr = sched.emit("vector", lambda e, t=t, half=half, bank=bank, buf=buf:
                                e.tensor_tensor(out=buf[:, half * 512:(half + 1) * 512],
                                                in0=sX[:, t, half * 512:(half + 1) * 512],
                                                in1=bank[:, :], op=ALU.add),
                                waits=[tfl] + ([out_free[t % 2]] if out_free[t % 2] else []))
                bank_free[half] = tr
                tr_toks.append(tr)
            if _SERIAL_LN:
                sched.serial = True
            tln, tbn = emit_layernorm(sched, buf[:], buf[:], sBn6, sBn2, sScr, 1e-5,
                                      deps=tr_toks, prev=prev_bn)
            sched.serial = _SERIAL
            prev_bn = tbn
            tst = sched.dma("sync", x2r[:, t, :], buf[:], waits=[tln],
                            sem="stA" if t % 2 == 0 else "stB")
            out_free[t % 2] = tst
            store_toks.append(tst)

        sched.barrier("gpsimd", [("stA", sched.cnt["stA"]), ("stB", sched.cnt["stB"])])
        block = stack.enter_context(nc.Block())
        sched.build(block)
    return nc


def host_B_inputs(x1_shard, G1_R, G2_R, G1_I, G2_I, bias_real, bias_imag):
    p = np.arange(128)
    seld = (p[None, :] // 16 == np.arange(8)[:, None]).astype(np.float32)
    selg = (p[None, :] % 16 == np.arange(16)[:, None]).astype(np.float32)
    def packA(G1):
        A = np.transpose(np.asarray(G1)[0], (2, 1, 0))
        A = A.reshape(4, 8, 4, 32).transpose(1, 0, 2, 3)
        return np.ascontiguousarray(A.reshape(8, 16, 32), np.float32)

    def packB(G2):
        B = np.transpose(np.asarray(G2)[:, :, 0, :], (2, 0, 1))
        return np.ascontiguousarray(B, np.float32)

    return {
        "x1": np.ascontiguousarray(x1_shard, np.float32),
        "G1r": packA(G1_R), "G1i": packA(G1_I),
        "G2r": packB(G2_R), "G2i": packB(G2_I),
        "biasri": np.concatenate([np.asarray(bias_real), np.asarray(bias_imag)])[None, :].astype(np.float32),
        "seld": seld, "selg": selg,
        "onesr": np.ones((1, 128), np.float32),
    }



def build_A2():
    nc = bass.Bass("TRN2", target_bir_lowering=False)
    NT = 16
    DFF = 2736
    NS = 22
    slabw = [128] * 21 + [48]
    x = nc.dram_tensor("x", [2048, 1024], F32, kind="ExternalInput").ap()
    oT = nc.dram_tensor("oT", [1024, 2048], BF16, kind="ExternalInput").ap()
    Wo = nc.dram_tensor("Wo", [1024, 1024], F32, kind="ExternalInput").ap()
    w1 = nc.dram_tensor("w1", [1024, DFF], F32, kind="ExternalInput").ap()
    w2 = nc.dram_tensor("w2", [1024, DFF], F32, kind="ExternalInput").ap()
    w3 = nc.dram_tensor("w3", [DFF, 1024], F32, kind="ExternalInput").ap()
    gff = nc.dram_tensor("gff", [128, 8], F32, kind="ExternalInput").ap()
    onesr = nc.dram_tensor("onesr", [1, 128], F32, kind="ExternalInput").ap()
    x1o = nc.dram_tensor("x1o", [2048, 1024], F32, kind="ExternalOutput").ap()
    x1r = x1o.rearrange("(t p) d -> p t d", p=128)
    if _DEBUG_A2:
        dbg1 = nc.dram_tensor("dbg1", [128, 16, 1024], F32, kind="ExternalOutput").ap()
        dbg2 = nc.dram_tensor("dbg2", [128, 8, 2048], BF16, kind="ExternalOutput").ap()

    with ExitStack() as stack:
        sched = Sched(nc, stack)
        sched.serial = _SERIAL
        sb = lambda n, s, d=F32: stack.enter_context(nc.sbuf_tensor(n, s, d))
        ps = lambda n, s, d=F32: stack.enter_context(nc.psum_tensor(n, s, d))

        sR = sb("sR", [128, NT, 1024])
        sU = sb("sU", [128, 16384], BF16)
        sOT = sU[:, 0:16384].rearrange("p (c t) -> p c t", c=8)
        sP = sU[:, 0:11264].rearrange("p (c t) -> p c t", c=22)
        sStage = sb("sStage", [128, 8, 128])
        sStage2 = sb("sStage2", [128, 8, 128])
        sStage3 = sStage[:].rearrange("p c o -> p (c o)")
        sStage4 = sStage2[:].rearrange("p c o -> p (c o)")
        sWoB = sb("sWoB", [128, 8, 1024], BF16)
        sW3B = sb("sW3B", [128, 22, 1024], BF16)
        sW1B = [sb("sW1Ba", [128, 8, 128], BF16), sb("sW1Bb", [128, 8, 128], BF16)]
        sW2B = [sb("sW2Ba", [128, 8, 128], BF16), sb("sW2Bb", [128, 8, 128], BF16)]
        sXF = sb("sXF", [128, 8, 2048], BF16)
        sXT = sb("sXT", [128, 1024], BF16)
        sSilu = sXT[:, 0:512]
        sUp = sXT[:, 512:1024]
        sG = sb("sG", [128, 8])
        sOnes = sb("sOnes", [1, 128])
        sE0 = sb("sE0", [128, 128], BF16)
        sBn6 = sb("sBn6", [128, 12]); sBn2 = sb("sBn2", [128, 2]); sScr = sb("sScr", [128, 6])
        _wof = sWoB[:].rearrange("p c o -> p (c o)")
        sOut = [_wof[:, 0:2048].bitcast(F32), _wof[:, 2048:4096].bitcast(F32)]
        sXr = [_wof[:, 4096:6144].bitcast(F32), _wof[:, 6144:8192].bitcast(F32)]
        pA = ps("pA", [128, 512])
        pB = ps("pB", [128, 512])
        pF = ps("pF", [128, 512])

        sched.dma("sync", sR[:], x.rearrange("(t p) d -> p t d", p=128), sem="ld")
        sched.dma("sync", sOT, oT.rearrange("(c p) t -> p c t", p=128), sem="ld")
        sched.dma("sync", sG[:], gff, sem="ld")
        sched.dma("sync", sOnes[:], onesr, sem="ld")
        ld_all = ("ld", sched.cnt["ld"])
        z1 = sched.emit("gpsimd", lambda e: e.memset(sE0[:], 0.0), sem="zz")
        z2 = sched.emit("gpsimd", lambda e: e.memset(sP[:, 21, :], 0.0), sem="zz")
        z3 = sched.emit("gpsimd", lambda e: e.memset(sW3B[:, 21, :], 0.0), sem="zz")
        zz_all = ("zz", sched.cnt["zz"])
        tE0 = sched.emit("vector", lambda e: e.tensor_copy(sE0[0:1, :], sOnes[:]),
                         waits=[ld_all, zz_all])

        cast_wo = []
        for j in range(8):
            stg = sStage if j % 2 == 0 else sStage2
            sched.dma("sync", stg[:],
                      Wo.rearrange("(c p) o -> p c o", p=128)[:, :, j * 128:(j + 1) * 128],
                      sem=f"wo{j % 2}",
                      waits=[cast_wo[j - 2]] if j >= 2 else [])
            tc = sched.emit("vector", lambda e, j=j, stg=stg: e.tensor_copy(
                sWoB[:, :, j * 128:(j + 1) * 128], stg[:]),
                waits=[(f"wo{j % 2}", sched.cnt[f"wo{j % 2}"])])
            cast_wo.append(tc)
        cast_w3 = []
        for c in range(NS):
            stg = sStage3 if c % 2 == 0 else sStage4
            rows = slabw[c]
            sched.dma("sync", stg[0:rows, :], w3[c * 128:c * 128 + rows, :],
                      sem=f"w3{c % 2}",
                      waits=[cast_w3[c - 2]] if c >= 2 else [cast_wo[-2], cast_wo[-1]])
            tc = sched.emit("vector", lambda e, c=c, stg=stg, rows=rows: e.tensor_copy(
                sW3B[0:rows, c, :], stg[0:rows, :]),
                waits=[(f"w3{c % 2}", sched.cnt[f"w3{c % 2}"]), zz_all])
            cast_w3.append(tc)
        w3_ready = [cast_w3[-1]]

        sched.set_flush(sE0[:], sE0[:], pF)

        bank_free = {0: None, 1: None}
        prev_rms = None
        prev_tt = None
        for t in range(NT):
            mm_deps = [cast_wo[-1], ld_all, tE0]
            for half, bank in ((0, pA), (1, pB)):
                d = mm_deps + ([bank_free[half]] if bank_free[half] else [])
                for ic in range(8):
                    sched.emit("tensor", lambda e, t=t, ic=ic, half=half, bank=bank: e.matmul(
                        bank[:, :], lhsT=sOT[:, ic, t * 128:(t + 1) * 128],
                        rhs=sWoB[:, ic, half * 512:(half + 1) * 512],
                        start=(ic == 0), stop=(ic == 7)), waits=d)
            tfl = sched.flush()
            radd = []
            for half, bank in ((0, pA), (1, pB)):
                tr = sched.emit("vector", lambda e, t=t, half=half, bank=bank: e.tensor_tensor(
                    out=sR[:, t, half * 512:(half + 1) * 512],
                    in0=sR[:, t, half * 512:(half + 1) * 512],
                    in1=bank[:, :], op=ALU.add), waits=[tfl])
                bank_free[half] = tr
                radd.append(tr)
            s1 = sched.emit("vector", lambda e, t=t: e.bn_stats(sBn6[:, 0:6], sR[:, t, 0:512]),
                            waits=radd)
            s2 = sched.emit("vector", lambda e, t=t: e.bn_stats(sBn6[:, 6:12], sR[:, t, 512:1024]),
                            waits=radd)
            s3 = sched.emit("vector", lambda e: e.bn_aggr(sBn2[:], sBn6[:]), waits=[s1, s2])
            s4 = sched.emit("vector", lambda e: e.tensor_scalar(
                out=sScr[:, 0:1], in0=sBn2[:, 0:1], scalar1=sBn2[:, 0:1], scalar2=sBn2[:, 1:2],
                op0=ALU.mult, op1=ALU.add), waits=[s3, prev_rms])
            s5 = sched.emit("vector", lambda e: e.tensor_scalar(
                out=sScr[:, 1:2], in0=sScr[:, 0:1], scalar1=1e-6, scalar2=None,
                op0=ALU.add), waits=[s4])
            t3 = sched.emit("scalar", lambda e: e.activation(out=sScr[:, 2:3], in_=sScr[:, 1:2],
                                                             func=AF.Ln), waits=[s5])
            t4 = sched.emit("scalar", lambda e: e.activation(out=sScr[:, 3:4], in_=sScr[:, 2:3],
                                                             func=AF.Exp, scale=-0.5), waits=[t3])
            prev_rms = t4
            t5 = sched.emit("scalar", lambda e, t=t: e.activation(
                out=sXT[:], in_=sR[:, t, :], func=AF.Identity, scale=sScr[:, 3:4]),
                waits=[t4, prev_tt])
            prev_tt = sched.dma("sync", sXF[:, :, t * 128:(t + 1) * 128], sXT[:],
                                waits=[t5], transpose=True, sem="trx")
        txf = ("trx", sched.cnt["trx"])
        gf = []
        for c in range(8):
            gf.append(sched.emit("vector", lambda e, c=c: e.tensor_scalar(
                out=sXF[:, c, :], in0=sXF[:, c, :], scalar1=sG[:, c:c + 1], scalar2=None,
                op0=ALU.mult), waits=[txf, ld_all]))
        gf = [gf[-1]]

        if _DEBUG_A2:
            sched.dma("sync", dbg1, sR[:], waits=gf, sem="stA")
            sched.dma("sync", dbg2, sXF[:], waits=gf, sem="stA")

        ph1_pe = ("tensor", sched.cnt["tensor"])
        xr_free = [ph1_pe, ph1_pe]
        out_free = [None, None]
        prev_bn = (None, None)
        pfree = None
        wbuf_free = [cast_wo[-1], cast_wo[-1]]
        e1_prev = e2_prev = None
        for q4 in range(4):
            tok0 = q4 * 512
            prod_toks = []
            for sl in range(NS):
                rows = slabw[sl]
                par = sl % 2
                d0 = [wbuf_free[par]] if wbuf_free[par] else []
                sched.dma("sync", sStage[:, :, 0:rows],
                          w1[:, sl * 128:sl * 128 + rows].rearrange("(c p) o -> p c o", p=128),
                          sem=f"sw{par}", waits=d0)
                sched.dma("sync", sStage2[:, :, 0:rows],
                          w2[:, sl * 128:sl * 128 + rows].rearrange("(c p) o -> p c o", p=128),
                          sem=f"sw{par}", waits=d0)
                tld = (f"sw{par}", sched.cnt[f"sw{par}"])
                c1 = sched.emit("vector", lambda e, par=par, rows=rows: e.tensor_copy(
                    sW1B[par][:, :, 0:rows], sStage[:, :, 0:rows]), waits=[tld])
                c2 = sched.emit("vector", lambda e, par=par, rows=rows: e.tensor_copy(
                    sW2B[par][:, :, 0:rows], sStage2[:, :, 0:rows]), waits=[tld])
                wbuf_free[par] = c2
                mmdeps = gf + [c1, c2]
                for ic in range(8):
                    sched.emit("tensor", lambda e, ic=ic, tok0=tok0, par=par:
                               e.matmul(pA[:, :], lhsT=sW1B[par][:, ic, :],
                                        rhs=sXF[:, ic, tok0:tok0 + 512],
                                        start=(ic == 0), stop=(ic == 7)),
                               waits=mmdeps + ([e1_prev] if e1_prev else []))
                for ic in range(8):
                    sched.emit("tensor", lambda e, ic=ic, tok0=tok0, par=par:
                               e.matmul(pB[:, :], lhsT=sW2B[par][:, ic, :],
                                        rhs=sXF[:, ic, tok0:tok0 + 512],
                                        start=(ic == 0), stop=(ic == 7)),
                               waits=mmdeps + ([e2_prev] if e2_prev else []))
                tfl = sched.flush()
                e1 = sched.emit("scalar", lambda e, rows=rows: e.activation(
                    out=sSilu[0:rows, :], in_=pA[0:rows, :], func=AF.Silu),
                    waits=[tfl, pfree])
                e2 = sched.emit("scalar", lambda e, rows=rows: e.activation(
                    out=sUp[0:rows, :], in_=pB[0:rows, :], func=AF.Identity),
                    waits=[tfl, pfree])
                e1_prev, e2_prev = e1, e2
                tp = sched.emit("vector", lambda e, sl=sl, rows=rows: e.tensor_tensor(
                    out=sP[0:rows, sl, :], in0=sSilu[0:rows, :], in1=sUp[0:rows, :],
                    op=ALU.mult), waits=[e1, e2] + ([pfree] if pfree else []))
                prod_toks.append(tp)
            prod_toks = [prod_toks[-1]]
            for tt_ in range(4):
                t = q4 * 4 + tt_
                buf = sOut[t % 2]
                for half, bank in ((0, pA), (1, pB)):
                    dd = prod_toks + w3_ready + ([bank_free[half]] if bank_free[half] else [])
                    for c in range(NS):
                        sched.emit("tensor", lambda e, c=c, tt_=tt_, half=half, bank=bank:
                                   e.matmul(bank[:, :],
                                            lhsT=sP[:, c, tt_ * 128:(tt_ + 1) * 128],
                                            rhs=sW3B[:, c, half * 512:(half + 1) * 512],
                                            start=(c == 0), stop=(c == NS - 1)), waits=dd)
                tfl = sched.flush()
                xrb = sXr[t % 2]
                txr = sched.dma("sync", xrb, x.rearrange("(t p) d -> p t d", p=128)[:, t, :],
                                sem=f"xr{t % 2}",
                                waits=[xr_free[t % 2]] if xr_free[t % 2] else [])
                tr_toks = []
                for half, bank in ((0, pA), (1, pB)):
                    tr = sched.emit("vector", lambda e, t=t, half=half, bank=bank, buf=buf:
                                    e.tensor_tensor(out=buf[:, half * 512:(half + 1) * 512],
                                                    in0=sR[:, t, half * 512:(half + 1) * 512],
                                                    in1=bank[:, :], op=ALU.add),
                                    waits=[tfl] + ([out_free[t % 2]] if out_free[t % 2] else []))
                    bank_free[half] = tr
                    tr2 = sched.emit("vector", lambda e, half=half, buf=buf, xrb=xrb:
                                     e.tensor_tensor(out=buf[:, half * 512:(half + 1) * 512],
                                                     in0=buf[:, half * 512:(half + 1) * 512],
                                                     in1=xrb[:, half * 512:(half + 1) * 512],
                                                     op=ALU.add), waits=[tr, txr])
                    tr_toks.append(tr2)
                xr_free[t % 2] = tr_toks[-1]
                tln, tbn = emit_layernorm(sched, buf, buf, sBn6, sBn2, sScr, 1e-5,
                                          deps=tr_toks, prev=prev_bn)
                prev_bn = tbn
                tst = sched.dma("sync", x1r[:, t, :], buf, waits=[tln],
                                sem="stA" if t % 2 == 0 else "stB")
                out_free[t % 2] = tst
            pfree = bank_free[1]
            e1_prev = bank_free[0]
            e2_prev = bank_free[1]

        sched.barrier("gpsimd", [("stA", sched.cnt["stA"]), ("stB", sched.cnt["stB"])])
        block = stack.enter_context(nc.Block())
        sched.build(block)
    return nc


def host_A2_inputs(x_shard, oT_shard, Wo, w1, w2, w3, g_ffn):
    return {
        "x": np.ascontiguousarray(x_shard, np.float32),
        "oT": np.ascontiguousarray(oT_shard).astype(BFDT),
        "Wo": np.ascontiguousarray(Wo, np.float32),
        "w1": np.ascontiguousarray(w1, np.float32),
        "w2": np.ascontiguousarray(w2, np.float32),
        "w3": np.ascontiguousarray(w3, np.float32),
        "gff": np.ascontiguousarray(np.asarray(g_ffn, np.float32).reshape(8, 128).T),
        "onesr": np.ones((1, 128), np.float32),
    }



def build_C():
    nc = bass.Bass("TRN2", target_bir_lowering=False)
    NTK = 32
    NTQ = 16
    x2kv = nc.dram_tensor("x2kv", [4096, 1024], F32, kind="ExternalInput").ap()
    WqT = nc.dram_tensor("WqT", [1024, 1024], F32, kind="ExternalInput").ap()
    WkT = nc.dram_tensor("WkT", [1024, 1024], F32, kind="ExternalInput").ap()
    WvT = nc.dram_tensor("WvT", [1024, 1024], F32, kind="ExternalInput").ap()
    WoT = nc.dram_tensor("WoT", [1024, 1024], F32, kind="ExternalInput").ap()
    bqc = nc.dram_tensor("bqc", [128, 8], F32, kind="ExternalInput").ap()
    bkc = nc.dram_tensor("bkc", [128, 8], F32, kind="ExternalInput").ap()
    bvr = nc.dram_tensor("bvr", [1, 1024], F32, kind="ExternalInput").ap()
    bor = nc.dram_tensor("bor", [1, 1024], F32, kind="ExternalInput").ap()
    onesb = nc.dram_tensor("onesb", [1, 128], BF16, kind="ExternalInput").ap()
    x3o = nc.dram_tensor("x3o", [2048, 1024], F32, kind="ExternalOutput").ap()
    x3r = x3o.rearrange("(t p) d -> p t d", p=128)

    with ExitStack() as stack:
        sched = Sched(nc, stack)
        sched.serial = _SERIAL
        sb = lambda n, s, d=F32: stack.enter_context(nc.sbuf_tensor(n, s, d))
        ps = lambda n, s, d=F32: stack.enter_context(nc.psum_tensor(n, s, d))

        sXF = sb("sXF", [128, 8, 4096], BF16)
        sK = sb("sK", [128, 4, 4096], BF16)
        sQ = sb("sQ", [128, 4, 2048], BF16)
        sV = sb("sV", [128, 32, 260], BF16)
        sOTb = sb("sOTb", [128, 8, 2048], BF16)
        sWoB = sb("sWoB", [128, 8, 1024], BF16)
        sWq = sb("sWq", [128, 8, 256], BF16)
        sWk = sb("sWk", [128, 8, 256], BF16)
        sWv = sb("sWv", [128, 8, 256], BF16)
        sStage = sb("sStage", [128, 8, 256])
        sXs = [sb("sXsA", [128, 1024]), sb("sXsB", [128, 1024])]
        _otf = sOTb[:].rearrange("p c t -> p (c t)")
        sXC = [_otf[:, 0:1024], _otf[:, 1024:2048]]
        sPT = [sb("sPTa", [128, 512], BF16), sb("sPTb", [128, 512], BF16)]
        sRec = sb("sRec", [128, 512], BF16)
        _stf = sStage[:].rearrange("p c o -> p (c o)")
        sRv = _stf[0:64, 0:512]
        sSr = _stf[0:1, 512:1024]
        sBoPad = _stf[:, 1024:1536].bitcast(BF16)
        sPT = sPT + [_stf[:, 1024:1280].bitcast(BF16), _stf[:, 1280:1536].bitcast(BF16)]
        sBq = sb("sBq", [128, 8]); sBk = sb("sBk", [128, 8])
        sBvR = sXs[0][:][0:1, :]
        sBoR = sXs[1][:][0:1, :]
        sBvPad = _stf[:, 1792:1920].bitcast(BF16)
        sE0 = sb("sE0", [128, 128], BF16)
        sBn6 = _stf[:, 1920:1932]; sBn2 = _stf[:, 1932:1934]; sScr = _stf[:, 1934:1940]
        sM1 = _stf[:, 1940:1946]
        sMW = _stf[:, 1946:1978]
        _qf = sQ[:].rearrange("p c t -> p (c t)")
        sOut = [_qf[:, 0:2048].bitcast(F32), _qf[:, 2048:4096].bitcast(F32)]
        sXr = [_qf[:, 4096:6144].bitcast(F32), _qf[:, 6144:8192].bitcast(F32)]
        pS = [ps("pS0", [128, 512]), ps("pS1", [128, 512])]
        pO = ps("pO", [128, 512])
        pO1 = ps("pO1", [128, 512])
        pR = ps("pR", [128, 512])
        pA = ps("pA", [128, 512])
        pB = ps("pB", [128, 512])
        pF = ps("pF", [128, 512])

        sched.dma("sync", sBq[:], bqc, sem="ld")
        sched.dma("sync", sBk[:], bkc, sem="ld")
        ld_all = ("ld", sched.cnt["ld"])
        sched.emit("gpsimd", lambda e: e.memset(sE0[:], 0.0), sem="zz")
        sched.emit("gpsimd", lambda e: e.memset(sK[:], 0.0), sem="zz")
        sched.emit("gpsimd", lambda e: e.memset(sQ[:], 0.0), sem="zz")
        sched.emit("gpsimd", lambda e: e.memset(sRec[:], 0.0), sem="zz")
        sched.emit("gpsimd", lambda e: e.memset(sM1, 0.0), sem="zz")
        for hh in range(4):
            sched.emit("gpsimd", lambda e, hh=hh: e.memset(
                sV[:, :, hh * 65 + 64:hh * 65 + 65], 1.0), sem="zz")
        zz_all = ("zz", sched.cnt["zz"])
        tE0 = sched.dma("sync", sE0[0:1, :], onesb, sem="ld", waits=[zz_all])
        tE0 = ("ld", sched.cnt["ld"])
        xc_free = [None, None]
        for t in range(NTK):
            par = t % 2
            tl = sched.dma("sync", sXs[par][:],
                           x2kv.rearrange("(t p) d -> p t d", p=128)[:, t, :],
                           sem=f"x{par}", waits=[xc_free[par]] if xc_free[par] else [])
            tc = sched.emit("vector", lambda e, par=par: e.tensor_copy(sXC[par], sXs[par][:]),
                            waits=[tl])
            ttr = sched.dma("sync", sXF[:, :, t * 128:(t + 1) * 128], sXC[par],
                            waits=[tc], transpose=True, sem="tr")
            xc_free[par] = ttr
        tXF = ("tr", sched.cnt["tr"])
        sched.dma("sync", sBvR, bvr, sem="ld2", waits=[tXF])
        sched.dma("sync", sBoR, bor, sem="ld2", waits=[tXF])
        rows_ld = ("ld2", sched.cnt["ld2"])

        cast_wo = []
        for j in range(4):
            sched.dma("sync", sStage[:],
                      WoT.rearrange("(c p) o -> p c o", p=128)[:, :, j * 256:(j + 1) * 256],
                      sem="wo", waits=[cast_wo[-1]] if cast_wo else [])
            cast_wo.append(sched.emit("vector", lambda e, j=j: e.tensor_copy(
                sWoB[:, :, j * 256:(j + 1) * 256], sStage[:]),
                waits=[("wo", sched.cnt["wo"])]))

        sched.set_flush(sE0[:], sE0[:], pF)

        g_gate = cast_wo[-1]
        wfree_prev = None
        att_done = None
        prev_exp_last = None
        prev_c3 = [None]
        prev_c5 = [None]
        att_sidx = [0]
        pend_tail = [None]
        tail_done = {}
        tail_read = {}

        def _emit_tail(sidx, pOb, slab, rbase, qp, avtok):
            c1 = sched.emit("vector", lambda e, pOb=pOb: e.tensor_copy(sSr, pOb[64:65, :]),
                            waits=[("tensor", sched.cnt["tensor"]), prev_c3[0], avtok])
            c2 = sched.emit("scalar", lambda e: e.activation(out=sSr, in_=sSr,
                                                             func=AF.Ln), waits=[c1])
            c3 = sched.emit("scalar", lambda e: e.activation(out=sRec[0:1, :],
                                                             in_=sSr, func=AF.Exp,
                                                             scale=-1.0), waits=[c2])
            prev_c3[0] = c3
            rmm = sched.emit("tensor", lambda e: e.matmul(
                pR[:, :], lhsT=sE0[:], rhs=sRec[:], start=True, stop=True), waits=[c3])
            fl2 = sched.flush()
            c4 = sched.emit("scalar", lambda e: e.activation(out=sRv, in_=pR[0:64, :],
                                                             func=AF.Identity),
                            waits=[fl2, prev_c5[0]])
            c5 = sched.emit("vector", lambda e, pOb=pOb, slab=slab, rbase=rbase, qp=qp:
                            e.tensor_tensor(
                                out=sOTb[rbase:rbase + 64, slab, qp * 512:(qp + 1) * 512],
                                in0=pOb[0:64, :], in1=sRv, op=ALU.mult), waits=[c4])
            prev_c5[0] = c5
            tail_done[sidx] = c5
            tail_read[sidx] = c5

        for g in range(4):
            wt_all = None
            for (wsrc, wdst) in ((WqT, sWq), (WkT, sWk), (WvT, sWv)):
                sched.dma("sync", sStage[:],
                          wsrc.rearrange("(c p) o -> p c o", p=128)[:, :, g * 256:(g + 1) * 256],
                          sem="wg", waits=[g_gate, wfree_prev, att_done])
                g_gate = sched.emit("vector", lambda e, wdst=wdst: e.tensor_copy(
                    wdst[:], sStage[:]), waits=[("wg", sched.cnt["wg"])])
            wt_all = g_gate
            gd = [wt_all, tXF, zz_all, ld_all, tE0]
            if att_done:
                gd.append(att_done)

            ev_prev = None
            pend = None
            kq_units = ([("K", j, p) for j in range(2) for p in range(8)] +
                        [("Q", j, p) for j in range(2) for p in range(4)])
            bank_last = {0: None, 1: None}
            for ui, (kind, j, piece) in enumerate(kq_units):
                bk = [pA, pO][ui % 2]
                wsel = sWk if kind == "K" else sWq
                for ic in range(8):
                    sched.emit("tensor", lambda e, j=j, piece=piece, ic=ic, bk=bk,
                               wsel=wsel: e.matmul(
                        bk[:, :], lhsT=wsel[:, ic, j * 128:(j + 1) * 128],
                        rhs=sXF[:, ic, piece * 512:(piece + 1) * 512],
                        start=(ic == 0), stop=(ic == 7)),
                        waits=gd + ([bank_last[ui % 2]] if bank_last[ui % 2] else []))
                tok_now = ("tensor", sched.cnt["tensor"])
                if pend is not None:
                    (pkind, pj, ppiece, pbk, pui) = pend
                    dst = sK if pkind == "K" else sQ
                    bias = sBk if pkind == "K" else sBq
                    e1 = sched.emit("scalar", lambda e, g=g, pj=pj, ppiece=ppiece, pbk=pbk,
                                    dst=dst, bias=bias: e.activation(
                        out=dst[0:64, 2 * pj, ppiece * 512:(ppiece + 1) * 512],
                        in_=pbk[0:64, :], func=AF.Identity,
                        bias=bias[0:64, g * 2 + pj:g * 2 + pj + 1]), waits=[tok_now])
                    e2 = sched.emit("scalar", lambda e, g=g, pj=pj, ppiece=ppiece, pbk=pbk,
                                    dst=dst, bias=bias: e.activation(
                        out=dst[0:64, 2 * pj + 1, ppiece * 512:(ppiece + 1) * 512],
                        in_=pbk[64:128, :], func=AF.Identity,
                        bias=bias[64:128, g * 2 + pj:g * 2 + pj + 1]), waits=[tok_now])
                    bank_last[pui % 2] = e2
                    ev_prev = e2
                pend = (kind, j, piece, bk, ui)
            (pkind, pj, ppiece, pbk, pui) = pend
            tfl = sched.flush()
            dst = sK if pkind == "K" else sQ
            bias = sBk if pkind == "K" else sBq
            e1 = sched.emit("scalar", lambda e, g=g, pj=pj, ppiece=ppiece, pbk=pbk, dst=dst,
                            bias=bias: e.activation(
                out=dst[0:64, 2 * pj, ppiece * 512:(ppiece + 1) * 512],
                in_=pbk[0:64, :], func=AF.Identity,
                bias=bias[0:64, g * 2 + pj:g * 2 + pj + 1]), waits=[tfl])
            ev_prev = sched.emit("scalar", lambda e, g=g, pj=pj, ppiece=ppiece, pbk=pbk,
                                 dst=dst, bias=bias: e.activation(
                out=dst[0:64, 2 * pj + 1, ppiece * 512:(ppiece + 1) * 512],
                in_=pbk[64:128, :], func=AF.Identity,
                bias=bias[64:128, g * 2 + pj:g * 2 + pj + 1]), waits=[tfl])

            zbv = sched.emit("gpsimd", lambda e: e.memset(sBvPad, 0.0),
                             waits=[wt_all], sem="zv")
            tbv = sched.emit("vector", lambda e, g=g: e.tensor_copy(
                sBvPad[0:1, :], sBvR[:, g * 256:(g + 1) * 256]),
                waits=[rows_ld, ("zv", sched.cnt["zv"]), att_done])
            vbank_last = {0: ev_prev, 1: ev_prev}
            vpend = None
            for t in range(NTK):
                vb = [pB, pO1][t % 2]
                for ic in range(8):
                    sched.emit("tensor", lambda e, t=t, ic=ic, vb=vb: e.matmul(
                        vb[:, 0:256], lhsT=sXF[:, ic, t * 128:(t + 1) * 128],
                        rhs=sWv[:, ic, :], start=(ic == 0), stop=False),
                        waits=gd + ([vbank_last[t % 2]] if vbank_last[t % 2] else []))
                sched.emit("tensor", lambda e, vb=vb: e.matmul(
                    vb[:, 0:256], lhsT=sE0[:], rhs=sBvPad, start=False, stop=True),
                    waits=[tbv])
                tok_now = ("tensor", sched.cnt["tensor"])
                if vpend is not None:
                    (pt, pvb) = vpend
                    ev_prev = sched.emit("vector", lambda e, pt=pt, pvb=pvb: e.tensor_copy(
                        sV[:, pt, 0:260].rearrange("p (h v) -> p h v", h=4)[:, :, 0:64],
                        pvb[:, 0:256].rearrange("p (h v) -> p h v", h=4)), waits=[tok_now])
                    vbank_last[pt % 2] = ev_prev
                vpend = (t, vb)
            (pt, pvb) = vpend
            tfl = sched.flush()
            ev_prev = sched.emit("vector", lambda e, pt=pt, pvb=pvb: e.tensor_copy(
                sV[:, pt, 0:260].rearrange("p (h v) -> p h v", h=4)[:, :, 0:64],
                pvb[:, 0:256].rearrange("p (h v) -> p h v", h=4)), waits=[tfl])
            qkv_done = ev_prev
            wfree_prev = qkv_done

            for hh in range(4):
                H = g * 4 + hh
                slab = H // 2
                rbase = (H % 2) * 64
                for qpair in range(2):
                    qpA, qpB = 2 * qpair, 2 * qpair + 1
                    sidx = att_sidx[0]
                    att_sidx[0] += 2
                    banksB = [pA, pB]
                    ptA = [sPT[0], sPT[1]]
                    ptB = [sPT[2], sPT[3]]
                    expA = {}; expB = {}
                    mmA = {}; mmB = {}
                    avA = avB = None
                    for sl in range(NTK):
                        par = sl % 2
                        wA = [qkv_done]
                        wB = [qkv_done]
                        if sl >= 2:
                            wA.append(expA[sl - 2]); wB.append(expB[sl - 2])
                        if sl < 2 and prev_exp_last is not None:
                            wA.append(prev_exp_last); wB.append(prev_exp_last)
                        mmA[sl] = sched.emit("tensor", lambda e, hh=hh, sl=sl, qpA=qpA,
                                             par=par: e.matmul(
                            pS[par][:, :],
                            lhsT=sK[:, hh, sl * 128:(sl + 1) * 128],
                            rhs=sQ[:, hh, qpA * 512:(qpA + 1) * 512],
                            start=True, stop=True), waits=wA)
                        mmB[sl] = sched.emit("tensor", lambda e, hh=hh, sl=sl, qpB=qpB,
                                             par=par, banksB=banksB: e.matmul(
                            banksB[par][:, :], lhsT=sK[:, hh, sl * 128:(sl + 1) * 128],
                            rhs=sQ[:, hh, qpB * 512:(qpB + 1) * 512],
                            start=True, stop=True), waits=wB)
                        if sl == 0 and pend_tail[0] is not None:
                            for tt in pend_tail[0]:
                                _emit_tail(*tt)
                            pend_tail[0] = None
                        if sl >= 1:
                            ep = (sl - 1) % 2
                            expA[sl - 1] = sched.emit("scalar", lambda e, ep=ep, ptA=ptA:
                                e.activation(out=ptA[ep][:, :], in_=pS[ep][:, :], func=AF.Exp,
                                             scale=1.0 / 32.0), waits=[mmA[sl]])
                            expB[sl - 1] = sched.emit("scalar", lambda e, ep=ep, ptB=ptB,
                                                      banksB=banksB:
                                e.activation(out=ptB[ep], in_=banksB[ep][:, :],
                                             func=AF.Exp, scale=1.0 / 32.0), waits=[mmB[sl]])
                            awA = [expA[sl - 1]]
                            awB = [expB[sl - 1]]
                            if sl - 1 == 0:
                                awA.append(tail_done.get(sidx - 2))
                                awB.append(tail_done.get(sidx - 1))
                            avA = sched.emit("tensor", lambda e, hh=hh, sl=sl, ep=ep, ptA=ptA:
                                e.matmul(pO[0:65, :],
                                         lhsT=sV[:, sl - 1, hh * 65:hh * 65 + 65],
                                         rhs=ptA[ep][:, :], start=(sl - 1 == 0), stop=False),
                                waits=awA)
                            avB = sched.emit("tensor", lambda e, hh=hh, sl=sl, ep=ep, ptB=ptB:
                                e.matmul(pO1[0:65, :],
                                         lhsT=sV[:, sl - 1, hh * 65:hh * 65 + 65],
                                         rhs=ptB[ep], start=(sl - 1 == 0), stop=False),
                                waits=awB)
                    tfl0 = sched.flush()
                    ep = (NTK - 1) % 2
                    eA = sched.emit("scalar", lambda e, ep=ep, ptA=ptA: e.activation(
                        out=ptA[ep][:, :], in_=pS[ep][:, :], func=AF.Exp,
                        scale=1.0 / 32.0), waits=[tfl0])
                    eB = sched.emit("scalar", lambda e, ep=ep, ptB=ptB, banksB=banksB:
                        e.activation(out=ptB[ep], in_=banksB[ep][:, :], func=AF.Exp,
                                     scale=1.0 / 32.0), waits=[tfl0, eA])
                    prev_exp_last = eB
                    avA = sched.emit("tensor", lambda e, hh=hh, ep=ep, ptA=ptA: e.matmul(
                        pO[0:65, :], lhsT=sV[:, NTK - 1, hh * 65:hh * 65 + 65],
                        rhs=ptA[ep][:, :], start=False, stop=True), waits=[eA])
                    avB = sched.emit("tensor", lambda e, hh=hh, ep=ep, ptB=ptB: e.matmul(
                        pO1[0:65, :], lhsT=sV[:, NTK - 1, hh * 65:hh * 65 + 65],
                        rhs=ptB[ep], start=False, stop=True), waits=[eB])
                    pend_tail[0] = [(sidx, pO, slab, rbase, qpA, avA),
                                    (sidx + 1, pO1, slab, rbase, qpB, avB)]
            if pend_tail[0] is not None:
                for tt in pend_tail[0]:
                    _emit_tail(*tt)
                pend_tail[0] = None
            att_done = tail_done[att_sidx[0] - 1]

        zbo = sched.emit("gpsimd", lambda e: e.memset(sBoPad, 0.0),
                         waits=[g_gate, att_done, prev_c5[0]], sem="zb")
        tbo = sched.emit("vector", lambda e: e.tensor_copy(sBoPad[0:1, :], sBoR),
                         waits=[("zb", sched.cnt["zb"]), rows_ld])
        ot_done = att_done
        bank_free = {0: None, 1: None}
        out_free = [None, None]
        xr_free = [("tensor", sched.cnt["tensor"]), ("tensor", sched.cnt["tensor"])]
        prev_bn = (None, None)
        mprev = None
        mtoks = {}
        for t in range(NTQ):
            buf = sOut[t % 2]
            for half, bank in ((0, pA), (1, pB)):
                dd = [ot_done, cast_wo[-1]] + ([bank_free[half]] if bank_free[half] else [])
                for ic in range(8):
                    sched.emit("tensor", lambda e, t=t, ic=ic, half=half, bank=bank: e.matmul(
                        bank[:, :], lhsT=sOTb[:, ic, t * 128:(t + 1) * 128],
                        rhs=sWoB[:, ic, half * 512:(half + 1) * 512],
                        start=(ic == 0), stop=False), waits=dd)
                sched.emit("tensor", lambda e, half=half, bank=bank: e.matmul(
                    bank[:, :], lhsT=sE0[:], rhs=sBoPad[:, half * 512:(half + 1) * 512],
                    start=False, stop=True), waits=[tbo])
            tfl = sched.flush()
            xrb = sXr[t % 2]
            txr = sched.dma("sync", xrb,
                            x2kv.rearrange("(t p) d -> p t d", p=128)[:, t, :],
                            sem=f"xr{t % 2}",
                            waits=[xr_free[t % 2]])
            tr_toks = []
            for half, bank in ((0, pA), (1, pB)):
                tr = sched.emit("vector", lambda e, half=half, bank=bank, buf=buf, xrb=xrb:
                                e.tensor_tensor(out=buf[:, half * 512:(half + 1) * 512],
                                                in0=xrb[:, half * 512:(half + 1) * 512],
                                                in1=bank[:, :], op=ALU.add),
                                waits=[tfl, txr] + ([out_free[t % 2]] if out_free[t % 2] else []))
                bank_free[half] = tr
                tr_toks.append(tr)
            xr_free[t % 2] = tr_toks[-1]
            tln, tbn = emit_layernorm(sched, buf, buf, sBn6, sBn2, sScr, 1e-5,
                                      deps=tr_toks, prev=prev_bn)
            prev_bn = tbn
            tst = sched.dma("sync", x3r[:, t, :], buf, waits=[tln],
                            sem="stA" if t % 2 == 0 else "stB")
            out_free[t % 2] = tst
        sched.barrier("gpsimd", [("stA", sched.cnt["stA"]), ("stB", sched.cnt["stB"])])
        block = stack.enter_context(nc.Block())
        sched.build(block)
    return nc


def build_Cm():
    nc = bass.Bass("TRN2", target_bir_lowering=False)
    x3 = nc.dram_tensor("x3", [2048, 1024], F32, kind="ExternalInput").ap()
    idf = nc.dram_tensor("idf", [128, 128], F32, kind="ExternalInput").ap()
    m1o = nc.dram_tensor("m1o", [1, 1], F32, kind="ExternalOutput").ap()
    with ExitStack() as stack:
        sched = Sched(nc, stack)
        sb = lambda n, s, d=F32: stack.enter_context(nc.sbuf_tensor(n, s, d))
        ps = lambda n, s, d=F32: stack.enter_context(nc.psum_tensor(n, s, d))
        sX = sb("sX", [128, 16, 1024])
        sMW = sb("sMW", [128, 16])
        sM1 = sb("sM1", [128, 1])
        sIdf = sb("sIdf", [128, 128])
        sM1r = sb("sM1r", [1, 128])
        pT = ps("pT", [128, 512])
        tld = sched.dma("sync", sX[:], x3.rearrange("(t p) d -> p t d", p=128), sem="ld")
        sched.dma("sync", sIdf[:], idf, sem="ld")
        ld_all = ("ld", sched.cnt["ld"])
        toks = []
        for t in range(16):
            toks.append(sched.emit("vector", lambda e, t=t: e.tensor_reduce(
                out=sMW[:, t:t + 1], in_=sX[:, t, :], axis=AX.X, op=ALU.max,
                apply_absolute_value=True), waits=[ld_all]))
        tg = sched.emit("scalar", lambda e: e.activation(out=sM1[:], in_=sMW[:, 0:1],
                                                         func=AF.Identity), waits=toks)
        tf = sched.emit("vector", lambda e: e.tensor_reduce(
            out=sM1[:, 0:1], in_=sMW[:], axis=AX.X, op=ALU.max), waits=[tg])
        tg2 = sched.emit("scalar", lambda e: e.activation(out=sMW[:, 0:1], in_=sM1[:],
                                                          func=AF.Identity), waits=[tf])
        ttp = sched.emit("tensor", lambda e: e.transpose(pT[0:1, 0:128], sM1[:, 0:1], sIdf[:]),
                         waits=[tg2, ld_all])
        tdr = sched.emit("tensor", lambda e: e.drain(), waits=[])
        tm2 = sched.emit("vector", lambda e: e.tensor_reduce(
            out=sM1r[0:1, 0:1], in_=pT[0:1, 0:128], axis=AX.X, op=ALU.max), waits=[tdr])
        sched.dma("sync", m1o, sM1r[0:1, 0:1], waits=[tm2], sem="stM")
        sched.barrier("gpsimd", [("stM", sched.cnt["stM"])])
        block = stack.enter_context(nc.Block())
        sched.build(block)
    return nc


def host_C_inputs(x2_full_rot, Wq, bq, Wk, bk, Wv, bv, Wo, bo):
    return {
        "x2kv": np.ascontiguousarray(x2_full_rot, np.float32),
        "WqT": np.ascontiguousarray(np.asarray(Wq, np.float32).T),
        "WkT": np.ascontiguousarray(np.asarray(Wk, np.float32).T),
        "WvT": np.ascontiguousarray(np.asarray(Wv, np.float32).T),
        "WoT": np.ascontiguousarray(np.asarray(Wo, np.float32).T),
        "bqc": np.ascontiguousarray(np.asarray(bq, np.float32).reshape(8, 128).T),
        "bkc": np.ascontiguousarray(np.asarray(bk, np.float32).reshape(8, 128).T),
        "bvr": np.asarray(bv, np.float32).reshape(1, 1024),
        "bor": np.asarray(bo, np.float32).reshape(1, 1024),
        "onesb": np.ones((1, 128), BFDT),
        "idf": np.eye(128, dtype=np.float32),
    }



def build_A1():
    nc = bass.Bass("TRN2", target_bir_lowering=False)
    NSTR = _A1_STRIPES
    _ser = _SERIAL or _A1_SERIAL
    x = nc.dram_tensor("x", [4096, 1024], F32, kind="ExternalInput").ap()
    Wq = nc.dram_tensor("Wq", [1024, 512], F32, kind="ExternalInput").ap()
    Wk = nc.dram_tensor("Wk", [1024, 512], F32, kind="ExternalInput").ap()
    Wv = nc.dram_tensor("Wv", [1024, 512], F32, kind="ExternalInput").ap()
    Wg = nc.dram_tensor("Wg", [1024, 512], F32, kind="ExternalInput").ap()
    gat = nc.dram_tensor("gat", [128, 8], F32, kind="ExternalInput").ap()
    triu = nc.dram_tensor("triu", [64, 64], F32, kind="ExternalInput").ap()
    blkp = nc.dram_tensor("blkp", [128, 128], F32, kind="ExternalInput").ap()
    selp = nc.dram_tensor("selp", [128, 128], F32, kind="ExternalInput").ap()
    oTo = nc.dram_tensor("oTo", [512, 4096], BF16, kind="ExternalOutput").ap()
    oTr = oTo.rearrange("(s p) t -> p s t", p=128)

    with ExitStack() as stack:
        sched = Sched(nc, stack)
        sched.serial = _ser
        sb = lambda n, s, d=F32: stack.enter_context(nc.sbuf_tensor(n, s, d))
        ps = lambda n, s, d=F32: stack.enter_context(nc.psum_tensor(n, s, d))

        sX = sb("sX", [128, 8, 1024])
        sWqB = sb("sWqB", [128, 8, 512], BF16)
        sWkB = sb("sWkB", [128, 8, 512], BF16)
        sWvB = sb("sWvB", [128, 8, 512], BF16)
        sWgB = sb("sWgB", [128, 8, 512], BF16)
        sStage = sb("sStage", [128, 8, 512])
        sXFM = sb("sXFM", [128, 8, 1024], BF16)
        sXT = sb("sXT", [128, 1024], BF16)
        sQT = sb("sQT", [64, 8, 1024], BF16)
        sKT = sb("sKT", [64, 8, 1024], BF16)
        sKTM = sb("sKTM", [128, 8, 512], BF16)
        sVTM = sb("sVTM", [128, 8, 512], BF16)
        sLnr = sX[:, 0:4, :]
        sLc = sX[:, 4:8, :]
        sE1 = sb("sE1", [128, 4, 1024], BF16)
        sE2 = sb("sE2", [128, 4, 1024], BF16)
        sEc = sb("sEc", [128, 4, 16])
        sS = sb("sS", [64, 8, 64], BF16)
        sOT = sb("sOT", [128, 4, 1024], BF16)
        sF32 = sb("sF32", [128, 1024])
        sSq = sb("sSq", [128, 1024], BF16)
        sQh = sb("sQh", [128, 1024], BF16)
        sEbf = sb("sEbf", [128, 1024])
        sRn = sb("sRn", [128, 512], BF16)
        sRsc = sb("sRsc", [128, 512])
        sTri = sb("sTri", [64, 64], BF16)
        sBlk = sb("sBlk", [128, 128], BF16)
        sSel = sb("sSel", [128, 128], BF16)
        sG = sb("sG", [128, 8])
        sSm = [sb("sSmA", [128, 64], BF16), sb("sSmB", [128, 64], BF16)]
        sT1 = sb("sT1", [64, 8, 64], BF16)
        sOev = sb("sOev", [64, 64], BF16)
        sBn6 = sb("sBn6", [128, 12]); sBn2 = sb("sBn2", [128, 2]); sScr = sb("sScr", [128, 6])
        pA = ps("pA", [128, 512])
        pB = ps("pB", [128, 512])
        pS2 = [ps("pS0", [128, 512]), ps("pS1", [128, 512])]
        pO2 = [ps("pO0", [128, 512]), ps("pO1", [128, 512])]
        pSt = ps("pSt", [128, 512])
        pF = ps("pF", [128, 512])

        sched.dma("sync", sG[:], gat, sem="ld")
        sTriF = sb("sTriF", [64, 64]); sBlkF = sb("sBlkF", [128, 128]); sSelF = sb("sSelF", [128, 128])
        sched.dma("sync", sTriF[:], triu, sem="ld")
        sched.dma("sync", sBlkF[:], blkp, sem="ld")
        sched.dma("sync", sSelF[:], selp, sem="ld")
        ld_all = ("ld", sched.cnt["ld"])
        sched.emit("gpsimd", lambda e: e.memset(sS[:], 0.0), sem="zz")
        sched.emit("gpsimd", lambda e: e.memset(sRn[:], 0.0), sem="zz")
        zz_all = ("zz", sched.cnt["zz"])
        cst = []
        cst.append(sched.emit("vector", lambda e: e.tensor_copy(sTri[:], sTriF[:]), waits=[ld_all]))
        cst.append(sched.emit("vector", lambda e: e.tensor_copy(sBlk[:], sBlkF[:]), waits=[ld_all]))
        cst.append(sched.emit("vector", lambda e: e.tensor_copy(sSel[:], sSelF[:]), waits=[ld_all]))
        wprev = None
        for (wsrc, wdst, fold) in ((Wq, sWqB, True), (Wk, sWkB, True), (Wg, sWgB, True),
                                   (Wv, sWvB, True)):
            sched.dma("sync", sStage[:], wsrc.rearrange("(c p) o -> p c o", p=128),
                      sem="wl", waits=[wprev])
            tw = ("wl", sched.cnt["wl"])
            for c in range(8):
                wprev = sched.emit("vector", lambda e, wdst=wdst, c=c: e.tensor_scalar(
                    out=wdst[:, c, :], in0=sStage[:, c, :], scalar1=sG[:, c:c + 1],
                    scalar2=None, op0=ALU.mult), waits=[tw, ld_all])
        w_all = wprev

        sched.set_flush(sBlk[:], sBlk[:], pF)

        def flush64(waits=()):
            return sched.emit("tensor", lambda e: e.matmul(
                pF[0:64, 0:64], lhsT=sTri[:], rhs=sTri[:], start=True, stop=True),
                waits=list(waits))

        st_free = None
        prev_chunk_done = None
        prev_e_done = None
        for st in range(NSTR):
            tld = sched.dma("sync", sX[:], x.rearrange("(t p) d -> p t d", p=128)
                            [:, st * 8:(st + 1) * 8, :], sem=f"xs{st % 2}",
                            waits=[st_free, prev_e_done])
            tprev_tt = None
            prev_rms = None
            tr_toks = []
            for t in range(8):
                s1 = sched.emit("vector", lambda e, t=t: e.bn_stats(sBn6[:, 0:6],
                                sX[:, t, 0:512]), waits=[tld, prev_rms])
                s2 = sched.emit("vector", lambda e, t=t: e.bn_stats(sBn6[:, 6:12],
                                sX[:, t, 512:1024]), waits=[tld, s1])
                s3 = sched.emit("vector", lambda e: e.bn_aggr(sBn2[:], sBn6[:]), waits=[s2])
                s4 = sched.emit("vector", lambda e: e.tensor_scalar(
                    out=sScr[:, 0:1], in0=sBn2[:, 0:1], scalar1=sBn2[:, 0:1],
                    scalar2=sBn2[:, 1:2], op0=ALU.mult, op1=ALU.add), waits=[s3])
                s5 = sched.emit("vector", lambda e: e.tensor_scalar(
                    out=sScr[:, 1:2], in0=sScr[:, 0:1], scalar1=1e-6, scalar2=None,
                    op0=ALU.add), waits=[s4])
                t3 = sched.emit("scalar", lambda e: e.activation(out=sScr[:, 2:3],
                                in_=sScr[:, 1:2], func=AF.Ln), waits=[s5])
                t4 = sched.emit("scalar", lambda e: e.activation(out=sScr[:, 3:4],
                                in_=sScr[:, 2:3], func=AF.Exp, scale=-0.5), waits=[t3])
                prev_rms = t4
                t5 = sched.emit("scalar", lambda e, t=t: e.activation(
                    out=sXT[:], in_=sX[:, t, :], func=AF.Identity, scale=sScr[:, 3:4]),
                    waits=[t4, tprev_tt])
                tprev_tt = sched.dma("sync", sXFM[:, :, t * 128:(t + 1) * 128], sXT[:],
                                     waits=[t5], transpose=True, sem=f"tf{st % 2}")
            tXF = (f"tf{st % 2}", sched.cnt[f"tf{st % 2}"])
            st_free = tXF

            pd = [tXF, w_all] + cst + [zz_all, prev_chunk_done]
            ev = None
            for t in range(8 if _A1_LEVEL >= 2 else 0):
                for ic in range(8):
                    sched.emit("tensor", lambda e, t=t, ic=ic: e.matmul(
                        pA[:, :], lhsT=sXFM[:, ic, t * 128:(t + 1) * 128],
                        rhs=sWvB[:, ic, :], start=(ic == 0), stop=(ic == 7)),
                        waits=pd + ([ev] if ev else []))
                tf = sched.flush()
                ev = sched.emit("vector", lambda e, t=t: e.tensor_copy(
                    sVTM[:, t, :], pA[:, :]), waits=[tf])
            v_done = ev
            for sl in range(4 if _A1_LEVEL >= 3 else 0):
                for piece in range(2):
                    for ic in range(8):
                        sched.emit("tensor", lambda e, sl=sl, piece=piece, ic=ic: e.matmul(
                            pA[:, :], lhsT=sWgB[:, ic, sl * 128:(sl + 1) * 128],
                            rhs=sXFM[:, ic, piece * 512:(piece + 1) * 512],
                            start=(ic == 0), stop=(ic == 7)),
                            waits=pd + ([ev] if ev else []))
                    tf = sched.flush()
                    e1 = sched.emit("scalar", lambda e, piece=piece: e.activation(
                        out=sEbf[:, piece * 512:(piece + 1) * 512], in_=pA[:, :],
                        func=AF.Exp, scale=-1.0), waits=[tf])
                    ev = sched.emit("scalar", lambda e, sl=sl, piece=piece: e.activation(
                        out=sLnr[:, sl, piece * 512:(piece + 1) * 512],
                        in_=sEbf[:, piece * 512:(piece + 1) * 512],
                        func=AF.Ln, bias=1.0), waits=[e1])
            lnr_done = ev
            sc = []
            for sl in range(4 if _A1_LEVEL >= 4 else 0):
                for wdw in range(16):
                    sc.append(sched.emit("vector", lambda e, sl=sl, wdw=wdw: e.tensor_tensor_scan(
                        out=sLc[:, sl, wdw * 64:(wdw + 1) * 64],
                        data0=sLnr[:, sl, wdw * 64:(wdw + 1) * 64],
                        data1=sLnr[:, sl, wdw * 64:(wdw + 1) * 64],
                        initial=0.0, op0=ALU.add, op1=ALU.bypass), waits=[lnr_done]))
            scan_done = sc[-1] if sc else lnr_done
            ee = []
            for sl in range(4 if _A1_LEVEL >= 5 else 0):
                ee.append(sched.emit("scalar", lambda e, sl=sl: e.activation(
                    out=sE1[:, sl, :], in_=sLc[:, sl, :], func=AF.Exp, scale=-1.0),
                    waits=[scan_done]))
                ee.append(sched.emit("scalar", lambda e, sl=sl: e.activation(
                    out=sE2[:, sl, :], in_=sLc[:, sl, :], func=AF.Exp, scale=1.0),
                    waits=[scan_done]))
                for cch in range(16):
                    ee.append(sched.emit("scalar", lambda e, sl=sl, cch=cch: e.activation(
                        out=sEc[:, sl, cch:cch + 1], in_=sLc[:, sl, cch * 64 + 63:cch * 64 + 64],
                        func=AF.Exp, scale=-1.0), waits=[scan_done]))
            e_done = ee[-1] if ee else scan_done
            prev_e_done = e_done
            prev_h1 = prev_n3 = None
            for (wb, dst, ebuf) in (((sWqB, sQT, sE1), (sWkB, sKT, sE2)) if _A1_LEVEL >= 6 else ()):
                for sl in range(4):
                    mm_t = None
                    for piece in range(2):
                        for ic in range(8):
                            mm_t = sched.emit("tensor", lambda e, sl=sl, piece=piece, ic=ic,
                                              wb=wb: e.matmul(
                                pB[:, :] if piece == 0 else pSt[:, :],
                                lhsT=wb[:, ic, sl * 128:(sl + 1) * 128],
                                rhs=sXFM[:, ic, piece * 512:(piece + 1) * 512],
                                start=(ic == 0), stop=(ic == 7)),
                                waits=pd + ([ev] if ev else []))
                    tf = sched.flush()
                    c1 = sched.emit("vector", lambda e: e.tensor_copy(sF32[:, 0:512],
                                    pB[:, :]), waits=[tf])
                    c2 = sched.emit("vector", lambda e: e.tensor_copy(sF32[:, 512:1024],
                                    pSt[:, :]), waits=[tf, c1])
                    q1 = sched.emit("vector", lambda e: e.tensor_tensor(
                        out=sSq[:, 0:512], in0=sF32[:, 0:512], in1=sF32[:, 0:512],
                        op=ALU.mult), waits=[c1])
                    q2 = sched.emit("vector", lambda e: e.tensor_tensor(
                        out=sSq[:, 512:1024], in0=sF32[:, 512:1024], in1=sF32[:, 512:1024],
                        op=ALU.mult), waits=[c2, q1])
                    if _A1_L6SUB < 2:
                        ev = q2
                    for piece in range(2 if _A1_L6SUB >= 2 else 0):
                        m1 = sched.emit("tensor", lambda e, piece=piece: e.matmul(
                            pA[:, :], lhsT=sBlk[:], rhs=sSq[:, piece * 512:(piece + 1) * 512],
                            start=True, stop=True), waits=[q1, q2, c1, c2, prev_h1])
                        tf2 = sched.flush()
                        n1 = sched.emit("vector", lambda e: e.tensor_scalar(
                            out=sRsc[:], in0=pA[:, :], scalar1=1e-18, scalar2=None,
                            op0=ALU.add), waits=[tf2, prev_n3])
                        if _A1_L6SUB < 3:
                            prev_h1, prev_n3 = m1, n1
                            ev = n1
                            continue
                        n2 = sched.emit("scalar", lambda e: e.activation(out=sRsc[:],
                                        in_=sRsc[:], func=AF.Ln), waits=[n1])
                        n3 = sched.emit("scalar", lambda e: e.activation(out=sRn[:],
                                        in_=sRsc[:], func=AF.Exp, scale=-0.5), waits=[n2])
                        if _A1_L6SUB < 4:
                            prev_h1, prev_n3 = m1, n3
                            ev = n3
                            continue
                        m2 = sched.emit("tensor", lambda e: e.matmul(
                            pA[:, :], lhsT=sSel[:], rhs=sRn[:], start=True, stop=True),
                            waits=[n3, zz_all])
                        tf3 = sched.flush()
                        h1 = sched.emit("vector", lambda e, piece=piece: e.tensor_tensor(
                            out=sQh[:, piece * 512:(piece + 1) * 512],
                            in0=sF32[:, piece * 512:(piece + 1) * 512], in1=pA[:, :],
                            op=ALU.mult), waits=[tf3])
                        prev_h1, prev_n3 = h1, n3
                        if _A1_L6SUB < 5:
                            prev_h1, prev_n3 = h1, n3
                            ev = h1
                            continue
                        for hloc in range(2):
                            off2 = hloc * 64
                            ev = sched.emit("vector", lambda e, sl=sl, piece=piece, dst=dst,
                                            ebuf=ebuf, off2=off2, hloc=hloc: e.tensor_tensor(
                                out=dst[0:64, 2 * sl + hloc, piece * 512:(piece + 1) * 512],
                                in0=sQh[off2:off2 + 64, piece * 512:(piece + 1) * 512],
                                in1=ebuf[off2:off2 + 64, sl, piece * 512:(piece + 1) * 512],
                                op=ALU.mult), waits=[h1, e_done])
            qk_done = ev if ev is not None else e_done
            ktm_t = []
            kcp = qk_done
            for sl in range(4 if _A1_LEVEL >= 7 else 0):
                kc1 = sched.emit("vector", lambda e, sl=sl: e.tensor_copy(
                    sSq[0:64, :], sKT[0:64, 2 * sl, :]), waits=[qk_done, kcp])
                kc2 = sched.emit("vector", lambda e, sl=sl: e.tensor_copy(
                    sSq[64:128, :], sKT[0:64, 2 * sl + 1, :]), waits=[kc1])
                tkt = sched.dma("sync", sKTM[:, :, sl * 128:(sl + 1) * 128],
                                sSq[:], waits=[kc2], transpose=True, sem=f"kt{st % 2}")
                kcp = tkt
                ktm_t.append(tkt)
            ktm_done = (f"kt{st % 2}", sched.cnt[f"kt{st % 2}"]) if sched.cnt.get(f"kt{st % 2}") else qk_done

            tdr = sched.emit("tensor", lambda e: e.drain(), waits=[])
            cdep = [ktm_done, v_done, qk_done, e_done, tdr] + cst
            prev_sm = [None, None]
            prev_oe1 = {}
            prev_oe2 = None
            prev_supd = {}
            prev_su1 = None
            oev = None
            for cch in range(_A1_CHUNKS):
                tt_ = cch // 2
                toff = (cch % 2) * 64
                tc0 = cch * 64
                stbank = [pSt, pB][cch % 2]
                pint = pO2[cch % 2]
                for hh in range(8):
                    sl = hh // 2
                    par = hh % 2
                    sc_t = sched.emit("tensor", lambda e, hh=hh, tc0=tc0, par=par: e.matmul(
                        pS2[par][0:64, 0:64], lhsT=sKT[0:64, hh, tc0:tc0 + 64],
                        rhs=sQT[0:64, hh, tc0:tc0 + 64], start=True, stop=True),
                        waits=cdep + [prev_sm[par]])
                    st_t = sched.emit("tensor", lambda e, hh=hh, tt_=tt_, toff=toff,
                                      stbank=stbank: e.matmul(
                        stbank[0:64, 0:64], lhsT=sKTM[toff:toff + 64, tt_, hh * 64:hh * 64 + 64],
                        rhs=sVTM[toff:toff + 64, tt_, hh * 64:hh * 64 + 64],
                        start=True, stop=True), waits=cdep + [prev_supd.get(hh), prev_su1])
                    f1 = flush64()
                    sm_t = sched.emit("vector", lambda e, par=par, toff=toff: e.tensor_tensor(
                        out=sSm[par][toff:toff + 64, :], in0=pS2[par][0:64, 0:64], in1=sTri[:],
                        op=ALU.mult), waits=[f1, prev_oe1.get(par)])
                    prev_sm[par] = sm_t
                    su1 = sched.emit("vector", lambda e, hh=hh, stbank=stbank: e.tensor_tensor(
                        out=sT1[0:64, hh, :], in0=sS[0:64, hh, :],
                        in1=stbank[0:64, 0:64], op=ALU.add),
                        waits=[f1, prev_supd.get(hh)])
                    prev_su1 = su1
                    oi1 = sched.emit("tensor", lambda e, hh=hh, tt_=tt_, toff=toff, par=par,
                                     pint=pint: e.matmul(
                        pint[0:64, 0:64], lhsT=sVTM[toff:toff + 64, tt_, hh * 64:hh * 64 + 64],
                        rhs=sSm[par][toff:toff + 64, :], start=True, stop=True),
                        waits=[sm_t, prev_oe1.get(par)])
                    oi2 = sched.emit("tensor", lambda e, hh=hh, tc0=tc0: e.matmul(
                        pA[0:64, 0:64], lhsT=sS[0:64, hh, :],
                        rhs=sQT[0:64, hh, tc0:tc0 + 64], start=True, stop=True),
                        waits=[prev_supd.get(hh), prev_oe2, cdep[0] if cch == 0 else None,
                               cdep[1] if cch == 0 else None])
                    f2 = flush64()
                    su2 = sched.emit("vector", lambda e, hh=hh, sl=sl, par=par, cch=cch:
                                     e.tensor_scalar(
                        out=sS[0:64, hh, :], in0=sT1[0:64, hh, :],
                        scalar1=sEc[par * 64:par * 64 + 64, sl, cch:cch + 1], scalar2=None,
                        op0=ALU.mult), waits=[su1, f2])
                    prev_supd[hh] = su2
                    oe1 = sched.emit("scalar", lambda e, pint=pint: e.activation(
                        out=sOev[:], in_=pint[0:64, 0:64], func=AF.Identity),
                        waits=[f2, prev_oe2])
                    prev_oe1[cch % 2] = oe1
                    oev = sched.emit("vector", lambda e, hh=hh, sl=sl, par=par, tc0=tc0:
                                     e.tensor_tensor(
                        out=sOT[par * 64:par * 64 + 64, sl, tc0:tc0 + 64], in0=sOev[:],
                        in1=pA[0:64, 0:64], op=ALU.add), waits=[oe1])
                    prev_oe2 = oev
            tdr2 = sched.emit("tensor", lambda e: e.drain(), waits=[])
            sched.dma("sync", oTr[:, :, st * 1024:(st + 1) * 1024], sOT[:],
                      waits=[oev, prev_supd.get(7), prev_supd.get(6), qk_done], sem="stO")
            prev_chunk_done = ("stO", sched.cnt["stO"])
            for kpar in (0, 1):
                prev_oe1[kpar] = prev_chunk_done
            prev_sm = [prev_chunk_done, prev_chunk_done]
            prev_oe2 = prev_chunk_done

        sched.barrier("gpsimd", [("stO", sched.cnt["stO"])])
        block = stack.enter_context(nc.Block())
        sched.build(block)
    return nc


def host_A1_inputs(x_b, Wq_s, Wk_s, Wv_s, Wg_s, g_attn):
    p = np.arange(128)
    tri = (np.arange(64)[:, None] <= np.arange(64)[None, :]).astype(np.float32)
    blkp = np.zeros((128, 128), np.float32)
    for j in range(2):
        blkp[:, j] = (p // 64 == j)
    selp = np.zeros((128, 128), np.float32)
    for j in range(2):
        selp[j, :] = (p // 64 == j)
    return {
        "x": np.ascontiguousarray(x_b, np.float32),
        "Wq": np.ascontiguousarray(Wq_s, np.float32),
        "Wk": np.ascontiguousarray(Wk_s, np.float32),
        "Wv": np.ascontiguousarray(Wv_s, np.float32),
        "Wg": np.ascontiguousarray(Wg_s, np.float32),
        "gat": np.ascontiguousarray(np.asarray(g_attn, np.float32).reshape(8, 128).T),
        "triu": tri, "blkp": blkp, "selp": selp,
    }



def _emit_scalar_prep(nc, sched, sb, ps, mrow_ap, clip_ap, sE0, pR, n_parts):
    sSc = sb("sScal", [1, 8])
    sPad = sb("sPad", [128, 2], BF16)
    sCols = sb("sCols", [128, 2])
    z = sched.emit("gpsimd", lambda e: e.memset(sPad[:], 0.0), sem="zz")
    t1 = sched.emit("vector", lambda e: e.tensor_reduce(
        out=sSc[0:1, 0:1], in_=mrow_ap, axis=AX.X, op=ALU.max), waits=[("ld", sched.cnt["ld"])])
    t2 = sched.emit("scalar", lambda e: e.activation(out=sSc[0:1, 1:2], in_=clip_ap,
                                                     func=AF.Sigmoid),
                    waits=[("ld", sched.cnt["ld"]), t1])
    t3 = sched.emit("vector", lambda e: e.tensor_tensor(out=sSc[0:1, 2:3], in0=sSc[0:1, 0:1],
                                                        in1=sSc[0:1, 1:2], op=ALU.mult),
                    waits=[t2, t1])
    t3 = sched.emit("vector", lambda e: e.tensor_scalar(
        out=sSc[0:1, 2:3], in0=sSc[0:1, 2:3], scalar1=1.0 / 127.0, scalar2=None,
        op0=ALU.mult), waits=[t3])
    t4 = sched.emit("vector", lambda e: e.reciprocal(sSc[0:1, 3:4], sSc[0:1, 2:3]), waits=[t3])
    t5 = sched.emit("vector", lambda e: e.tensor_scalar(
        out=sSc[0:1, 4:5], in0=sSc[0:1, 3:4], scalar1=1.0, scalar2=None, op0=ALU.mult),
        waits=[t4])
    t6 = sched.emit("vector", lambda e: e.tensor_copy(sPad[0:1, 0:1], sSc[0:1, 4:5]),
                    waits=[t5, z])
    t7 = sched.emit("vector", lambda e: e.tensor_copy(sPad[0:1, 1:2], sSc[0:1, 2:3]),
                    waits=[t6])
    m1 = sched.emit("tensor", lambda e: e.matmul(pR[:, 0:2], lhsT=sE0[:], rhs=sPad[:],
                                                 start=True, stop=True), waits=[t7])
    fl = sched.flush()
    t8 = sched.emit("vector", lambda e: e.tensor_copy(sCols[:], pR[:, 0:2]), waits=[fl])
    return sCols[:, 0:1], sCols[:, 1:2], t8


def build_D1():
    nc = bass.Bass("TRN2", target_bir_lowering=False)
    x3 = nc.dram_tensor("x3", [2048, 1024], F32, kind="ExternalInput").ap()
    m1row = nc.dram_tensor("m1row", [1, 8], F32, kind="ExternalInput").ap()
    clipx = nc.dram_tensor("clipx", [1, 1], F32, kind="ExternalInput").ap()
    wg = nc.dram_tensor("wg", [2816, 1024], F32, kind="ExternalInput").ap()
    wu = nc.dram_tensor("wu", [2816, 1024], F32, kind="ExternalInput").ap()
    onesb = nc.dram_tensor("onesb", [1, 128], BF16, kind="ExternalInput").ap()
    po = nc.dram_tensor("po", [2816, 2048], BF16, kind="ExternalOutput").ap()
    m2o = nc.dram_tensor("m2o", [1, 1], F32, kind="ExternalOutput").ap()
    idf = nc.dram_tensor("idf", [128, 128], F32, kind="ExternalInput").ap()
    dbgS = nc.dram_tensor("dbgS", [128, 6], F32, kind="ExternalOutput").ap() if _DEBUG_D1 else None
    dbgG = nc.dram_tensor("dbgG", [128, 64], BF16, kind="ExternalOutput").ap() if _DEBUG_D1 else None
    dbgU = nc.dram_tensor("dbgU", [128, 64], BF16, kind="ExternalOutput").ap() if _DEBUG_D1 else None
    dbgX = nc.dram_tensor("dbgX", [128, 64], BF16, kind="ExternalOutput").ap() if _DEBUG_D1 else None

    with ExitStack() as stack:
        sched = Sched(nc, stack)
        sched.serial = _SERIAL
        sb = lambda n, s, d=F32: stack.enter_context(nc.sbuf_tensor(n, s, d))
        ps = lambda n, s, d=F32: stack.enter_context(nc.psum_tensor(n, s, d))
        sXs = [sb("sXsA", [128, 1024]), sb("sXsB", [128, 1024])]
        sXq = sb("sXq", [128, 1024], BF16)
        sXF = sb("sXF", [128, 8, 2048], BF16)
        sWs = [sb("sWsA", [128, 1024]), sb("sWsB", [128, 1024])]
        sWq = [sb("sWqA", [128, 1024], BF16), sb("sWqB2", [128, 1024], BF16)]
        sWgF = [sb("sWgFa", [128, 8, 128], BF16), sb("sWgFb", [128, 8, 128], BF16)]
        sWuF = [sb("sWuFa", [128, 8, 128], BF16), sb("sWuFb", [128, 8, 128], BF16)]
        sQt = sb("sQt", [128, 1024])
        sQi = sb("sQi", [128, 1024], mybir.dt.int32)
        sCol = sb("sCol", [128, 6])
        sSilu = sb("sSilu", [128, 2048], BF16)
        sUp = sb("sUp", [128, 2048], BF16)
        sPp = [sb("sPpA", [128, 2048], BF16), sb("sPpB", [128, 2048], BF16)]
        sMW = sb("sMW", [128, 44])
        sM1 = sb("sM1", [128, 1])
        sM1r = sb("sM1r", [1, 128])
        sIdf = sb("sIdf", [128, 128])
        sE0 = sb("sE0", [128, 128], BF16)
        pB4 = [ps(f"p{i}", [128, 512]) for i in range(4)]
        pR = ps("pR", [128, 512])
        pF = ps("pF", [128, 512])

        sched.dma("sync", sIdf[:], idf, sem="ld")
        z0 = sched.emit("gpsimd", lambda e: e.memset(sE0[:], 0.0), sem="zz")
        te = sched.dma("sync", sE0[0:1, :], onesb, sem="ld", waits=[z0])
        zM = sched.emit("gpsimd", lambda e: e.memset(sCol[:, 0:1], MAGIC), sem="zz")
        m1l = sched.dma("sync", sM1r[0:1, 0:8].bitcast(F32) if False else sM1r[0:1, 0:8],
                        m1row, sem="ld")
        cxl = sched.dma("sync", sM1[0:1, 0:1], clipx, sem="ld")
        sched.set_flush(sE0[:], sE0[:], pF)
        rs_col, s_col, tcols = _emit_scalar_prep(nc, sched, sb, ps, sM1r[0:1, 0:8],
                                                 sM1[0:1, 0:1], sE0, pR, 128)

        xfree = [None, None]
        prev_xq = None
        for t in range(16):
            par = t % 2
            tl = sched.dma("sync", sXs[par][:], x3.rearrange("(t p) d -> p t d", p=128)[:, t, :],
                           sem=f"x{par}", waits=[xfree[par]])
            a1 = sched.emit("scalar", lambda e, par=par: e.activation(
                out=sQt[:], in_=sXs[par][:], func=AF.Identity, scale=rs_col),
                waits=[tl, tcols, prev_xq if t else None])
            a2 = sched.emit("vector", lambda e: e.tensor_copy(sQi[:], sQt[:]), waits=[a1])
            a3 = sched.emit("vector", lambda e: e.tensor_scalar(
                out=sXq[:], in0=sQi[:], scalar1=-128.0, scalar2=127.0,
                op0=ALU.max, op1=ALU.min), waits=[a2])
            prev_xq = sched.dma("sync", sXF[:, :, t * 128:(t + 1) * 128], sXq[:],
                                waits=[a3], transpose=True, sem="tq")
            xfree[par] = a1
        txq = ("tq", sched.cnt["tq"])

        wfree = [None, None]
        mm_free = None
        ev_free = None
        ppfree = [None, None]
        mtoks = []
        for sl in range(22):
            par = sl % 2
            tlg = sched.dma("sync", sWs[0][:], wg[sl * 128:(sl + 1) * 128, :],
                            sem=f"wg{0}", waits=[wfree[0]])
            am = sched.emit("vector", lambda e: e.tensor_reduce(
                out=sCol[:, 1:2], in_=sWs[0][:], axis=AX.X, op=ALU.max,
                apply_absolute_value=True), waits=[tlg])
            rc = sched.emit("vector", lambda e: e.reciprocal(sCol[:, 2:3], sCol[:, 1:2]),
                            waits=[am])
            rc2 = sched.emit("vector", lambda e: e.tensor_scalar(
                out=sCol[:, 2:3], in0=sCol[:, 2:3], scalar1=127.0, scalar2=None,
                op0=ALU.mult), waits=[rc])
            scg = sched.emit("vector", lambda e: e.tensor_scalar(
                out=sCol[:, 3:4], in0=sCol[:, 1:2], scalar1=1.0 / 127.0, scalar2=s_col,
                op0=ALU.mult, op1=ALU.mult), waits=[rc2, tcols])
            q1 = sched.emit("scalar", lambda e: e.activation(
                out=sQt[:], in_=sWs[0][:], func=AF.Identity, scale=sCol[:, 2:3]),
                waits=[rc2, ev_free])
            q2 = sched.emit("vector", lambda e: e.tensor_copy(sQi[:], sQt[:]), waits=[q1])
            q3 = sched.emit("vector", lambda e, par=par: e.tensor_scalar(
                out=sWq[0][:], in0=sQi[:], scalar1=-128.0, scalar2=127.0,
                op0=ALU.max, op1=ALU.min), waits=[q2, wfree[0]])
            tq1 = sched.dma("sync", sWgF[par][:], sWq[0][:], waits=[q3, mm_free],
                            transpose=True, sem=f"tw{par}")
            wfree[0] = tq1
            tlu = sched.dma("sync", sWs[1][:], wu[sl * 128:(sl + 1) * 128, :],
                            sem=f"wu{0}", waits=[wfree[1]])
            am2 = sched.emit("vector", lambda e: e.tensor_reduce(
                out=sCol[:, 1:2], in_=sWs[1][:], axis=AX.X, op=ALU.max,
                apply_absolute_value=True), waits=[tlu, scg])
            rcB = sched.emit("vector", lambda e: e.reciprocal(sCol[:, 2:3], sCol[:, 1:2]),
                             waits=[am2])
            rcB2 = sched.emit("vector", lambda e: e.tensor_scalar(
                out=sCol[:, 2:3], in0=sCol[:, 2:3], scalar1=127.0, scalar2=None,
                op0=ALU.mult), waits=[rcB])
            scu = sched.emit("vector", lambda e: e.tensor_scalar(
                out=sCol[:, 4:5], in0=sCol[:, 1:2], scalar1=1.0 / 127.0, scalar2=s_col,
                op0=ALU.mult, op1=ALU.mult), waits=[rcB2])
            u1 = sched.emit("scalar", lambda e: e.activation(
                out=sQt[:], in_=sWs[1][:], func=AF.Identity, scale=sCol[:, 2:3]),
                waits=[rcB2, q3])
            u2 = sched.emit("vector", lambda e: e.tensor_copy(sQi[:], sQt[:]), waits=[u1])
            u3 = sched.emit("vector", lambda e, par=par: e.tensor_scalar(
                out=sWq[1][:], in0=sQi[:], scalar1=-128.0, scalar2=127.0,
                op0=ALU.max, op1=ALU.min), waits=[u2, wfree[1]])
            tq2 = sched.dma("sync", sWuF[par][:], sWq[1][:], waits=[u3, mm_free],
                            transpose=True, sem=f"tw{par}")
            wfree[1] = tq2
            twq = (f"tw{par}", sched.cnt[f"tw{par}"])

            gmm = []
            for piece in range(4):
                for ic in range(8):
                    gmm.append(sched.emit("tensor", lambda e, piece=piece, ic=ic, par=par:
                                          e.matmul(
                        pB4[piece][:, :], lhsT=sWgF[par][:, ic, :],
                        rhs=sXF[:, ic, piece * 512:(piece + 1) * 512],
                        start=(ic == 0), stop=(ic == 7)),
                        waits=[twq, txq, te, ev_free]))
            fl = sched.flush()
            evg = []
            for piece in range(4):
                evg.append(sched.emit("scalar", lambda e, piece=piece: e.activation(
                    out=sSilu[:, piece * 512:(piece + 1) * 512], in_=pB4[piece][:, :],
                    func=AF.Silu, scale=sCol[:, 3:4]), waits=[fl, scg]))
            ummh = []
            for piece in range(4):
                for ic in range(8):
                    ummh.append(sched.emit("tensor", lambda e, piece=piece, ic=ic, par=par:
                                           e.matmul(
                        pB4[piece][:, :], lhsT=sWuF[par][:, ic, :],
                        rhs=sXF[:, ic, piece * 512:(piece + 1) * 512],
                        start=(ic == 0), stop=(ic == 7)),
                        waits=[twq, evg[piece]]))
            fl2 = sched.flush()
            mm_free = fl2
            evu = []
            for piece in range(4):
                evu.append(sched.emit("scalar", lambda e, piece=piece: e.activation(
                    out=sUp[:, piece * 512:(piece + 1) * 512], in_=pB4[piece][:, :],
                    func=AF.Identity, scale=sCol[:, 4:5]), waits=[fl2, scu]))
            if _DEBUG_D1 and sl == 0:
                sched.dma("sync", dbgS, sCol[:, 0:8].rearrange("p c -> p c")
                          if False else sCol[:], waits=[scg, scu], sem="stM")
                sched.dma("sync", dbgG, sSilu[:, 0:64], waits=evg + evu, sem="stM")
                sched.dma("sync", dbgU, sUp[:, 0:64], waits=evg + evu, sem="stM")
                sched.dma("sync", dbgX, sXF[:, 0, 0:64], waits=[txq], sem="stM")
            pp = sPp[par]
            tp = sched.emit("vector", lambda e, pp=pp: e.tensor_tensor(
                out=pp[:], in0=sSilu[:], in1=sUp[:], op=ALU.mult),
                waits=evu + evg + [ppfree[par]])
            ev_free = tp
            tmx = sched.emit("vector", lambda e, sl=sl, pp=pp: e.tensor_reduce(
                out=sMW[:, sl:sl + 1], in_=pp[:], axis=AX.X, op=ALU.max), waits=[tp])
            tmn = sched.emit("vector", lambda e, sl=sl, pp=pp: e.tensor_reduce(
                out=sMW[:, 22 + sl:23 + sl], in_=pp[:], axis=AX.X, op=ALU.min,
                negate=True), waits=[tp, tmx])
            mtoks.append(tmn)
            tst = sched.dma("sync", po[sl * 128:(sl + 1) * 128, :], pp[:],
                            waits=[tp], sem=f"sp{par}")
            ppfree[par] = tst

        tg = sched.emit("scalar", lambda e: e.activation(out=sM1[:], in_=sMW[:, 0:1],
                                                         func=AF.Identity), waits=mtoks)
        tf_ = sched.emit("vector", lambda e: e.tensor_reduce(
            out=sM1[:, 0:1], in_=sMW[:], axis=AX.X, op=ALU.max), waits=[tg])
        tg2 = sched.emit("scalar", lambda e: e.activation(out=sMW[:, 0:1], in_=sM1[:],
                                                          func=AF.Identity), waits=[tf_])
        tdr0 = sched.emit("tensor", lambda e: e.drain(), waits=[])
        ttp = sched.emit("tensor", lambda e: e.transpose(pR[0:1, 0:128], sM1[:, 0:1], sIdf[:]),
                         waits=[tg2, ("ld", sched.cnt["ld"]), tdr0])
        tdr = sched.emit("tensor", lambda e: e.drain(), waits=[])
        tm2 = sched.emit("vector", lambda e: e.tensor_reduce(
            out=sM1r[0:1, 0:1], in_=pR[0:1, 0:128], axis=AX.X, op=ALU.max), waits=[tdr])
        sched.dma("sync", m2o, sM1r[0:1, 0:1], waits=[tm2], sem="stM")
        sched.barrier("gpsimd", [("stM", sched.cnt["stM"]), ("sp0", sched.cnt["sp0"]),
                                 ("sp1", sched.cnt["sp1"])])
        block = stack.enter_context(nc.Block())
        sched.build(block)
    return nc


def build_D2():
    nc = bass.Bass("TRN2", target_bir_lowering=False)
    NT = 16
    po = nc.dram_tensor("po", [2816, 2048], BF16, kind="ExternalInput").ap()
    m2row = nc.dram_tensor("m2row", [1, 8], F32, kind="ExternalInput").ap()
    clipd = nc.dram_tensor("clipd", [1, 1], F32, kind="ExternalInput").ap()
    wd = nc.dram_tensor("wd", [1024, 2816], F32, kind="ExternalInput").ap()
    x3 = nc.dram_tensor("x3", [2048, 1024], F32, kind="ExternalInput").ap()
    onesb = nc.dram_tensor("onesb", [1, 128], BF16, kind="ExternalInput").ap()
    idf = nc.dram_tensor("idf", [128, 128], F32, kind="ExternalInput").ap()
    x4o = nc.dram_tensor("x4o", [2048, 1024], F32, kind="ExternalOutput").ap()
    x4r = x4o.rearrange("(t p) d -> p t d", p=128)

    with ExitStack() as stack:
        sched = Sched(nc, stack)
        sched.serial = _SERIAL
        sb = lambda n, s, d=F32: stack.enter_context(nc.sbuf_tensor(n, s, d))
        ps = lambda n, s, d=F32: stack.enter_context(nc.psum_tensor(n, s, d))
        sWdF = sb("sWdF", [128, 22, 1024], BF16)
        sHq = sb("sHq", [128, 22, 1024], BF16)
        sWs = [sb("sWsA", [128, 2816]), sb("sWsB", [128, 2816])]
        sWqT = sb("sWqT", [128, 2816], BF16)
        sPs = [sb("sPsA", [128, 1024], BF16), sb("sPsB", [128, 1024], BF16)]
        sQt = sb("sQt", [128, 2816])
        sQi = sb("sQi", [128, 2816], mybir.dt.int32)
        sCol = sb("sCol", [128, 6])
        sSd = sb("sSd", [128, 8])
        sRow = sb("sRow", [1, 1024])
        sRowP = sb("sRowP", [128, 1024], BF16)
        sRsb = sb("sRsb", [128, 1024])
        sXr = [sb("sXrA", [128, 1024]), sb("sXrB", [128, 1024])]
        sOut = [sb("sOutA", [128, 1024]), sb("sOutB", [128, 1024])]
        sM1 = sb("sM1", [128, 1]); sM1r = sb("sM1r", [1, 128])
        sIdf = sb("sIdf", [128, 128])
        sE0 = sb("sE0", [128, 128], BF16)
        sBn6 = sb("sBn6", [128, 12]); sBn2 = sb("sBn2", [128, 2]); sScr = sb("sScr", [128, 6])
        pA = ps("pA", [128, 512]); pB = ps("pB", [128, 512])
        pR = ps("pR", [128, 512]); pF = ps("pF", [128, 512])

        sched.dma("sync", sIdf[:], idf, sem="ld")
        z0 = sched.emit("gpsimd", lambda e: e.memset(sE0[:], 0.0), sem="zz")
        te = sched.dma("sync", sE0[0:1, :], onesb, sem="ld", waits=[z0])
        zM = sched.emit("gpsimd", lambda e: e.memset(sCol[:, 0:1], MAGIC), sem="zz")
        zR = sched.emit("gpsimd", lambda e: e.memset(sRowP[:], 0.0), sem="zz")
        sched.dma("sync", sM1r[0:1, 0:8], m2row, sem="ld")
        sched.dma("sync", sM1[0:1, 0:1], clipd, sem="ld")
        sched.set_flush(sE0[:], sE0[:], pF)
        rs_col, s_col, tcols = _emit_scalar_prep(nc, sched, sb, ps, sM1r[0:1, 0:8],
                                                 sM1[0:1, 0:1], sE0, pR, 128)

        wfree = [None, None]
        wq_toks = []
        trp_toks = []
        for j in range(8):
            par = j % 2
            tl = sched.dma("sync", sWs[par][:], wd[j * 128:(j + 1) * 128, :],
                           sem=f"w{par}", waits=[wfree[par]])
            am = sched.emit("vector", lambda e, par=par: e.tensor_reduce(
                out=sCol[:, 1:2], in_=sWs[par][:], axis=AX.X, op=ALU.max,
                apply_absolute_value=True), waits=[tl, wq_toks[-1] if wq_toks else None])
            rc = sched.emit("vector", lambda e: e.reciprocal(sCol[:, 2:3], sCol[:, 1:2]),
                            waits=[am])
            rc2 = sched.emit("vector", lambda e: e.tensor_scalar(
                out=sCol[:, 2:3], in0=sCol[:, 2:3], scalar1=127.0, scalar2=None,
                op0=ALU.mult), waits=[rc])
            sw = sched.emit("vector", lambda e, j=j: e.tensor_scalar(
                out=sSd[:, j:j + 1], in0=sCol[:, 1:2], scalar1=1.0 / 127.0, scalar2=None,
                op0=ALU.mult), waits=[rc2])
            q1 = sched.emit("scalar", lambda e, par=par: e.activation(
                out=sQt[:], in_=sWs[par][:], func=AF.Identity, scale=sCol[:, 2:3]),
                waits=[rc2, wq_toks[-1] if wq_toks else None])
            q2 = sched.emit("vector", lambda e: e.tensor_copy(sQi[:], sQt[:]), waits=[q1])
            q3 = sched.emit("vector", lambda e: e.tensor_scalar(
                out=sWqT[:], in0=sQi[:], scalar1=-128.0, scalar2=127.0,
                op0=ALU.max, op1=ALU.min), waits=[q2, trp_toks[-1] if trp_toks else None])
            wq_toks.append(q3)
            ttr = sched.dma("sync", sWdF[:, :, j * 128:(j + 1) * 128], sWqT[:],
                            waits=[q3], transpose=True, sem="tw")
            trp_toks.append(ttr)
            wfree[par] = ttr
        tw_all = ("tw", sched.cnt["tw"])

        rowp = []
        sched.emit("tensor", lambda e: e.drain(), waits=[])
        for j in range(8):
            ttp = sched.emit("tensor", lambda e, j=j: e.transpose(
                pR[0:1, 0:128], sSd[:, j:j + 1], sIdf[:]),
                waits=[("ld", sched.cnt["ld"]),
                       wq_toks[-1] if j == 0 else None, rowp[-1] if rowp else None])
            td = sched.emit("tensor", lambda e: e.drain(), waits=[])
            cp = sched.emit("vector", lambda e, j=j: e.tensor_copy(
                sRow[0:1, j * 128:(j + 1) * 128], pR[0:1, 0:128]), waits=[td])
            rowp.append(cp)
        trow = sched.emit("vector", lambda e: e.tensor_scalar(
            out=sRow[:], in0=sRow[:], scalar1=s_col[0:1, 0:1], scalar2=None,
            op0=ALU.mult), waits=rowp + [tcols])
        trp = sched.emit("vector", lambda e: e.tensor_copy(sRowP[0:1, :], sRow[:]),
                         waits=[trow, zR])
        rmm = sched.emit("tensor", lambda e: e.matmul(pR[:, 0:512], lhsT=sE0[:],
                                                      rhs=sRowP[:, 0:512], start=True,
                                                      stop=True), waits=[trp, te])
        rmm2 = sched.emit("tensor", lambda e: e.matmul(pA[:, 0:512], lhsT=sE0[:],
                                                       rhs=sRowP[:, 512:1024], start=True,
                                                       stop=True), waits=[trp, te])
        flr = sched.flush()
        tre = sched.emit("vector", lambda e: e.tensor_copy(sRsb[:, 0:512], pR[:, 0:512]),
                         waits=[flr])
        tre2 = sched.emit("vector", lambda e: e.tensor_copy(sRsb[:, 512:1024], pA[:, 0:512]),
                          waits=[flr, tre])

        out_free = [None, None]
        xr_free = [None, None]
        prev_bn = (None, None)
        hq_free = None
        bank_free = {0: None, 1: None}
        for h2 in range(2):
            pfree = [None, None]
            hq_toks = []
            for c in range(22):
                par = c % 2
                tl = sched.dma("sync", sPs[par][:],
                               po[c * 128:(c + 1) * 128, h2 * 1024:(h2 + 1) * 1024],
                               sem=f"p{par}", waits=[pfree[par]])
                a1 = sched.emit("scalar", lambda e, par=par: e.activation(
                    out=sQt[:, 0:1024], in_=sPs[par][:], func=AF.Identity, scale=rs_col),
                    waits=[tl, tcols, hq_toks[-1] if hq_toks else hq_free])
                a2 = sched.emit("vector", lambda e: e.tensor_copy(
                    sQi[:, 0:1024], sQt[:, 0:1024]), waits=[a1])
                a3 = sched.emit("vector", lambda e, c=c: e.tensor_scalar(
                    out=sHq[:, c, :], in0=sQi[:, 0:1024], scalar1=-128.0, scalar2=127.0,
                    op0=ALU.max, op1=ALU.min), waits=[a2])
                pfree[par] = a3
                hq_toks.append(a3)
            hq_all = hq_toks[-1]
            for tt_ in range(8):
                t = h2 * 8 + tt_
                buf = sOut[t % 2]
                for half, bank in ((0, pA), (1, pB)):
                    dd = [hq_all, tw_all, tre2] + ([bank_free[half]] if bank_free[half]
                                                   else [])
                    for c in range(22):
                        sched.emit("tensor", lambda e, c=c, tt_=tt_, half=half, bank=bank:
                                   e.matmul(bank[:, :],
                                            lhsT=sHq[:, c, tt_ * 128:(tt_ + 1) * 128],
                                            rhs=sWdF[:, c, half * 512:(half + 1) * 512],
                                            start=(c == 0), stop=(c == 21)), waits=dd)
                tfl = sched.flush()
                xrb = sXr[t % 2]
                txr = sched.dma("sync", xrb[:],
                                x3.rearrange("(t p) d -> p t d", p=128)[:, t, :],
                                sem=f"xr{t % 2}", waits=[xr_free[t % 2]])
                tr_toks = []
                for half, bank in ((0, pA), (1, pB)):
                    tr = sched.emit("vector", lambda e, half=half, bank=bank, buf=buf:
                                    e.tensor_tensor(out=buf[:, half * 512:(half + 1) * 512],
                                                    in0=bank[:, :],
                                                    in1=sRsb[:, half * 512:(half + 1) * 512],
                                                    op=ALU.mult),
                                    waits=[tfl] + ([out_free[t % 2]] if out_free[t % 2]
                                                   else []))
                    bank_free[half] = tr
                    tr2 = sched.emit("vector", lambda e, half=half, buf=buf, xrb=xrb:
                                     e.tensor_tensor(out=buf[:, half * 512:(half + 1) * 512],
                                                     in0=buf[:, half * 512:(half + 1) * 512],
                                                     in1=xrb[:, half * 512:(half + 1) * 512],
                                                     op=ALU.add), waits=[tr, txr])
                    tr_toks.append(tr2)
                xr_free[t % 2] = tr_toks[-1]
                tln, tbn = emit_layernorm(sched, buf[:], buf[:], sBn6, sBn2, sScr, 1e-5,
                                          deps=tr_toks, prev=prev_bn)
                prev_bn = tbn
                tst = sched.dma("sync", x4r[:, t, :], buf[:], waits=[tln],
                                sem="stA" if t % 2 == 0 else "stB")
                out_free[t % 2] = tst
            hq_free = ("tensor", sched.cnt["tensor"])

        sched.barrier("gpsimd", [("stA", sched.cnt["stA"]), ("stB", sched.cnt["stB"])])
        block = stack.enter_context(nc.Block())
        sched.build(block)
    return nc



_NCS = {}


def _get(name, builder):
    if name not in _NCS:
        _NCS[name] = builder()
    return _NCS[name]


def kernel(**inputs):
    inp = {k: np.asarray(v) for k, v in inputs.items()}
    x = np.ascontiguousarray(inp["L0_attn_x"], np.float32)
    cores = list(range(8))
    idf = np.eye(128, dtype=np.float32)
    onesb = np.ones((1, 128), BFDT)
    onesr = np.ones((1, 128), np.float32)

    nc = _get("A1", build_A1)
    im = []
    for c in cores:
        b, hg = c // 2, c % 2
        cols = slice(hg * 512, (hg + 1) * 512)
        im.append(host_A1_inputs(x[b], inp["L0_attn_Wq"][:, cols], inp["L0_attn_Wk"][:, cols],
                                 inp["L0_attn_Wv"][:, cols], inp["L0_attn_Wg"][:, cols],
                                 inp["L0_attn_g_attn"]))
    rA1 = run_bass_kernel_spmd(nc, im, core_ids=cores).results

    nc = _get("A2", build_A2)
    im = []
    for c in cores:
        b, hf = c // 2, c % 2
        oT_shard = np.concatenate([rA1[2 * b]["oTo"], rA1[2 * b + 1]["oTo"]], axis=0)
        oT_shard = oT_shard[:, hf * 2048:(hf + 1) * 2048]
        im.append(host_A2_inputs(x[b, hf * 2048:(hf + 1) * 2048], oT_shard,
                                 inp["L0_attn_Wo"], inp["L0_attn_w1"], inp["L0_attn_w2"],
                                 inp["L0_attn_w3"], inp["L0_attn_g_ffn"]))
    rA2 = run_bass_kernel_spmd(nc, im, core_ids=cores).results

    nc = _get("B", build_B)
    im = [host_B_inputs(rA2[c]["x1o"], inp["L0_ffn_G1_R"], inp["L0_ffn_G2_R"],
                        inp["L0_ffn_G1_I"], inp["L0_ffn_G2_I"],
                        inp["L0_ffn_bias_real"], inp["L0_ffn_bias_imag"]) for c in cores]
    rB = run_bass_kernel_spmd(nc, im, core_ids=cores).results

    nc = _get("C", build_C)
    im = []
    for c in cores:
        b, hf = c // 2, c % 2
        own = rB[c]["x2"]
        other = rB[2 * b + (1 - hf)]["x2"]
        rot = np.concatenate([own, other], axis=0)
        im.append(host_C_inputs(rot, inp["L1_attn_Wq"], inp["L1_attn_bq"],
                                inp["L1_attn_Wk"], inp["L1_attn_bk"],
                                inp["L1_attn_Wv"], inp["L1_attn_bv"],
                                inp["L1_attn_Wo"], inp["L1_attn_bo"]))
    rC = run_bass_kernel_spmd(nc, im, core_ids=cores).results

    nc = _get("Cm", build_Cm)
    im = [{"x3": rC[c]["x3o"], "idf": idf} for c in cores]
    rCm = run_bass_kernel_spmd(nc, im, core_ids=cores).results
    m1row = np.concatenate([rCm[c]["m1o"] for c in cores], axis=1)

    nc = _get("D1", build_D1)
    im = [{"x3": rC[c]["x3o"], "m1row": m1row,
           "clipx": np.asarray(inp["L1_ffn_clip_x"], np.float32).reshape(1, 1),
           "wg": np.ascontiguousarray(inp["L1_ffn_w_gate"], np.float32),
           "wu": np.ascontiguousarray(inp["L1_ffn_w_up"], np.float32),
           "onesb": onesb, "idf": idf} for c in cores]
    rD1 = run_bass_kernel_spmd(nc, im, core_ids=cores).results
    m2row = np.concatenate([rD1[c]["m2o"] for c in cores], axis=1)

    nc = _get("D2", build_D2)
    im = [{"po": rD1[c]["po"], "m2row": m2row,
           "clipd": np.asarray(inp["L1_ffn_clip_down"], np.float32).reshape(1, 1),
           "wd": np.ascontiguousarray(inp["L1_ffn_w_down"], np.float32),
           "x3": rC[c]["x3o"], "onesb": onesb, "idf": idf} for c in cores]
    rD2 = run_bass_kernel_spmd(nc, im, core_ids=cores).results

    out = np.zeros((4, 4096, 1024), np.float32)
    for c in cores:
        b, hf = c // 2, c % 2
        out[b, hf * 2048:(hf + 1) * 2048, :] = rD2[c]["x4o"]
    return out


# version 07
# speedup vs baseline: 1.0769x; 1.0569x over previous
import numpy as np
import ml_dtypes
from contextlib import ExitStack
import concourse.bass as bass
import concourse.mybir as mybir
from concourse.bass_utils import run_bass_kernel_spmd

F32 = mybir.dt.float32
BF16 = mybir.dt.bfloat16
AF = mybir.ActivationFunctionType
ALU = mybir.AluOpType
AX = mybir.AxisListType
BFDT = ml_dtypes.bfloat16

MAGIC = 12582912.0

_DEBUG_B = False
_DEBUG_A2 = False
_DEBUG_C = False
_A1_STRIPES = 4
_A1_CHUNKS = 16
_A1_LEVEL = 9
_A1_L6SUB = 9
_DEBUG_D1 = False
_A1_SERIAL = False
_SERIAL = False
_SERIAL_LN = False
_FLUSH_N = 1

ENGINES = ("sync", "gpsimd", "tensor", "vector", "scalar")


class Sched:

    HAZ = 4

    def __init__(self, nc, stack):
        self.nc = nc
        self.stack = stack
        self.ops = {e: [] for e in ENGINES}
        self.sems = {}
        self.cnt = {}
        self.serial = False
        self.last = None
        for e in ENGINES:
            self._mksem(e)

    def _mksem(self, name):
        if name not in self.sems:
            self.sems[name] = self.stack.enter_context(self.nc.semaphore("S" + name))
            self.cnt[name] = 0

    def emit(self, engine, fn, waits=(), sem=None, amt=None):
        if sem is None:
            sem = engine
        self._mksem(sem)
        if amt is None:
            amt = 16 if engine in ("sync", "gpsimd") else 1
        self.cnt[sem] += amt
        tok = (sem, self.cnt[sem])
        w = [x for x in waits if x is not None]
        if self.serial and self.last is not None:
            w = w + [self.last]
        self.last = tok
        self.ops[engine].append((fn, w, sem, amt, tok))
        return tok

    def dma(self, engine, out, in_, waits=(), transpose=False, sem=None):
        def fn(eng, out=out, in_=in_, transpose=transpose):
            if transpose:
                return eng.dma_start(out=out, in_=in_, transpose=True)
            return eng.dma_start(out=out, in_=in_)
        return self.emit(engine, fn, waits, sem=sem, amt=16)

    def set_flush(self, lhsT_row, rhs_row, scratch_bank):
        self._flush = (lhsT_row, rhs_row, scratch_bank)

    def flush(self, waits=()):
        lhsT, rhs, bank = self._flush
        return self.emit("tensor", lambda e: e.matmul(
            bank[:, 0:128], lhsT=lhsT, rhs=rhs, start=True, stop=True),
            waits=list(waits))

    def barrier(self, engine, toks):
        def fn(eng):
            return None
        return self.emit(engine, fn, waits=list(toks))

    def build(self, block):
        for engine in ENGINES:
            if self.ops[engine]:
                self._one(block, engine)

    def _one(self, block, engine):
        ops = self.ops[engine]
        sems = self.sems

        def body(eng):
            watermark = {}
            recent = []
            for fn, waits, sem, amt, tok in ops:
                for (wsem, wval) in waits:
                    if wsem == engine and (wsem, wval) not in recent:
                        continue
                    if watermark.get(wsem, -1) >= wval:
                        continue
                    eng.wait_ge(sems[wsem], wval)
                    watermark[wsem] = wval
                inst = fn(eng)
                if inst is not None:
                    inst.then_inc(sems[sem], amt)
                recent.append(tok)
                if len(recent) > Sched.HAZ:
                    recent.pop(0)

        getattr(block, engine)(body)



def emit_layernorm(sched, r_ap, out_ap, bn6, bn2, scr, eps, deps, prev=(None, None)):
    D = r_ap.shape[-1]
    halves = [(0, min(512, D))] + ([(512, D)] if D > 512 else [])
    toks = []
    for i, (a, b) in enumerate(halves):
        toks.append(sched.emit(
            "vector", lambda e, i=i, a=a, b=b: e.bn_stats(bn6[:, i * 6:(i + 1) * 6], r_ap[:, a:b]),
            waits=list(deps)))
    t2 = sched.emit("vector", lambda e, n=len(halves): e.bn_aggr(bn2[:], bn6[:, 0:6 * n]),
                    waits=toks)
    t2b = sched.emit("vector", lambda e: e.tensor_scalar(out=scr[:, 3:4], in0=bn2[:, 1:2],
                                                         scalar1=float(eps), scalar2=None,
                                                         op0=ALU.add), waits=[t2, prev[0]])
    t3 = sched.emit("scalar", lambda e: e.activation(out=scr[:, 0:1], in_=scr[:, 3:4],
                                                     func=AF.Ln), waits=[t2b])
    t4 = sched.emit("scalar", lambda e: e.activation(out=scr[:, 1:2], in_=scr[:, 0:1],
                                                     func=AF.Exp, scale=-0.5), waits=[t3])
    t5 = sched.emit("vector", lambda e: e.tensor_scalar(out=scr[:, 2:3], in0=bn2[:, 0:1],
                                                        scalar1=scr[:, 1:2], scalar2=-1.0,
                                                        op0=ALU.mult, op1=ALU.mult),
                    waits=[t4, prev[1]])
    t6 = sched.emit("scalar", lambda e: e.activation(out=out_ap, in_=r_ap,
                                                     func=AF.Identity, bias=scr[:, 2:3],
                                                     scale=scr[:, 1:2]), waits=[t5])
    return t6, (t3, t6)



def build_B():
    nc = bass.Bass("TRN2", target_bir_lowering=False)
    NT = 16
    x1 = nc.dram_tensor("x1", [2048, 1024], F32, kind="ExternalInput").ap()
    G1r = nc.dram_tensor("G1r", [8, 16, 32], F32, kind="ExternalInput").ap()
    G1i = nc.dram_tensor("G1i", [8, 16, 32], F32, kind="ExternalInput").ap()
    G2r = nc.dram_tensor("G2r", [16, 4, 16], F32, kind="ExternalInput").ap()
    G2i = nc.dram_tensor("G2i", [16, 4, 16], F32, kind="ExternalInput").ap()
    biasri = nc.dram_tensor("biasri", [1, 1024], F32, kind="ExternalInput").ap()
    seld = nc.dram_tensor("seld", [8, 128], F32, kind="ExternalInput").ap()
    selg = nc.dram_tensor("selg", [16, 128], F32, kind="ExternalInput").ap()
    onesr = nc.dram_tensor("onesr", [1, 128], F32, kind="ExternalInput").ap()
    x2 = nc.dram_tensor("x2", [2048, 1024], F32, kind="ExternalOutput").ap()
    x2r = x2.rearrange("(t p) d -> p t d", p=128)
    dbgW = nc.dram_tensor("dbgW", [128, 8, 1024], BF16, kind="ExternalOutput").ap() if _DEBUG_B else None

    with ExitStack() as stack:
        sched = Sched(nc, stack)
        sched.serial = _SERIAL
        sb = lambda n, s, d=F32: stack.enter_context(nc.sbuf_tensor(n, s, d))
        ps = lambda n, s, d=F32: stack.enter_context(nc.psum_tensor(n, s, d))

        sX = sb("sX", [128, NT, 1024])
        sXb = sb("sXb", [128, NT, 1024], BF16)
        sFM = sb("sFM", [128, 8, 2048], BF16)
        sG1 = sb("sG1", [8, 16, 32]); sG1i = sb("sG1i", [8, 16, 32])
        sG2 = sb("sG2", [16, 4, 16]); sG2i = sb("sG2i", [16, 4, 16])
        sG1b = sb("sG1b", [8, 16, 32], BF16); sG1ib = sb("sG1ib", [8, 16, 32], BF16)
        sG2b = sb("sG2b", [16, 4, 16], BF16); sG2ib = sb("sG2ib", [16, 4, 16], BF16)
        sSelD = sb("sSelD", [8, 128]); sSelDb = sb("sSelDb", [128, 128], BF16)
        sSelG = sb("sSelG", [16, 128]); sSelGb = sb("sSelGb", [128, 128], BF16)
        sOnes = sb("sOnes", [1, 128]); sOnesB = sb("sOnesB", [1, 128], BF16)
        sBias = sb("sBias", [1, 1024])
        sBiasPad = sb("sBiasPad", [128, 1024], BF16)
        sE0 = sb("sE0", [128, 128], BF16)
        sA8e = sb("sA8e", [128, 512], BF16)
        sB16e = sb("sB16e", [128, 512], BF16)
        sP1 = sb("sP1", [128, 512])
        sWr = [sb(f"sWr{q}", [128, 512]) for q in range(4)]
        sWi = [sb(f"sWi{q}", [128, 512]) for q in range(4)]
        sTmp = sb("sTmp", [128, 512])
        sRhs = sb("sRhs", [128, 8, 1024], BF16)
        sBn6 = sb("sBn6", [128, 12]); sBn2 = sb("sBn2", [128, 2]); sScr = sb("sScr", [128, 4])
        sOut = [sb("sOutA", [128, 1024]), sb("sOutB", [128, 1024])]
        pA = ps("pA", [128, 512])
        pB = ps("pB", [128, 512])
        pC = ps("pC", [128, 512])
        pD = ps("pD", [128, 512])
        pF = ps("pF", [128, 512])

        sched.dma("sync", sX[:], x1.rearrange("(t p) d -> p t d", p=128), sem="ld")
        sched.dma("sync", sG1[:], G1r, sem="ld")
        sched.dma("sync", sG1i[:], G1i, sem="ld")
        sched.dma("sync", sG2[:], G2r, sem="ld")
        sched.dma("sync", sG2i[:], G2i, sem="ld")
        sched.dma("sync", sSelD[:], seld, sem="ld")
        sched.dma("sync", sSelG[:], selg, sem="ld")
        sched.dma("sync", sOnes[:], onesr, sem="ld")
        sched.dma("sync", sBias[:], biasri, sem="ld")
        ld_all = ("ld", sched.cnt["ld"])

        z1 = sched.emit("gpsimd", lambda e: e.memset(sSelDb[:], 0.0), sem="zz")
        sched.emit("gpsimd", lambda e: e.memset(sSelGb[:], 0.0), sem="zz")
        sched.emit("gpsimd", lambda e: e.memset(sA8e[:], 0.0), sem="zz")
        sched.emit("gpsimd", lambda e: e.memset(sB16e[:], 0.0), sem="zz")
        sched.emit("gpsimd", lambda e: e.memset(sBiasPad[:], 0.0), sem="zz")
        sched.emit("gpsimd", lambda e: e.memset(sE0[:], 0.0), sem="zz")
        zz_all = ("zz", sched.cnt["zz"])

        casts = []
        for dst, src in [(sG1b, sG1), (sG1ib, sG1i), (sG2b, sG2),
                         (sG2ib, sG2i), (sOnesB, sOnes)]:
            casts.append(sched.emit("vector",
                                    lambda e, d=dst, s=src: e.tensor_copy(d[:], s[:]),
                                    waits=[ld_all]))
        casts.append(sched.emit("vector", lambda e: e.tensor_copy(sSelDb[0:8, :], sSelD[:]),
                                waits=[ld_all, zz_all]))
        casts.append(sched.emit("vector", lambda e: e.tensor_copy(sSelGb[0:16, :], sSelG[:]),
                                waits=[ld_all, zz_all]))
        casts.append(sched.emit("vector", lambda e: e.tensor_copy(sBiasPad[0:1, :], sBias[:]),
                                waits=[ld_all, zz_all]))
        casts.append(sched.emit("vector", lambda e: e.tensor_copy(sE0[0:1, :], sOnes[:]),
                                waits=[ld_all, zz_all]))
        casts.append(zz_all)

        for t in range(NT):
            tc = sched.emit("vector", lambda e, t=t: e.tensor_copy(sXb[:, t, :], sX[:, t, :]),
                            waits=[ld_all])
            sched.dma("sync", sFM[:, :, t * 128:(t + 1) * 128], sXb[:, t, :],
                      waits=[tc], transpose=True, sem="tr")
        tFM = ("tr", sched.cnt["tr"])

        wq_done = []
        ch = None
        for which in range(2):
            g1t = sG1b if which == 0 else sG1ib
            g2t = sG2b if which == 0 else sG2ib
            wdst = sWr if which == 0 else sWi
            for q in range(4):
                for c in range(4):
                    ch = sched.emit("vector", lambda e, c=c, q=q, g1t=g1t: e.tensor_copy(
                        sA8e[0:8, :], g1t[:, q * 4 + c, :, None]
                        .broadcast_to([8, 32, 16])), waits=casts + [ch])
                    ch = sched.emit("vector", lambda e, c=c, g2t=g2t: e.tensor_copy(
                        sB16e[0:16, :], g2t[:, c, None, :]
                        .broadcast_to([16, 32, 16])), waits=casts + [ch])
                    ch = sched.emit("tensor", lambda e: e.matmul(
                        pA[:, :], lhsT=sSelDb[:], rhs=sA8e[:], start=True, stop=True),
                        waits=casts + [ch])
                    ch = sched.emit("tensor", lambda e: e.matmul(
                        pB[:, :], lhsT=sSelGb[:], rhs=sB16e[:], start=True, stop=True),
                        waits=casts + [ch])
                    ch = sched.emit("vector", lambda e: e.tensor_copy(sP1[:], pA[:, :]),
                                    waits=[ch])
                    ch = sched.emit("vector", lambda e: e.tensor_tensor(
                        out=sTmp[:], in0=sP1[:], in1=pB[:, :], op=ALU.mult), waits=[ch])
                    if c == 0:
                        ch = sched.emit("vector", lambda e, w=wdst[q]: e.tensor_copy(
                            w[:], sTmp[:]), waits=[ch])
                    else:
                        ch = sched.emit("vector", lambda e, w=wdst[q]: e.tensor_tensor(
                            out=w[:], in0=w[:], in1=sTmp[:], op=ALU.add), waits=[ch])
                wq_done.append(ch)

        asm = []
        for q in range(4):
            asm.append(sched.emit("vector", lambda e, q=q: e.tensor_copy(
                sRhs[:, q, 0:512], sWr[q][:]), waits=wq_done))
            asm.append(sched.emit("vector", lambda e, q=q: e.tensor_copy(
                sRhs[:, q, 512:1024], sWi[q][:]), waits=wq_done))
            asm.append(sched.emit("vector", lambda e, q=q: e.tensor_scalar(
                out=sRhs[:, q + 4, 0:512], in0=sWi[q][:], scalar1=-1.0, scalar2=None,
                op0=ALU.mult), waits=wq_done))
            asm.append(sched.emit("vector", lambda e, q=q: e.tensor_copy(
                sRhs[:, q + 4, 512:1024], sWr[q][:]), waits=wq_done))

        if _DEBUG_B:
            sched.dma("sync", dbgW, sRhs[:], waits=asm, sem="stA")

        bank_free = {0: None, 1: None}
        out_free = [None, None]
        prev_bn = (None, None)
        store_toks = []
        sched.set_flush(sE0[:], sBiasPad[:, 0:128], pF)
        for t in range(NT):
            buf = sOut[t % 2]
            tr_toks = []
            for half, bank in ((0, pC), (1, pD)):
                deps = asm + [tFM]
                if bank_free[half]:
                    deps = deps + [bank_free[half]]
                if out_free[t % 2]:
                    deps = deps + [out_free[t % 2]]
                for ic in range(8):
                    sched.emit("tensor",
                               lambda e, t=t, ic=ic, half=half, bank=bank: e.matmul(
                                   bank[:, :], lhsT=sFM[:, ic, t * 128:(t + 1) * 128],
                                   rhs=sRhs[:, ic, half * 512:(half + 1) * 512],
                                   start=(ic == 0), stop=False), waits=deps)
                sched.emit("tensor", lambda e, half=half, bank=bank: e.matmul(
                    bank[:, :], lhsT=sE0[:], rhs=sBiasPad[:, half * 512:(half + 1) * 512],
                    start=False, stop=True), waits=deps)
            tfl = sched.flush()
            for _ in range(_FLUSH_N - 1):
                tfl = sched.flush()
            for half, bank in ((0, pC), (1, pD)):
                tr = sched.emit("vector", lambda e, t=t, half=half, bank=bank, buf=buf:
                                e.tensor_tensor(out=buf[:, half * 512:(half + 1) * 512],
                                                in0=sX[:, t, half * 512:(half + 1) * 512],
                                                in1=bank[:, :], op=ALU.add),
                                waits=[tfl] + ([out_free[t % 2]] if out_free[t % 2] else []))
                bank_free[half] = tr
                tr_toks.append(tr)
            if _SERIAL_LN:
                sched.serial = True
            tln, tbn = emit_layernorm(sched, buf[:], buf[:], sBn6, sBn2, sScr, 1e-5,
                                      deps=tr_toks, prev=prev_bn)
            sched.serial = _SERIAL
            prev_bn = tbn
            tst = sched.dma("sync", x2r[:, t, :], buf[:], waits=[tln],
                            sem="stA" if t % 2 == 0 else "stB")
            out_free[t % 2] = tst
            store_toks.append(tst)

        sched.barrier("gpsimd", [("stA", sched.cnt["stA"]), ("stB", sched.cnt["stB"])])
        block = stack.enter_context(nc.Block())
        sched.build(block)
    return nc


def host_B_inputs(x1_shard, G1_R, G2_R, G1_I, G2_I, bias_real, bias_imag):
    p = np.arange(128)
    seld = (p[None, :] // 16 == np.arange(8)[:, None]).astype(np.float32)
    selg = (p[None, :] % 16 == np.arange(16)[:, None]).astype(np.float32)
    def packA(G1):
        A = np.transpose(np.asarray(G1)[0], (2, 1, 0))
        A = A.reshape(4, 8, 4, 32).transpose(1, 0, 2, 3)
        return np.ascontiguousarray(A.reshape(8, 16, 32), np.float32)

    def packB(G2):
        B = np.transpose(np.asarray(G2)[:, :, 0, :], (2, 0, 1))
        return np.ascontiguousarray(B, np.float32)

    return {
        "x1": np.ascontiguousarray(x1_shard, np.float32),
        "G1r": packA(G1_R), "G1i": packA(G1_I),
        "G2r": packB(G2_R), "G2i": packB(G2_I),
        "biasri": np.concatenate([np.asarray(bias_real), np.asarray(bias_imag)])[None, :].astype(np.float32),
        "seld": seld, "selg": selg,
        "onesr": np.ones((1, 128), np.float32),
    }



def build_A2():
    nc = bass.Bass("TRN2", target_bir_lowering=False)
    NT = 16
    DFF = 2736
    NS = 22
    slabw = [128] * 21 + [48]
    x = nc.dram_tensor("x", [2048, 1024], F32, kind="ExternalInput").ap()
    oT = nc.dram_tensor("oT", [1024, 2048], BF16, kind="ExternalInput").ap()
    Wo = nc.dram_tensor("Wo", [1024, 1024], F32, kind="ExternalInput").ap()
    w1 = nc.dram_tensor("w1", [1024, DFF], F32, kind="ExternalInput").ap()
    w2 = nc.dram_tensor("w2", [1024, DFF], F32, kind="ExternalInput").ap()
    w3 = nc.dram_tensor("w3", [DFF, 1024], F32, kind="ExternalInput").ap()
    gff = nc.dram_tensor("gff", [128, 8], F32, kind="ExternalInput").ap()
    onesr = nc.dram_tensor("onesr", [1, 128], F32, kind="ExternalInput").ap()
    x1o = nc.dram_tensor("x1o", [2048, 1024], F32, kind="ExternalOutput").ap()
    x1r = x1o.rearrange("(t p) d -> p t d", p=128)
    if _DEBUG_A2:
        dbg1 = nc.dram_tensor("dbg1", [128, 16, 1024], F32, kind="ExternalOutput").ap()
        dbg2 = nc.dram_tensor("dbg2", [128, 8, 2048], BF16, kind="ExternalOutput").ap()

    with ExitStack() as stack:
        sched = Sched(nc, stack)
        sched.serial = _SERIAL
        sb = lambda n, s, d=F32: stack.enter_context(nc.sbuf_tensor(n, s, d))
        ps = lambda n, s, d=F32: stack.enter_context(nc.psum_tensor(n, s, d))

        sR = sb("sR", [128, NT, 1024])
        sU = sb("sU", [128, 16384], BF16)
        sOT = sU[:, 0:16384].rearrange("p (c t) -> p c t", c=8)
        sP = sU[:, 0:11264].rearrange("p (c t) -> p c t", c=22)
        sStage = sb("sStage", [128, 8, 128])
        sStage2 = sb("sStage2", [128, 8, 128])
        sStage3 = sStage[:].rearrange("p c o -> p (c o)")
        sStage4 = sStage2[:].rearrange("p c o -> p (c o)")
        sWoB = sb("sWoB", [128, 8, 1024], BF16)
        sW3B = sb("sW3B", [128, 22, 1024], BF16)
        sW1B = [sb("sW1Ba", [128, 8, 128], BF16), sb("sW1Bb", [128, 8, 128], BF16)]
        sW2B = [sb("sW2Ba", [128, 8, 128], BF16), sb("sW2Bb", [128, 8, 128], BF16)]
        sXF = sb("sXF", [128, 8, 2048], BF16)
        sXT = sb("sXT", [128, 1024], BF16)
        sSilu = sXT[:, 0:512]
        sUp = sXT[:, 512:1024]
        sG = sb("sG", [128, 8])
        sOnes = sb("sOnes", [1, 128])
        sE0 = sb("sE0", [128, 128], BF16)
        sBn6 = sb("sBn6", [128, 12]); sBn2 = sb("sBn2", [128, 2]); sScr = sb("sScr", [128, 6])
        _wof = sWoB[:].rearrange("p c o -> p (c o)")
        sOut = [_wof[:, 0:2048].bitcast(F32), _wof[:, 2048:4096].bitcast(F32)]
        sXr = [_wof[:, 4096:6144].bitcast(F32), _wof[:, 6144:8192].bitcast(F32)]
        pA = ps("pA", [128, 512])
        pB = ps("pB", [128, 512])
        pF = ps("pF", [128, 512])

        sched.dma("sync", sR[:], x.rearrange("(t p) d -> p t d", p=128), sem="ld")
        sched.dma("sync", sOT, oT.rearrange("(c p) t -> p c t", p=128), sem="ld")
        sched.dma("sync", sG[:], gff, sem="ld")
        sched.dma("sync", sOnes[:], onesr, sem="ld")
        ld_all = ("ld", sched.cnt["ld"])
        z1 = sched.emit("gpsimd", lambda e: e.memset(sE0[:], 0.0), sem="zz")
        z2 = sched.emit("gpsimd", lambda e: e.memset(sP[:, 21, :], 0.0), sem="zz")
        z3 = sched.emit("gpsimd", lambda e: e.memset(sW3B[:, 21, :], 0.0), sem="zz")
        zz_all = ("zz", sched.cnt["zz"])
        tE0 = sched.emit("vector", lambda e: e.tensor_copy(sE0[0:1, :], sOnes[:]),
                         waits=[ld_all, zz_all])

        cast_wo = []
        for j in range(8):
            stg = sStage if j % 2 == 0 else sStage2
            sched.dma("sync", stg[:],
                      Wo.rearrange("(c p) o -> p c o", p=128)[:, :, j * 128:(j + 1) * 128],
                      sem=f"wo{j % 2}",
                      waits=[cast_wo[j - 2]] if j >= 2 else [])
            tc = sched.emit("vector", lambda e, j=j, stg=stg: e.tensor_copy(
                sWoB[:, :, j * 128:(j + 1) * 128], stg[:]),
                waits=[(f"wo{j % 2}", sched.cnt[f"wo{j % 2}"])])
            cast_wo.append(tc)
        cast_w3 = []
        for c in range(NS):
            stg = sStage3 if c % 2 == 0 else sStage4
            rows = slabw[c]
            sched.dma("sync", stg[0:rows, :], w3[c * 128:c * 128 + rows, :],
                      sem=f"w3{c % 2}",
                      waits=[cast_w3[c - 2]] if c >= 2 else [cast_wo[-2], cast_wo[-1]])
            tc = sched.emit("vector", lambda e, c=c, stg=stg, rows=rows: e.tensor_copy(
                sW3B[0:rows, c, :], stg[0:rows, :]),
                waits=[(f"w3{c % 2}", sched.cnt[f"w3{c % 2}"]), zz_all])
            cast_w3.append(tc)
        w3_ready = [cast_w3[-1]]

        sched.set_flush(sE0[:], sE0[:], pF)

        bank_free = {0: None, 1: None}
        prev_rms = None
        prev_tt = None
        for t in range(NT):
            mm_deps = [cast_wo[-1], ld_all, tE0]
            for half, bank in ((0, pA), (1, pB)):
                d = mm_deps + ([bank_free[half]] if bank_free[half] else [])
                for ic in range(8):
                    sched.emit("tensor", lambda e, t=t, ic=ic, half=half, bank=bank: e.matmul(
                        bank[:, :], lhsT=sOT[:, ic, t * 128:(t + 1) * 128],
                        rhs=sWoB[:, ic, half * 512:(half + 1) * 512],
                        start=(ic == 0), stop=(ic == 7)), waits=d)
            tfl = sched.flush()
            radd = []
            for half, bank in ((0, pA), (1, pB)):
                tr = sched.emit("vector", lambda e, t=t, half=half, bank=bank: e.tensor_tensor(
                    out=sR[:, t, half * 512:(half + 1) * 512],
                    in0=sR[:, t, half * 512:(half + 1) * 512],
                    in1=bank[:, :], op=ALU.add), waits=[tfl])
                bank_free[half] = tr
                radd.append(tr)
            s1 = sched.emit("vector", lambda e, t=t: e.bn_stats(sBn6[:, 0:6], sR[:, t, 0:512]),
                            waits=radd)
            s2 = sched.emit("vector", lambda e, t=t: e.bn_stats(sBn6[:, 6:12], sR[:, t, 512:1024]),
                            waits=radd)
            s3 = sched.emit("vector", lambda e: e.bn_aggr(sBn2[:], sBn6[:]), waits=[s1, s2])
            s4 = sched.emit("vector", lambda e: e.tensor_scalar(
                out=sScr[:, 0:1], in0=sBn2[:, 0:1], scalar1=sBn2[:, 0:1], scalar2=sBn2[:, 1:2],
                op0=ALU.mult, op1=ALU.add), waits=[s3, prev_rms])
            s5 = sched.emit("vector", lambda e: e.tensor_scalar(
                out=sScr[:, 1:2], in0=sScr[:, 0:1], scalar1=1e-6, scalar2=None,
                op0=ALU.add), waits=[s4])
            t3 = sched.emit("scalar", lambda e: e.activation(out=sScr[:, 2:3], in_=sScr[:, 1:2],
                                                             func=AF.Ln), waits=[s5])
            t4 = sched.emit("scalar", lambda e: e.activation(out=sScr[:, 3:4], in_=sScr[:, 2:3],
                                                             func=AF.Exp, scale=-0.5), waits=[t3])
            prev_rms = t4
            t5 = sched.emit("scalar", lambda e, t=t: e.activation(
                out=sXT[:], in_=sR[:, t, :], func=AF.Identity, scale=sScr[:, 3:4]),
                waits=[t4, prev_tt])
            prev_tt = sched.dma("sync", sXF[:, :, t * 128:(t + 1) * 128], sXT[:],
                                waits=[t5], transpose=True, sem="trx")
        txf = ("trx", sched.cnt["trx"])
        gf = []
        for c in range(8):
            gf.append(sched.emit("vector", lambda e, c=c: e.tensor_scalar(
                out=sXF[:, c, :], in0=sXF[:, c, :], scalar1=sG[:, c:c + 1], scalar2=None,
                op0=ALU.mult), waits=[txf, ld_all]))
        gf = [gf[-1]]

        if _DEBUG_A2:
            sched.dma("sync", dbg1, sR[:], waits=gf, sem="stA")
            sched.dma("sync", dbg2, sXF[:], waits=gf, sem="stA")

        ph1_pe = ("tensor", sched.cnt["tensor"])
        xr_free = [ph1_pe, ph1_pe]
        out_free = [None, None]
        prev_bn = (None, None)
        pfree = None
        wbuf_free = [cast_wo[-1], cast_wo[-1]]
        e1_prev = e2_prev = None
        for q4 in range(4):
            tok0 = q4 * 512
            prod_toks = []
            for sl in range(NS):
                rows = slabw[sl]
                par = sl % 2
                d0 = [wbuf_free[par]] if wbuf_free[par] else []
                sched.dma("sync", sStage[:, :, 0:rows],
                          w1[:, sl * 128:sl * 128 + rows].rearrange("(c p) o -> p c o", p=128),
                          sem=f"sw{par}", waits=d0)
                sched.dma("sync", sStage2[:, :, 0:rows],
                          w2[:, sl * 128:sl * 128 + rows].rearrange("(c p) o -> p c o", p=128),
                          sem=f"sw{par}", waits=d0)
                tld = (f"sw{par}", sched.cnt[f"sw{par}"])
                c1 = sched.emit("vector", lambda e, par=par, rows=rows: e.tensor_copy(
                    sW1B[par][:, :, 0:rows], sStage[:, :, 0:rows]), waits=[tld])
                c2 = sched.emit("vector", lambda e, par=par, rows=rows: e.tensor_copy(
                    sW2B[par][:, :, 0:rows], sStage2[:, :, 0:rows]), waits=[tld])
                wbuf_free[par] = c2
                mmdeps = gf + [c1, c2]
                for ic in range(8):
                    sched.emit("tensor", lambda e, ic=ic, tok0=tok0, par=par:
                               e.matmul(pA[:, :], lhsT=sW1B[par][:, ic, :],
                                        rhs=sXF[:, ic, tok0:tok0 + 512],
                                        start=(ic == 0), stop=(ic == 7)),
                               waits=mmdeps + ([e1_prev] if e1_prev else []))
                for ic in range(8):
                    sched.emit("tensor", lambda e, ic=ic, tok0=tok0, par=par:
                               e.matmul(pB[:, :], lhsT=sW2B[par][:, ic, :],
                                        rhs=sXF[:, ic, tok0:tok0 + 512],
                                        start=(ic == 0), stop=(ic == 7)),
                               waits=mmdeps + ([e2_prev] if e2_prev else []))
                tfl = sched.flush()
                e1 = sched.emit("scalar", lambda e, rows=rows: e.activation(
                    out=sSilu[0:rows, :], in_=pA[0:rows, :], func=AF.Silu),
                    waits=[tfl, pfree])
                e2 = sched.emit("scalar", lambda e, rows=rows: e.activation(
                    out=sUp[0:rows, :], in_=pB[0:rows, :], func=AF.Identity),
                    waits=[tfl, pfree])
                e1_prev, e2_prev = e1, e2
                tp = sched.emit("vector", lambda e, sl=sl, rows=rows: e.tensor_tensor(
                    out=sP[0:rows, sl, :], in0=sSilu[0:rows, :], in1=sUp[0:rows, :],
                    op=ALU.mult), waits=[e1, e2] + ([pfree] if pfree else []))
                prod_toks.append(tp)
            prod_toks = [prod_toks[-1]]
            for tt_ in range(4):
                t = q4 * 4 + tt_
                buf = sOut[t % 2]
                for half, bank in ((0, pA), (1, pB)):
                    dd = prod_toks + w3_ready + ([bank_free[half]] if bank_free[half] else [])
                    for c in range(NS):
                        sched.emit("tensor", lambda e, c=c, tt_=tt_, half=half, bank=bank:
                                   e.matmul(bank[:, :],
                                            lhsT=sP[:, c, tt_ * 128:(tt_ + 1) * 128],
                                            rhs=sW3B[:, c, half * 512:(half + 1) * 512],
                                            start=(c == 0), stop=(c == NS - 1)), waits=dd)
                tfl = sched.flush()
                xrb = sXr[t % 2]
                txr = sched.dma("sync", xrb, x.rearrange("(t p) d -> p t d", p=128)[:, t, :],
                                sem=f"xr{t % 2}",
                                waits=[xr_free[t % 2]] if xr_free[t % 2] else [])
                tr_toks = []
                for half, bank in ((0, pA), (1, pB)):
                    tr = sched.emit("vector", lambda e, t=t, half=half, bank=bank, buf=buf:
                                    e.tensor_tensor(out=buf[:, half * 512:(half + 1) * 512],
                                                    in0=sR[:, t, half * 512:(half + 1) * 512],
                                                    in1=bank[:, :], op=ALU.add),
                                    waits=[tfl] + ([out_free[t % 2]] if out_free[t % 2] else []))
                    bank_free[half] = tr
                    tr2 = sched.emit("vector", lambda e, half=half, buf=buf, xrb=xrb:
                                     e.tensor_tensor(out=buf[:, half * 512:(half + 1) * 512],
                                                     in0=buf[:, half * 512:(half + 1) * 512],
                                                     in1=xrb[:, half * 512:(half + 1) * 512],
                                                     op=ALU.add), waits=[tr, txr])
                    tr_toks.append(tr2)
                xr_free[t % 2] = tr_toks[-1]
                tln, tbn = emit_layernorm(sched, buf, buf, sBn6, sBn2, sScr, 1e-5,
                                          deps=tr_toks, prev=prev_bn)
                prev_bn = tbn
                tst = sched.dma("sync", x1r[:, t, :], buf, waits=[tln],
                                sem="stA" if t % 2 == 0 else "stB")
                out_free[t % 2] = tst
            pfree = bank_free[1]
            e1_prev = bank_free[0]
            e2_prev = bank_free[1]

        sched.barrier("gpsimd", [("stA", sched.cnt["stA"]), ("stB", sched.cnt["stB"])])
        block = stack.enter_context(nc.Block())
        sched.build(block)
    return nc


def host_A2_inputs(x_shard, oT_shard, Wo, w1, w2, w3, g_ffn):
    return {
        "x": np.ascontiguousarray(x_shard, np.float32),
        "oT": np.ascontiguousarray(oT_shard).astype(BFDT),
        "Wo": np.ascontiguousarray(Wo, np.float32),
        "w1": np.ascontiguousarray(w1, np.float32),
        "w2": np.ascontiguousarray(w2, np.float32),
        "w3": np.ascontiguousarray(w3, np.float32),
        "gff": np.ascontiguousarray(np.asarray(g_ffn, np.float32).reshape(8, 128).T),
        "onesr": np.ones((1, 128), np.float32),
    }



def build_C():
    nc = bass.Bass("TRN2", target_bir_lowering=False)
    NTK = 32
    NTQ = 16
    x2kv = nc.dram_tensor("x2kv", [4096, 1024], F32, kind="ExternalInput").ap()
    WqT = nc.dram_tensor("WqT", [1024, 1024], F32, kind="ExternalInput").ap()
    WkT = nc.dram_tensor("WkT", [1024, 1024], F32, kind="ExternalInput").ap()
    WvT = nc.dram_tensor("WvT", [1024, 1024], F32, kind="ExternalInput").ap()
    WoT = nc.dram_tensor("WoT", [1024, 1024], F32, kind="ExternalInput").ap()
    bqc = nc.dram_tensor("bqc", [128, 8], F32, kind="ExternalInput").ap()
    bkc = nc.dram_tensor("bkc", [128, 8], F32, kind="ExternalInput").ap()
    bvr = nc.dram_tensor("bvr", [1, 1024], F32, kind="ExternalInput").ap()
    bor = nc.dram_tensor("bor", [1, 1024], F32, kind="ExternalInput").ap()
    onesb = nc.dram_tensor("onesb", [1, 128], BF16, kind="ExternalInput").ap()
    x3o = nc.dram_tensor("x3o", [2048, 1024], F32, kind="ExternalOutput").ap()
    x3r = x3o.rearrange("(t p) d -> p t d", p=128)

    with ExitStack() as stack:
        sched = Sched(nc, stack)
        sched.serial = _SERIAL
        sb = lambda n, s, d=F32: stack.enter_context(nc.sbuf_tensor(n, s, d))
        ps = lambda n, s, d=F32: stack.enter_context(nc.psum_tensor(n, s, d))

        sXF = sb("sXF", [128, 8, 4096], BF16)
        sK = sb("sK", [128, 4, 4096], BF16)
        sQ = sb("sQ", [128, 4, 2048], BF16)
        sV = sb("sV", [128, 32, 260], BF16)
        sOTb = sb("sOTb", [128, 8, 2048], BF16)
        sWoB = sb("sWoB", [128, 8, 1024], BF16)
        sWq = sb("sWq", [128, 8, 256], BF16)
        sWk = sb("sWk", [128, 8, 256], BF16)
        sWv = sb("sWv", [128, 8, 256], BF16)
        sStage = sb("sStage", [128, 8, 256])
        sXs = [sb("sXsA", [128, 1024]), sb("sXsB", [128, 1024])]
        _otf = sOTb[:].rearrange("p c t -> p (c t)")
        sXC = [_otf[:, 0:1024], _otf[:, 1024:2048]]
        sPT = [sb("sPTa", [128, 512], BF16), sb("sPTb", [128, 512], BF16)]
        sRec = sb("sRec", [128, 512], BF16)
        _stf = sStage[:].rearrange("p c o -> p (c o)")
        sRv = _stf[0:64, 0:512]
        sSr = _stf[0:1, 512:1024]
        sBoPad = _stf[:, 1024:1536].bitcast(BF16)
        sPT = [sPT[0][:], sPT[1][:],
               _stf[:, 1024:1280].bitcast(BF16), _stf[:, 1280:1536].bitcast(BF16),
               _stf[:, 1536:1792].bitcast(BF16), _stf[:, 1792:2048].bitcast(BF16)]
        sBq = sb("sBq", [128, 8]); sBk = sb("sBk", [128, 8])
        sBvR = sXs[0][:][0:1, :]
        sBoR = sXs[1][:][0:1, :]
        sBvPad = _stf[:, 1792:1920].bitcast(BF16)
        sE0 = sb("sE0", [128, 128], BF16)
        sBn6 = _stf[:, 1920:1932]; sBn2 = _stf[:, 1932:1934]; sScr = _stf[:, 1934:1940]
        sM1 = _stf[:, 1940:1946]
        sMW = _stf[:, 1946:1978]
        _qf = sQ[:].rearrange("p c t -> p (c t)")
        sOut = [_qf[:, 0:2048].bitcast(F32), _qf[:, 2048:4096].bitcast(F32)]
        sXr = [_qf[:, 4096:6144].bitcast(F32), _qf[:, 6144:8192].bitcast(F32)]
        pS = [ps("pS0", [128, 512]), ps("pS1", [128, 512])]
        pO = ps("pO", [128, 512])
        pO1 = ps("pO1", [128, 512])
        pR = ps("pR", [128, 512])
        pA = ps("pA", [128, 512])
        pB = ps("pB", [128, 512])
        pF = ps("pF", [128, 512])

        sched.dma("sync", sBq[:], bqc, sem="ld")
        sched.dma("sync", sBk[:], bkc, sem="ld")
        ld_all = ("ld", sched.cnt["ld"])
        sched.emit("gpsimd", lambda e: e.memset(sE0[:], 0.0), sem="zz")
        sched.emit("gpsimd", lambda e: e.memset(sK[:], 0.0), sem="zz")
        sched.emit("gpsimd", lambda e: e.memset(sQ[:], 0.0), sem="zz")
        sched.emit("gpsimd", lambda e: e.memset(sRec[:], 0.0), sem="zz")
        sched.emit("gpsimd", lambda e: e.memset(sM1, 0.0), sem="zz")
        for hh in range(4):
            sched.emit("gpsimd", lambda e, hh=hh: e.memset(
                sV[:, :, hh * 65 + 64:hh * 65 + 65], 1.0), sem="zz")
        zz_all = ("zz", sched.cnt["zz"])
        tE0 = sched.dma("sync", sE0[0:1, :], onesb, sem="ld", waits=[zz_all])
        tE0 = ("ld", sched.cnt["ld"])
        xc_free = [None, None]
        for t in range(NTK):
            par = t % 2
            tl = sched.dma("sync", sXs[par][:],
                           x2kv.rearrange("(t p) d -> p t d", p=128)[:, t, :],
                           sem=f"x{par}", waits=[xc_free[par]] if xc_free[par] else [])
            tc = sched.emit("vector", lambda e, par=par: e.tensor_copy(sXC[par], sXs[par][:]),
                            waits=[tl])
            ttr = sched.dma("sync", sXF[:, :, t * 128:(t + 1) * 128], sXC[par],
                            waits=[tc], transpose=True, sem="tr")
            xc_free[par] = ttr
        tXF = ("tr", sched.cnt["tr"])
        sched.dma("sync", sBvR, bvr, sem="ld2", waits=[tXF])
        sched.dma("sync", sBoR, bor, sem="ld2", waits=[tXF])
        rows_ld = ("ld2", sched.cnt["ld2"])

        cast_wo = []
        for j in range(4):
            sched.dma("sync", sStage[:],
                      WoT.rearrange("(c p) o -> p c o", p=128)[:, :, j * 256:(j + 1) * 256],
                      sem="wo", waits=[cast_wo[-1]] if cast_wo else [])
            cast_wo.append(sched.emit("vector", lambda e, j=j: e.tensor_copy(
                sWoB[:, :, j * 256:(j + 1) * 256], sStage[:]),
                waits=[("wo", sched.cnt["wo"])]))

        sched.set_flush(sE0[:], sE0[:], pF)

        g_gate = cast_wo[-1]
        wfree_prev = None
        att_done = None
        prev_exp_last = None
        prev_c3 = [None]
        prev_c5 = [None]
        att_sidx = [0]
        pend_tail = [None]
        tail_done = {}
        tail_read = {}

        def _emit_tail(sidx, pOb, slab, rbase, qp, avtok):
            c1 = sched.emit("vector", lambda e, pOb=pOb: e.tensor_copy(sSr, pOb[64:65, :]),
                            waits=[("tensor", sched.cnt["tensor"]), prev_c3[0], avtok])
            c2 = sched.emit("scalar", lambda e: e.activation(out=sSr, in_=sSr,
                                                             func=AF.Ln), waits=[c1])
            c3 = sched.emit("scalar", lambda e: e.activation(out=sRec[0:1, :],
                                                             in_=sSr, func=AF.Exp,
                                                             scale=-1.0), waits=[c2])
            prev_c3[0] = c3
            rmm = sched.emit("tensor", lambda e: e.matmul(
                pR[:, :], lhsT=sE0[:], rhs=sRec[:], start=True, stop=True), waits=[c3])
            fl2 = sched.flush()
            c4 = sched.emit("scalar", lambda e: e.activation(out=sRv, in_=pR[0:64, :],
                                                             func=AF.Identity),
                            waits=[fl2, prev_c5[0]])
            c5 = sched.emit("vector", lambda e, pOb=pOb, slab=slab, rbase=rbase, qp=qp:
                            e.tensor_tensor(
                                out=sOTb[rbase:rbase + 64, slab, qp * 512:(qp + 1) * 512],
                                in0=pOb[0:64, :], in1=sRv, op=ALU.mult), waits=[c4])
            prev_c5[0] = c5
            tail_done[sidx] = c5
            tail_read[sidx] = c5

        for g in range(4):
            wt_all = None
            for (wsrc, wdst) in ((WqT, sWq), (WkT, sWk), (WvT, sWv)):
                sched.dma("sync", sStage[:],
                          wsrc.rearrange("(c p) o -> p c o", p=128)[:, :, g * 256:(g + 1) * 256],
                          sem="wg", waits=[g_gate, wfree_prev, att_done])
                g_gate = sched.emit("vector", lambda e, wdst=wdst: e.tensor_copy(
                    wdst[:], sStage[:]), waits=[("wg", sched.cnt["wg"])])
            wt_all = g_gate
            gd = [wt_all, tXF, zz_all, ld_all, tE0]
            if att_done:
                gd.append(att_done)

            ev_prev = None
            pend = None
            kq_units = ([("K", j, p) for j in range(2) for p in range(8)] +
                        [("Q", j, p) for j in range(2) for p in range(4)])
            bank_last = {0: None, 1: None}
            for ui, (kind, j, piece) in enumerate(kq_units):
                bk = [pA, pO][ui % 2]
                wsel = sWk if kind == "K" else sWq
                for ic in range(8):
                    sched.emit("tensor", lambda e, j=j, piece=piece, ic=ic, bk=bk,
                               wsel=wsel: e.matmul(
                        bk[:, :], lhsT=wsel[:, ic, j * 128:(j + 1) * 128],
                        rhs=sXF[:, ic, piece * 512:(piece + 1) * 512],
                        start=(ic == 0), stop=(ic == 7)),
                        waits=gd + ([bank_last[ui % 2]] if bank_last[ui % 2] else []))
                tok_now = ("tensor", sched.cnt["tensor"])
                if pend is not None:
                    (pkind, pj, ppiece, pbk, pui) = pend
                    dst = sK if pkind == "K" else sQ
                    bias = sBk if pkind == "K" else sBq
                    e1 = sched.emit("scalar", lambda e, g=g, pj=pj, ppiece=ppiece, pbk=pbk,
                                    dst=dst, bias=bias: e.activation(
                        out=dst[0:64, 2 * pj, ppiece * 512:(ppiece + 1) * 512],
                        in_=pbk[0:64, :], func=AF.Identity,
                        bias=bias[0:64, g * 2 + pj:g * 2 + pj + 1]), waits=[tok_now])
                    e2 = sched.emit("scalar", lambda e, g=g, pj=pj, ppiece=ppiece, pbk=pbk,
                                    dst=dst, bias=bias: e.activation(
                        out=dst[0:64, 2 * pj + 1, ppiece * 512:(ppiece + 1) * 512],
                        in_=pbk[64:128, :], func=AF.Identity,
                        bias=bias[64:128, g * 2 + pj:g * 2 + pj + 1]), waits=[tok_now])
                    bank_last[pui % 2] = e2
                    ev_prev = e2
                pend = (kind, j, piece, bk, ui)
            (pkind, pj, ppiece, pbk, pui) = pend
            tfl = sched.flush()
            dst = sK if pkind == "K" else sQ
            bias = sBk if pkind == "K" else sBq
            e1 = sched.emit("scalar", lambda e, g=g, pj=pj, ppiece=ppiece, pbk=pbk, dst=dst,
                            bias=bias: e.activation(
                out=dst[0:64, 2 * pj, ppiece * 512:(ppiece + 1) * 512],
                in_=pbk[0:64, :], func=AF.Identity,
                bias=bias[0:64, g * 2 + pj:g * 2 + pj + 1]), waits=[tfl])
            ev_prev = sched.emit("scalar", lambda e, g=g, pj=pj, ppiece=ppiece, pbk=pbk,
                                 dst=dst, bias=bias: e.activation(
                out=dst[0:64, 2 * pj + 1, ppiece * 512:(ppiece + 1) * 512],
                in_=pbk[64:128, :], func=AF.Identity,
                bias=bias[64:128, g * 2 + pj:g * 2 + pj + 1]), waits=[tfl])

            zbv = sched.emit("gpsimd", lambda e: e.memset(sBvPad, 0.0),
                             waits=[wt_all], sem="zv")
            tbv = sched.emit("vector", lambda e, g=g: e.tensor_copy(
                sBvPad[0:1, :], sBvR[:, g * 256:(g + 1) * 256]),
                waits=[rows_ld, ("zv", sched.cnt["zv"]), att_done])
            vbank_last = {0: ev_prev, 1: ev_prev}
            vpend = None
            for t in range(NTK):
                vb = [pB, pO1][t % 2]
                for ic in range(8):
                    sched.emit("tensor", lambda e, t=t, ic=ic, vb=vb: e.matmul(
                        vb[:, 0:256], lhsT=sXF[:, ic, t * 128:(t + 1) * 128],
                        rhs=sWv[:, ic, :], start=(ic == 0), stop=False),
                        waits=gd + ([vbank_last[t % 2]] if vbank_last[t % 2] else []))
                sched.emit("tensor", lambda e, vb=vb: e.matmul(
                    vb[:, 0:256], lhsT=sE0[:], rhs=sBvPad, start=False, stop=True),
                    waits=[tbv])
                tok_now = ("tensor", sched.cnt["tensor"])
                if vpend is not None:
                    (pt, pvb) = vpend
                    ev_prev = sched.emit("vector", lambda e, pt=pt, pvb=pvb: e.tensor_copy(
                        sV[:, pt, 0:260].rearrange("p (h v) -> p h v", h=4)[:, :, 0:64],
                        pvb[:, 0:256].rearrange("p (h v) -> p h v", h=4)), waits=[tok_now])
                    vbank_last[pt % 2] = ev_prev
                vpend = (t, vb)
            (pt, pvb) = vpend
            tfl = sched.flush()
            ev_prev = sched.emit("vector", lambda e, pt=pt, pvb=pvb: e.tensor_copy(
                sV[:, pt, 0:260].rearrange("p (h v) -> p h v", h=4)[:, :, 0:64],
                pvb[:, 0:256].rearrange("p (h v) -> p h v", h=4)), waits=[tfl])
            qkv_done = ev_prev
            wfree_prev = qkv_done

            units = [(hh, qp) for hh in range(4) for qp in range(4)]
            ci = 0
            while ci < len(units):
                chunk = units[ci:ci + 3]
                S = len(chunk)
                sidx0 = att_sidx[0]
                att_sidx[0] += S
                SB = [pS[0], pS[1], pA][:S]
                AVB = [pB, pO, pO1][:S]
                PT = [[sPT[0], sPT[1]], [sPT[2], sPT[3]], [sPT[4], sPT[5]]][:S]
                exps = [dict() for _ in range(S)]
                mms = [dict() for _ in range(S)]
                avh = [dict() for _ in range(S)]
                avl = [None] * S
                for sl in range(NTK):
                    par = sl % 2
                    for si, (hh, qp) in enumerate(chunk):
                        w = [qkv_done]
                        if sl >= 1:
                            w.append(exps[si][sl - 1])
                        elif prev_exp_last is not None:
                            w.append(prev_exp_last)
                        mms[si][sl] = sched.emit("tensor", lambda e, hh=hh, qp=qp, sl=sl,
                                                 si=si, SB=SB: e.matmul(
                            SB[si][:, :], lhsT=sK[:, hh, sl * 128:(sl + 1) * 128],
                            rhs=sQ[:, hh, qp * 512:(qp + 1) * 512],
                            start=True, stop=True), waits=w)
                    if sl == 0 and pend_tail[0] is not None:
                        for tt in pend_tail[0]:
                            _emit_tail(*tt)
                        pend_tail[0] = None
                    for si, (hh, qp) in enumerate(chunk):
                        ew = [mms[si][sl]]
                        if sl >= 2:
                            ew.append(avh[si].get(sl - 2))
                        exps[si][sl] = sched.emit("scalar", lambda e, si=si, par=par,
                                                  PT=PT, SB=SB:
                            e.activation(out=PT[si][par], in_=SB[si][:, :], func=AF.Exp,
                                         scale=1.0 / 32.0), waits=ew)
                    for si, (hh, qp) in enumerate(chunk):
                        aw = [exps[si][sl]]
                        if sl == 0:
                            aw.append(tail_done.get(sidx0 + si - 3))
                        avh[si][sl] = sched.emit("tensor", lambda e, hh=hh, sl=sl, si=si,
                                                 par=par, AVB=AVB, PT=PT: e.matmul(
                            AVB[si][0:65, :], lhsT=sV[:, sl, hh * 65:hh * 65 + 65],
                            rhs=PT[si][par], start=(sl == 0), stop=(sl == NTK - 1)),
                            waits=aw)
                        avl[si] = avh[si][sl]
                prev_exp_last = exps[S - 1][NTK - 1]
                pend_tail[0] = [(sidx0 + si, AVB[si], (chunk[si][0] + g * 4) // 2,
                                 ((chunk[si][0] + g * 4) % 2) * 64, chunk[si][1], avl[si])
                                for si in range(S)]
                ci += S
            if pend_tail[0] is not None:
                for tt in pend_tail[0]:
                    _emit_tail(*tt)
                pend_tail[0] = None
            att_done = tail_done[att_sidx[0] - 1]

        zbo = sched.emit("gpsimd", lambda e: e.memset(sBoPad, 0.0),
                         waits=[g_gate, att_done, prev_c5[0]], sem="zb")
        tbo = sched.emit("vector", lambda e: e.tensor_copy(sBoPad[0:1, :], sBoR),
                         waits=[("zb", sched.cnt["zb"]), rows_ld])
        ot_done = att_done
        bank_free = {0: None, 1: None}
        out_free = [None, None]
        xr_free = [("tensor", sched.cnt["tensor"]), ("tensor", sched.cnt["tensor"])]
        prev_bn = (None, None)
        mprev = None
        mtoks = {}
        for t in range(NTQ):
            buf = sOut[t % 2]
            for half, bank in ((0, pA), (1, pB)):
                dd = [ot_done, cast_wo[-1]] + ([bank_free[half]] if bank_free[half] else [])
                for ic in range(8):
                    sched.emit("tensor", lambda e, t=t, ic=ic, half=half, bank=bank: e.matmul(
                        bank[:, :], lhsT=sOTb[:, ic, t * 128:(t + 1) * 128],
                        rhs=sWoB[:, ic, half * 512:(half + 1) * 512],
                        start=(ic == 0), stop=False), waits=dd)
                sched.emit("tensor", lambda e, half=half, bank=bank: e.matmul(
                    bank[:, :], lhsT=sE0[:], rhs=sBoPad[:, half * 512:(half + 1) * 512],
                    start=False, stop=True), waits=[tbo])
            tfl = sched.flush()
            xrb = sXr[t % 2]
            txr = sched.dma("sync", xrb,
                            x2kv.rearrange("(t p) d -> p t d", p=128)[:, t, :],
                            sem=f"xr{t % 2}",
                            waits=[xr_free[t % 2]])
            tr_toks = []
            for half, bank in ((0, pA), (1, pB)):
                tr = sched.emit("vector", lambda e, half=half, bank=bank, buf=buf, xrb=xrb:
                                e.tensor_tensor(out=buf[:, half * 512:(half + 1) * 512],
                                                in0=xrb[:, half * 512:(half + 1) * 512],
                                                in1=bank[:, :], op=ALU.add),
                                waits=[tfl, txr] + ([out_free[t % 2]] if out_free[t % 2] else []))
                bank_free[half] = tr
                tr_toks.append(tr)
            xr_free[t % 2] = tr_toks[-1]
            tln, tbn = emit_layernorm(sched, buf, buf, sBn6, sBn2, sScr, 1e-5,
                                      deps=tr_toks, prev=prev_bn)
            prev_bn = tbn
            tst = sched.dma("sync", x3r[:, t, :], buf, waits=[tln],
                            sem="stA" if t % 2 == 0 else "stB")
            out_free[t % 2] = tst
        sched.barrier("gpsimd", [("stA", sched.cnt["stA"]), ("stB", sched.cnt["stB"])])
        block = stack.enter_context(nc.Block())
        sched.build(block)
    return nc


def build_Cm():
    nc = bass.Bass("TRN2", target_bir_lowering=False)
    x3 = nc.dram_tensor("x3", [2048, 1024], F32, kind="ExternalInput").ap()
    idf = nc.dram_tensor("idf", [128, 128], F32, kind="ExternalInput").ap()
    m1o = nc.dram_tensor("m1o", [1, 1], F32, kind="ExternalOutput").ap()
    with ExitStack() as stack:
        sched = Sched(nc, stack)
        sb = lambda n, s, d=F32: stack.enter_context(nc.sbuf_tensor(n, s, d))
        ps = lambda n, s, d=F32: stack.enter_context(nc.psum_tensor(n, s, d))
        sX = sb("sX", [128, 16, 1024])
        sMW = sb("sMW", [128, 16])
        sM1 = sb("sM1", [128, 1])
        sIdf = sb("sIdf", [128, 128])
        sM1r = sb("sM1r", [1, 128])
        pT = ps("pT", [128, 512])
        tld = sched.dma("sync", sX[:], x3.rearrange("(t p) d -> p t d", p=128), sem="ld")
        sched.dma("sync", sIdf[:], idf, sem="ld")
        ld_all = ("ld", sched.cnt["ld"])
        toks = []
        for t in range(16):
            toks.append(sched.emit("vector", lambda e, t=t: e.tensor_reduce(
                out=sMW[:, t:t + 1], in_=sX[:, t, :], axis=AX.X, op=ALU.max,
                apply_absolute_value=True), waits=[ld_all]))
        tg = sched.emit("scalar", lambda e: e.activation(out=sM1[:], in_=sMW[:, 0:1],
                                                         func=AF.Identity), waits=toks)
        tf = sched.emit("vector", lambda e: e.tensor_reduce(
            out=sM1[:, 0:1], in_=sMW[:], axis=AX.X, op=ALU.max), waits=[tg])
        tg2 = sched.emit("scalar", lambda e: e.activation(out=sMW[:, 0:1], in_=sM1[:],
                                                          func=AF.Identity), waits=[tf])
        ttp = sched.emit("tensor", lambda e: e.transpose(pT[0:1, 0:128], sM1[:, 0:1], sIdf[:]),
                         waits=[tg2, ld_all])
        tdr = sched.emit("tensor", lambda e: e.drain(), waits=[])
        tm2 = sched.emit("vector", lambda e: e.tensor_reduce(
            out=sM1r[0:1, 0:1], in_=pT[0:1, 0:128], axis=AX.X, op=ALU.max), waits=[tdr])
        sched.dma("sync", m1o, sM1r[0:1, 0:1], waits=[tm2], sem="stM")
        sched.barrier("gpsimd", [("stM", sched.cnt["stM"])])
        block = stack.enter_context(nc.Block())
        sched.build(block)
    return nc


def host_C_inputs(x2_full_rot, Wq, bq, Wk, bk, Wv, bv, Wo, bo):
    return {
        "x2kv": np.ascontiguousarray(x2_full_rot, np.float32),
        "WqT": np.ascontiguousarray(np.asarray(Wq, np.float32).T),
        "WkT": np.ascontiguousarray(np.asarray(Wk, np.float32).T),
        "WvT": np.ascontiguousarray(np.asarray(Wv, np.float32).T),
        "WoT": np.ascontiguousarray(np.asarray(Wo, np.float32).T),
        "bqc": np.ascontiguousarray(np.asarray(bq, np.float32).reshape(8, 128).T),
        "bkc": np.ascontiguousarray(np.asarray(bk, np.float32).reshape(8, 128).T),
        "bvr": np.asarray(bv, np.float32).reshape(1, 1024),
        "bor": np.asarray(bo, np.float32).reshape(1, 1024),
        "onesb": np.ones((1, 128), BFDT),
        "idf": np.eye(128, dtype=np.float32),
    }



def build_A1():
    nc = bass.Bass("TRN2", target_bir_lowering=False)
    NSTR = _A1_STRIPES
    _ser = _SERIAL or _A1_SERIAL
    x = nc.dram_tensor("x", [4096, 1024], F32, kind="ExternalInput").ap()
    Wq = nc.dram_tensor("Wq", [1024, 512], F32, kind="ExternalInput").ap()
    Wk = nc.dram_tensor("Wk", [1024, 512], F32, kind="ExternalInput").ap()
    Wv = nc.dram_tensor("Wv", [1024, 512], F32, kind="ExternalInput").ap()
    Wg = nc.dram_tensor("Wg", [1024, 512], F32, kind="ExternalInput").ap()
    gat = nc.dram_tensor("gat", [128, 8], F32, kind="ExternalInput").ap()
    triu = nc.dram_tensor("triu", [64, 64], F32, kind="ExternalInput").ap()
    blkp = nc.dram_tensor("blkp", [128, 128], F32, kind="ExternalInput").ap()
    selp = nc.dram_tensor("selp", [128, 128], F32, kind="ExternalInput").ap()
    oTo = nc.dram_tensor("oTo", [512, 4096], BF16, kind="ExternalOutput").ap()
    oTr = oTo.rearrange("(s p) t -> p s t", p=128)

    with ExitStack() as stack:
        sched = Sched(nc, stack)
        sched.serial = _ser
        sb = lambda n, s, d=F32: stack.enter_context(nc.sbuf_tensor(n, s, d))
        ps = lambda n, s, d=F32: stack.enter_context(nc.psum_tensor(n, s, d))

        sX = sb("sX", [128, 8, 1024])
        sWqB = sb("sWqB", [128, 8, 512], BF16)
        sWkB = sb("sWkB", [128, 8, 512], BF16)
        sWvB = sb("sWvB", [128, 8, 512], BF16)
        sWgB = sb("sWgB", [128, 8, 512], BF16)
        sStage = sb("sStage", [128, 8, 512])
        sXFM = sb("sXFM", [128, 8, 1024], BF16)
        sXT = sb("sXT", [128, 1024], BF16)
        sQT = sb("sQT", [64, 8, 1024], BF16)
        sKT = sb("sKT", [64, 8, 1024], BF16)
        sKTM = sb("sKTM", [128, 8, 512], BF16)
        sVTM = sb("sVTM", [128, 8, 512], BF16)
        sLnr = sX[:, 0:4, :]
        sLc = sX[:, 4:8, :]
        sE1 = sb("sE1", [128, 4, 1024], BF16)
        sE2 = sb("sE2", [128, 4, 1024], BF16)
        sEc = sb("sEc", [128, 4, 16])
        sS = sb("sS", [64, 8, 64], BF16)
        sOT = sb("sOT", [128, 4, 1024], BF16)
        sF32 = sb("sF32", [128, 1024])
        sSq = sb("sSq", [128, 1024], BF16)
        sQh = sb("sQh", [128, 1024], BF16)
        sEbf = sb("sEbf", [128, 1024])
        sRn = sb("sRn", [128, 512], BF16)
        sRsc = sb("sRsc", [128, 512])
        sTri = sb("sTri", [64, 64], BF16)
        sBlk = sb("sBlk", [128, 128], BF16)
        sSel = sb("sSel", [128, 128], BF16)
        sG = sb("sG", [128, 8])
        sSm = [sb("sSmA", [128, 64], BF16), sb("sSmB", [128, 64], BF16)]
        sT1 = sb("sT1", [64, 8, 64], BF16)
        sOev = sb("sOev", [64, 64], BF16)
        sBn6 = sb("sBn6", [128, 12]); sBn2 = sb("sBn2", [128, 2]); sScr = sb("sScr", [128, 6])
        pA = ps("pA", [128, 512])
        pB = ps("pB", [128, 512])
        pS2 = [ps("pS0", [128, 512]), ps("pS1", [128, 512])]
        pO2 = [ps("pO0", [128, 512]), ps("pO1", [128, 512])]
        pSt = ps("pSt", [128, 512])
        pF = ps("pF", [128, 512])

        sched.dma("sync", sG[:], gat, sem="ld")
        sTriF = sb("sTriF", [64, 64]); sBlkF = sb("sBlkF", [128, 128]); sSelF = sb("sSelF", [128, 128])
        sched.dma("sync", sTriF[:], triu, sem="ld")
        sched.dma("sync", sBlkF[:], blkp, sem="ld")
        sched.dma("sync", sSelF[:], selp, sem="ld")
        ld_all = ("ld", sched.cnt["ld"])
        sched.emit("gpsimd", lambda e: e.memset(sS[:], 0.0), sem="zz")
        sched.emit("gpsimd", lambda e: e.memset(sRn[:], 0.0), sem="zz")
        zz_all = ("zz", sched.cnt["zz"])
        cst = []
        cst.append(sched.emit("vector", lambda e: e.tensor_copy(sTri[:], sTriF[:]), waits=[ld_all]))
        cst.append(sched.emit("vector", lambda e: e.tensor_copy(sBlk[:], sBlkF[:]), waits=[ld_all]))
        cst.append(sched.emit("vector", lambda e: e.tensor_copy(sSel[:], sSelF[:]), waits=[ld_all]))
        wprev = None
        for (wsrc, wdst, fold) in ((Wq, sWqB, True), (Wk, sWkB, True), (Wg, sWgB, True),
                                   (Wv, sWvB, True)):
            sched.dma("sync", sStage[:], wsrc.rearrange("(c p) o -> p c o", p=128),
                      sem="wl", waits=[wprev])
            tw = ("wl", sched.cnt["wl"])
            for c in range(8):
                wprev = sched.emit("vector", lambda e, wdst=wdst, c=c: e.tensor_scalar(
                    out=wdst[:, c, :], in0=sStage[:, c, :], scalar1=sG[:, c:c + 1],
                    scalar2=None, op0=ALU.mult), waits=[tw, ld_all])
        w_all = wprev

        sched.set_flush(sBlk[:], sBlk[:], pF)

        def flush64(waits=()):
            return sched.emit("tensor", lambda e: e.matmul(
                pF[0:64, 0:64], lhsT=sTri[:], rhs=sTri[:], start=True, stop=True),
                waits=list(waits))

        st_free = None
        prev_chunk_done = None
        prev_e_done = None
        for st in range(NSTR):
            tld = sched.dma("sync", sX[:], x.rearrange("(t p) d -> p t d", p=128)
                            [:, st * 8:(st + 1) * 8, :], sem=f"xs{st % 2}",
                            waits=[st_free, prev_e_done])
            tprev_tt = None
            prev_rms = None
            tr_toks = []
            for t in range(8):
                s1 = sched.emit("vector", lambda e, t=t: e.bn_stats(sBn6[:, 0:6],
                                sX[:, t, 0:512]), waits=[tld, prev_rms])
                s2 = sched.emit("vector", lambda e, t=t: e.bn_stats(sBn6[:, 6:12],
                                sX[:, t, 512:1024]), waits=[tld, s1])
                s3 = sched.emit("vector", lambda e: e.bn_aggr(sBn2[:], sBn6[:]), waits=[s2])
                s4 = sched.emit("vector", lambda e: e.tensor_scalar(
                    out=sScr[:, 0:1], in0=sBn2[:, 0:1], scalar1=sBn2[:, 0:1],
                    scalar2=sBn2[:, 1:2], op0=ALU.mult, op1=ALU.add), waits=[s3])
                s5 = sched.emit("vector", lambda e: e.tensor_scalar(
                    out=sScr[:, 1:2], in0=sScr[:, 0:1], scalar1=1e-6, scalar2=None,
                    op0=ALU.add), waits=[s4])
                t3 = sched.emit("scalar", lambda e: e.activation(out=sScr[:, 2:3],
                                in_=sScr[:, 1:2], func=AF.Ln), waits=[s5])
                t4 = sched.emit("scalar", lambda e: e.activation(out=sScr[:, 3:4],
                                in_=sScr[:, 2:3], func=AF.Exp, scale=-0.5), waits=[t3])
                prev_rms = t4
                t5 = sched.emit("scalar", lambda e, t=t: e.activation(
                    out=sXT[:], in_=sX[:, t, :], func=AF.Identity, scale=sScr[:, 3:4]),
                    waits=[t4, tprev_tt])
                tprev_tt = sched.dma("sync", sXFM[:, :, t * 128:(t + 1) * 128], sXT[:],
                                     waits=[t5], transpose=True, sem=f"tf{st % 2}")
            tXF = (f"tf{st % 2}", sched.cnt[f"tf{st % 2}"])
            st_free = tXF

            pd = [tXF, w_all] + cst + [zz_all, prev_chunk_done]
            ev = None
            for t in range(8 if _A1_LEVEL >= 2 else 0):
                for ic in range(8):
                    sched.emit("tensor", lambda e, t=t, ic=ic: e.matmul(
                        pA[:, :], lhsT=sXFM[:, ic, t * 128:(t + 1) * 128],
                        rhs=sWvB[:, ic, :], start=(ic == 0), stop=(ic == 7)),
                        waits=pd + ([ev] if ev else []))
                tf = sched.flush()
                ev = sched.emit("vector", lambda e, t=t: e.tensor_copy(
                    sVTM[:, t, :], pA[:, :]), waits=[tf])
            v_done = ev
            for sl in range(4 if _A1_LEVEL >= 3 else 0):
                for piece in range(2):
                    for ic in range(8):
                        sched.emit("tensor", lambda e, sl=sl, piece=piece, ic=ic: e.matmul(
                            pA[:, :], lhsT=sWgB[:, ic, sl * 128:(sl + 1) * 128],
                            rhs=sXFM[:, ic, piece * 512:(piece + 1) * 512],
                            start=(ic == 0), stop=(ic == 7)),
                            waits=pd + ([ev] if ev else []))
                    tf = sched.flush()
                    e1 = sched.emit("scalar", lambda e, piece=piece: e.activation(
                        out=sEbf[:, piece * 512:(piece + 1) * 512], in_=pA[:, :],
                        func=AF.Exp, scale=-1.0), waits=[tf])
                    ev = sched.emit("scalar", lambda e, sl=sl, piece=piece: e.activation(
                        out=sLnr[:, sl, piece * 512:(piece + 1) * 512],
                        in_=sEbf[:, piece * 512:(piece + 1) * 512],
                        func=AF.Ln, bias=1.0), waits=[e1])
            lnr_done = ev
            sc = []
            for sl in range(4 if _A1_LEVEL >= 4 else 0):
                for wdw in range(16):
                    sc.append(sched.emit("vector", lambda e, sl=sl, wdw=wdw: e.tensor_tensor_scan(
                        out=sLc[:, sl, wdw * 64:(wdw + 1) * 64],
                        data0=sLnr[:, sl, wdw * 64:(wdw + 1) * 64],
                        data1=sLnr[:, sl, wdw * 64:(wdw + 1) * 64],
                        initial=0.0, op0=ALU.add, op1=ALU.bypass), waits=[lnr_done]))
            scan_done = sc[-1] if sc else lnr_done
            ee = []
            for sl in range(4 if _A1_LEVEL >= 5 else 0):
                ee.append(sched.emit("scalar", lambda e, sl=sl: e.activation(
                    out=sE1[:, sl, :], in_=sLc[:, sl, :], func=AF.Exp, scale=-1.0),
                    waits=[scan_done]))
                ee.append(sched.emit("scalar", lambda e, sl=sl: e.activation(
                    out=sE2[:, sl, :], in_=sLc[:, sl, :], func=AF.Exp, scale=1.0),
                    waits=[scan_done]))
                for cch in range(16):
                    ee.append(sched.emit("scalar", lambda e, sl=sl, cch=cch: e.activation(
                        out=sEc[:, sl, cch:cch + 1], in_=sLc[:, sl, cch * 64 + 63:cch * 64 + 64],
                        func=AF.Exp, scale=-1.0), waits=[scan_done]))
            e_done = ee[-1] if ee else scan_done
            prev_e_done = e_done
            prev_h1 = prev_n3 = None
            for (wb, dst, ebuf) in (((sWqB, sQT, sE1), (sWkB, sKT, sE2)) if _A1_LEVEL >= 6 else ()):
                for sl in range(4):
                    mm_t = None
                    for piece in range(2):
                        for ic in range(8):
                            mm_t = sched.emit("tensor", lambda e, sl=sl, piece=piece, ic=ic,
                                              wb=wb: e.matmul(
                                pB[:, :] if piece == 0 else pSt[:, :],
                                lhsT=wb[:, ic, sl * 128:(sl + 1) * 128],
                                rhs=sXFM[:, ic, piece * 512:(piece + 1) * 512],
                                start=(ic == 0), stop=(ic == 7)),
                                waits=pd + ([ev] if ev else []))
                    tf = sched.flush()
                    c1 = sched.emit("vector", lambda e: e.tensor_copy(sF32[:, 0:512],
                                    pB[:, :]), waits=[tf])
                    c2 = sched.emit("vector", lambda e: e.tensor_copy(sF32[:, 512:1024],
                                    pSt[:, :]), waits=[tf, c1])
                    q1 = sched.emit("vector", lambda e: e.tensor_tensor(
                        out=sSq[:, 0:512], in0=sF32[:, 0:512], in1=sF32[:, 0:512],
                        op=ALU.mult), waits=[c1])
                    q2 = sched.emit("vector", lambda e: e.tensor_tensor(
                        out=sSq[:, 512:1024], in0=sF32[:, 512:1024], in1=sF32[:, 512:1024],
                        op=ALU.mult), waits=[c2, q1])
                    if _A1_L6SUB < 2:
                        ev = q2
                    for piece in range(2 if _A1_L6SUB >= 2 else 0):
                        m1 = sched.emit("tensor", lambda e, piece=piece: e.matmul(
                            pA[:, :], lhsT=sBlk[:], rhs=sSq[:, piece * 512:(piece + 1) * 512],
                            start=True, stop=True), waits=[q1, q2, c1, c2, prev_h1])
                        tf2 = sched.flush()
                        n1 = sched.emit("vector", lambda e: e.tensor_scalar(
                            out=sRsc[:], in0=pA[:, :], scalar1=1e-18, scalar2=None,
                            op0=ALU.add), waits=[tf2, prev_n3])
                        if _A1_L6SUB < 3:
                            prev_h1, prev_n3 = m1, n1
                            ev = n1
                            continue
                        n2 = sched.emit("scalar", lambda e: e.activation(out=sRsc[:],
                                        in_=sRsc[:], func=AF.Ln), waits=[n1])
                        n3 = sched.emit("scalar", lambda e: e.activation(out=sRn[:],
                                        in_=sRsc[:], func=AF.Exp, scale=-0.5), waits=[n2])
                        if _A1_L6SUB < 4:
                            prev_h1, prev_n3 = m1, n3
                            ev = n3
                            continue
                        m2 = sched.emit("tensor", lambda e: e.matmul(
                            pA[:, :], lhsT=sSel[:], rhs=sRn[:], start=True, stop=True),
                            waits=[n3, zz_all])
                        tf3 = sched.flush()
                        h1 = sched.emit("vector", lambda e, piece=piece: e.tensor_tensor(
                            out=sQh[:, piece * 512:(piece + 1) * 512],
                            in0=sF32[:, piece * 512:(piece + 1) * 512], in1=pA[:, :],
                            op=ALU.mult), waits=[tf3])
                        prev_h1, prev_n3 = h1, n3
                        if _A1_L6SUB < 5:
                            prev_h1, prev_n3 = h1, n3
                            ev = h1
                            continue
                        for hloc in range(2):
                            off2 = hloc * 64
                            ev = sched.emit("vector", lambda e, sl=sl, piece=piece, dst=dst,
                                            ebuf=ebuf, off2=off2, hloc=hloc: e.tensor_tensor(
                                out=dst[0:64, 2 * sl + hloc, piece * 512:(piece + 1) * 512],
                                in0=sQh[off2:off2 + 64, piece * 512:(piece + 1) * 512],
                                in1=ebuf[off2:off2 + 64, sl, piece * 512:(piece + 1) * 512],
                                op=ALU.mult), waits=[h1, e_done])
            qk_done = ev if ev is not None else e_done
            ktm_t = []
            kcp = qk_done
            for sl in range(4 if _A1_LEVEL >= 7 else 0):
                kc1 = sched.emit("vector", lambda e, sl=sl: e.tensor_copy(
                    sSq[0:64, :], sKT[0:64, 2 * sl, :]), waits=[qk_done, kcp])
                kc2 = sched.emit("vector", lambda e, sl=sl: e.tensor_copy(
                    sSq[64:128, :], sKT[0:64, 2 * sl + 1, :]), waits=[kc1])
                tkt = sched.dma("sync", sKTM[:, :, sl * 128:(sl + 1) * 128],
                                sSq[:], waits=[kc2], transpose=True, sem=f"kt{st % 2}")
                kcp = tkt
                ktm_t.append(tkt)
            ktm_done = (f"kt{st % 2}", sched.cnt[f"kt{st % 2}"]) if sched.cnt.get(f"kt{st % 2}") else qk_done

            tdr = sched.emit("tensor", lambda e: e.drain(), waits=[])
            cdep = [ktm_done, v_done, qk_done, e_done, tdr] + cst
            prev_sm = [None, None]
            prev_oe1 = {}
            prev_oe2 = None
            prev_supd = {}
            prev_su1 = None
            oev = None
            for cch in range(_A1_CHUNKS):
                tt_ = cch // 2
                toff = (cch % 2) * 64
                tc0 = cch * 64
                stbank = [pSt, pB][cch % 2]
                pint = pO2[cch % 2]
                for hh in range(8):
                    sl = hh // 2
                    par = hh % 2
                    sc_t = sched.emit("tensor", lambda e, hh=hh, tc0=tc0, par=par: e.matmul(
                        pS2[par][0:64, 0:64], lhsT=sKT[0:64, hh, tc0:tc0 + 64],
                        rhs=sQT[0:64, hh, tc0:tc0 + 64], start=True, stop=True),
                        waits=cdep + [prev_sm[par]])
                    st_t = sched.emit("tensor", lambda e, hh=hh, tt_=tt_, toff=toff,
                                      stbank=stbank: e.matmul(
                        stbank[0:64, 0:64], lhsT=sKTM[toff:toff + 64, tt_, hh * 64:hh * 64 + 64],
                        rhs=sVTM[toff:toff + 64, tt_, hh * 64:hh * 64 + 64],
                        start=True, stop=True), waits=cdep + [prev_supd.get(hh), prev_su1])
                    f1 = flush64()
                    sm_t = sched.emit("vector", lambda e, par=par, toff=toff: e.tensor_tensor(
                        out=sSm[par][toff:toff + 64, :], in0=pS2[par][0:64, 0:64], in1=sTri[:],
                        op=ALU.mult), waits=[f1, prev_oe1.get(par)])
                    prev_sm[par] = sm_t
                    su1 = sched.emit("vector", lambda e, hh=hh, stbank=stbank: e.tensor_tensor(
                        out=sT1[0:64, hh, :], in0=sS[0:64, hh, :],
                        in1=stbank[0:64, 0:64], op=ALU.add),
                        waits=[f1, prev_supd.get(hh)])
                    prev_su1 = su1
                    oi1 = sched.emit("tensor", lambda e, hh=hh, tt_=tt_, toff=toff, par=par,
                                     pint=pint: e.matmul(
                        pint[0:64, 0:64], lhsT=sVTM[toff:toff + 64, tt_, hh * 64:hh * 64 + 64],
                        rhs=sSm[par][toff:toff + 64, :], start=True, stop=True),
                        waits=[sm_t, prev_oe1.get(par)])
                    oi2 = sched.emit("tensor", lambda e, hh=hh, tc0=tc0: e.matmul(
                        pA[0:64, 0:64], lhsT=sS[0:64, hh, :],
                        rhs=sQT[0:64, hh, tc0:tc0 + 64], start=True, stop=True),
                        waits=[prev_supd.get(hh), prev_oe2, cdep[0] if cch == 0 else None,
                               cdep[1] if cch == 0 else None])
                    f2 = flush64()
                    su2 = sched.emit("vector", lambda e, hh=hh, sl=sl, par=par, cch=cch:
                                     e.tensor_scalar(
                        out=sS[0:64, hh, :], in0=sT1[0:64, hh, :],
                        scalar1=sEc[par * 64:par * 64 + 64, sl, cch:cch + 1], scalar2=None,
                        op0=ALU.mult), waits=[su1, f2])
                    prev_supd[hh] = su2
                    oe1 = sched.emit("scalar", lambda e, pint=pint: e.activation(
                        out=sOev[:], in_=pint[0:64, 0:64], func=AF.Identity),
                        waits=[f2, prev_oe2])
                    prev_oe1[cch % 2] = oe1
                    oev = sched.emit("vector", lambda e, hh=hh, sl=sl, par=par, tc0=tc0:
                                     e.tensor_tensor(
                        out=sOT[par * 64:par * 64 + 64, sl, tc0:tc0 + 64], in0=sOev[:],
                        in1=pA[0:64, 0:64], op=ALU.add), waits=[oe1])
                    prev_oe2 = oev
            tdr2 = sched.emit("tensor", lambda e: e.drain(), waits=[])
            sched.dma("sync", oTr[:, :, st * 1024:(st + 1) * 1024], sOT[:],
                      waits=[oev, prev_supd.get(7), prev_supd.get(6), qk_done], sem="stO")
            prev_chunk_done = ("stO", sched.cnt["stO"])
            for kpar in (0, 1):
                prev_oe1[kpar] = prev_chunk_done
            prev_sm = [prev_chunk_done, prev_chunk_done]
            prev_oe2 = prev_chunk_done

        sched.barrier("gpsimd", [("stO", sched.cnt["stO"])])
        block = stack.enter_context(nc.Block())
        sched.build(block)
    return nc


def host_A1_inputs(x_b, Wq_s, Wk_s, Wv_s, Wg_s, g_attn):
    p = np.arange(128)
    tri = (np.arange(64)[:, None] <= np.arange(64)[None, :]).astype(np.float32)
    blkp = np.zeros((128, 128), np.float32)
    for j in range(2):
        blkp[:, j] = (p // 64 == j)
    selp = np.zeros((128, 128), np.float32)
    for j in range(2):
        selp[j, :] = (p // 64 == j)
    return {
        "x": np.ascontiguousarray(x_b, np.float32),
        "Wq": np.ascontiguousarray(Wq_s, np.float32),
        "Wk": np.ascontiguousarray(Wk_s, np.float32),
        "Wv": np.ascontiguousarray(Wv_s, np.float32),
        "Wg": np.ascontiguousarray(Wg_s, np.float32),
        "gat": np.ascontiguousarray(np.asarray(g_attn, np.float32).reshape(8, 128).T),
        "triu": tri, "blkp": blkp, "selp": selp,
    }



def _emit_scalar_prep(nc, sched, sb, ps, mrow_ap, clip_ap, sE0, pR, n_parts):
    sSc = sb("sScal", [1, 8])
    sPad = sb("sPad", [128, 2], BF16)
    sCols = sb("sCols", [128, 2])
    z = sched.emit("gpsimd", lambda e: e.memset(sPad[:], 0.0), sem="zz")
    t1 = sched.emit("vector", lambda e: e.tensor_reduce(
        out=sSc[0:1, 0:1], in_=mrow_ap, axis=AX.X, op=ALU.max), waits=[("ld", sched.cnt["ld"])])
    t2 = sched.emit("scalar", lambda e: e.activation(out=sSc[0:1, 1:2], in_=clip_ap,
                                                     func=AF.Sigmoid),
                    waits=[("ld", sched.cnt["ld"]), t1])
    t3 = sched.emit("vector", lambda e: e.tensor_tensor(out=sSc[0:1, 2:3], in0=sSc[0:1, 0:1],
                                                        in1=sSc[0:1, 1:2], op=ALU.mult),
                    waits=[t2, t1])
    t3 = sched.emit("vector", lambda e: e.tensor_scalar(
        out=sSc[0:1, 2:3], in0=sSc[0:1, 2:3], scalar1=1.0 / 127.0, scalar2=None,
        op0=ALU.mult), waits=[t3])
    t4 = sched.emit("vector", lambda e: e.reciprocal(sSc[0:1, 3:4], sSc[0:1, 2:3]), waits=[t3])
    t5 = sched.emit("vector", lambda e: e.tensor_scalar(
        out=sSc[0:1, 4:5], in0=sSc[0:1, 3:4], scalar1=1.0, scalar2=None, op0=ALU.mult),
        waits=[t4])
    t6 = sched.emit("vector", lambda e: e.tensor_copy(sPad[0:1, 0:1], sSc[0:1, 4:5]),
                    waits=[t5, z])
    t7 = sched.emit("vector", lambda e: e.tensor_copy(sPad[0:1, 1:2], sSc[0:1, 2:3]),
                    waits=[t6])
    m1 = sched.emit("tensor", lambda e: e.matmul(pR[:, 0:2], lhsT=sE0[:], rhs=sPad[:],
                                                 start=True, stop=True), waits=[t7])
    fl = sched.flush()
    t8 = sched.emit("vector", lambda e: e.tensor_copy(sCols[:], pR[:, 0:2]), waits=[fl])
    return sCols[:, 0:1], sCols[:, 1:2], t8


def build_D1():
    nc = bass.Bass("TRN2", target_bir_lowering=False)
    x3 = nc.dram_tensor("x3", [2048, 1024], F32, kind="ExternalInput").ap()
    m1row = nc.dram_tensor("m1row", [1, 8], F32, kind="ExternalInput").ap()
    clipx = nc.dram_tensor("clipx", [1, 1], F32, kind="ExternalInput").ap()
    wg = nc.dram_tensor("wg", [2816, 1024], F32, kind="ExternalInput").ap()
    wu = nc.dram_tensor("wu", [2816, 1024], F32, kind="ExternalInput").ap()
    onesb = nc.dram_tensor("onesb", [1, 128], BF16, kind="ExternalInput").ap()
    po = nc.dram_tensor("po", [2816, 2048], BF16, kind="ExternalOutput").ap()
    m2o = nc.dram_tensor("m2o", [1, 1], F32, kind="ExternalOutput").ap()
    idf = nc.dram_tensor("idf", [128, 128], F32, kind="ExternalInput").ap()
    dbgS = nc.dram_tensor("dbgS", [128, 6], F32, kind="ExternalOutput").ap() if _DEBUG_D1 else None
    dbgG = nc.dram_tensor("dbgG", [128, 64], BF16, kind="ExternalOutput").ap() if _DEBUG_D1 else None
    dbgU = nc.dram_tensor("dbgU", [128, 64], BF16, kind="ExternalOutput").ap() if _DEBUG_D1 else None
    dbgX = nc.dram_tensor("dbgX", [128, 64], BF16, kind="ExternalOutput").ap() if _DEBUG_D1 else None

    with ExitStack() as stack:
        sched = Sched(nc, stack)
        sched.serial = _SERIAL
        sb = lambda n, s, d=F32: stack.enter_context(nc.sbuf_tensor(n, s, d))
        ps = lambda n, s, d=F32: stack.enter_context(nc.psum_tensor(n, s, d))
        sXs = [sb("sXsA", [128, 1024]), sb("sXsB", [128, 1024])]
        sXq = sb("sXq", [128, 1024], BF16)
        sXF = sb("sXF", [128, 8, 2048], BF16)
        sWs = [sb("sWsA", [128, 1024]), sb("sWsB", [128, 1024])]
        sWq = [sb("sWqA", [128, 1024], BF16), sb("sWqB2", [128, 1024], BF16)]
        sWgF = [sb("sWgFa", [128, 8, 128], BF16), sb("sWgFb", [128, 8, 128], BF16)]
        sWuF = [sb("sWuFa", [128, 8, 128], BF16), sb("sWuFb", [128, 8, 128], BF16)]
        sQt = sb("sQt", [128, 1024])
        sQi = sb("sQi", [128, 1024], mybir.dt.int32)
        sCol = sb("sCol", [128, 6])
        sSilu = sb("sSilu", [128, 2048], BF16)
        sUp = sb("sUp", [128, 2048], BF16)
        sPp = [sb("sPpA", [128, 2048], BF16), sb("sPpB", [128, 2048], BF16)]
        sMW = sb("sMW", [128, 44])
        sM1 = sb("sM1", [128, 1])
        sM1r = sb("sM1r", [1, 128])
        sIdf = sb("sIdf", [128, 128])
        sE0 = sb("sE0", [128, 128], BF16)
        pB4 = [ps(f"p{i}", [128, 512]) for i in range(4)]
        pR = ps("pR", [128, 512])
        pF = ps("pF", [128, 512])

        sched.dma("sync", sIdf[:], idf, sem="ld")
        z0 = sched.emit("gpsimd", lambda e: e.memset(sE0[:], 0.0), sem="zz")
        te = sched.dma("sync", sE0[0:1, :], onesb, sem="ld", waits=[z0])
        zM = sched.emit("gpsimd", lambda e: e.memset(sCol[:, 0:1], MAGIC), sem="zz")
        m1l = sched.dma("sync", sM1r[0:1, 0:8].bitcast(F32) if False else sM1r[0:1, 0:8],
                        m1row, sem="ld")
        cxl = sched.dma("sync", sM1[0:1, 0:1], clipx, sem="ld")
        sched.set_flush(sE0[:], sE0[:], pF)
        rs_col, s_col, tcols = _emit_scalar_prep(nc, sched, sb, ps, sM1r[0:1, 0:8],
                                                 sM1[0:1, 0:1], sE0, pR, 128)

        xfree = [None, None]
        prev_xq = None
        for t in range(16):
            par = t % 2
            tl = sched.dma("sync", sXs[par][:], x3.rearrange("(t p) d -> p t d", p=128)[:, t, :],
                           sem=f"x{par}", waits=[xfree[par]])
            a1 = sched.emit("scalar", lambda e, par=par: e.activation(
                out=sQt[:], in_=sXs[par][:], func=AF.Identity, scale=rs_col),
                waits=[tl, tcols, prev_xq if t else None])
            a2 = sched.emit("vector", lambda e: e.tensor_copy(sQi[:], sQt[:]), waits=[a1])
            a3 = sched.emit("vector", lambda e: e.tensor_scalar(
                out=sXq[:], in0=sQi[:], scalar1=-128.0, scalar2=127.0,
                op0=ALU.max, op1=ALU.min), waits=[a2])
            prev_xq = sched.dma("sync", sXF[:, :, t * 128:(t + 1) * 128], sXq[:],
                                waits=[a3], transpose=True, sem="tq")
            xfree[par] = a1
        txq = ("tq", sched.cnt["tq"])

        wfree = [None, None]
        mm_free = None
        ev_free = None
        ppfree = [None, None]
        mtoks = []
        for sl in range(22):
            par = sl % 2
            tlg = sched.dma("sync", sWs[0][:], wg[sl * 128:(sl + 1) * 128, :],
                            sem=f"wg{0}", waits=[wfree[0]])
            am = sched.emit("vector", lambda e: e.tensor_reduce(
                out=sCol[:, 1:2], in_=sWs[0][:], axis=AX.X, op=ALU.max,
                apply_absolute_value=True), waits=[tlg])
            rc = sched.emit("vector", lambda e: e.reciprocal(sCol[:, 2:3], sCol[:, 1:2]),
                            waits=[am])
            rc2 = sched.emit("vector", lambda e: e.tensor_scalar(
                out=sCol[:, 2:3], in0=sCol[:, 2:3], scalar1=127.0, scalar2=None,
                op0=ALU.mult), waits=[rc])
            scg = sched.emit("vector", lambda e: e.tensor_scalar(
                out=sCol[:, 3:4], in0=sCol[:, 1:2], scalar1=1.0 / 127.0, scalar2=s_col,
                op0=ALU.mult, op1=ALU.mult), waits=[rc2, tcols])
            q1 = sched.emit("scalar", lambda e: e.activation(
                out=sQt[:], in_=sWs[0][:], func=AF.Identity, scale=sCol[:, 2:3]),
                waits=[rc2, ev_free])
            q2 = sched.emit("vector", lambda e: e.tensor_copy(sQi[:], sQt[:]), waits=[q1])
            q3 = sched.emit("vector", lambda e, par=par: e.tensor_scalar(
                out=sWq[0][:], in0=sQi[:], scalar1=-128.0, scalar2=127.0,
                op0=ALU.max, op1=ALU.min), waits=[q2, wfree[0]])
            tq1 = sched.dma("sync", sWgF[par][:], sWq[0][:], waits=[q3, mm_free],
                            transpose=True, sem=f"tw{par}")
            wfree[0] = tq1
            tlu = sched.dma("sync", sWs[1][:], wu[sl * 128:(sl + 1) * 128, :],
                            sem=f"wu{0}", waits=[wfree[1]])
            am2 = sched.emit("vector", lambda e: e.tensor_reduce(
                out=sCol[:, 1:2], in_=sWs[1][:], axis=AX.X, op=ALU.max,
                apply_absolute_value=True), waits=[tlu, scg])
            rcB = sched.emit("vector", lambda e: e.reciprocal(sCol[:, 2:3], sCol[:, 1:2]),
                             waits=[am2])
            rcB2 = sched.emit("vector", lambda e: e.tensor_scalar(
                out=sCol[:, 2:3], in0=sCol[:, 2:3], scalar1=127.0, scalar2=None,
                op0=ALU.mult), waits=[rcB])
            scu = sched.emit("vector", lambda e: e.tensor_scalar(
                out=sCol[:, 4:5], in0=sCol[:, 1:2], scalar1=1.0 / 127.0, scalar2=s_col,
                op0=ALU.mult, op1=ALU.mult), waits=[rcB2])
            u1 = sched.emit("scalar", lambda e: e.activation(
                out=sQt[:], in_=sWs[1][:], func=AF.Identity, scale=sCol[:, 2:3]),
                waits=[rcB2, q3])
            u2 = sched.emit("vector", lambda e: e.tensor_copy(sQi[:], sQt[:]), waits=[u1])
            u3 = sched.emit("vector", lambda e, par=par: e.tensor_scalar(
                out=sWq[1][:], in0=sQi[:], scalar1=-128.0, scalar2=127.0,
                op0=ALU.max, op1=ALU.min), waits=[u2, wfree[1]])
            tq2 = sched.dma("sync", sWuF[par][:], sWq[1][:], waits=[u3, mm_free],
                            transpose=True, sem=f"tw{par}")
            wfree[1] = tq2
            twq = (f"tw{par}", sched.cnt[f"tw{par}"])

            gmm = []
            for piece in range(4):
                for ic in range(8):
                    gmm.append(sched.emit("tensor", lambda e, piece=piece, ic=ic, par=par:
                                          e.matmul(
                        pB4[piece][:, :], lhsT=sWgF[par][:, ic, :],
                        rhs=sXF[:, ic, piece * 512:(piece + 1) * 512],
                        start=(ic == 0), stop=(ic == 7)),
                        waits=[twq, txq, te, ev_free]))
            fl = sched.flush()
            evg = []
            for piece in range(4):
                evg.append(sched.emit("scalar", lambda e, piece=piece: e.activation(
                    out=sSilu[:, piece * 512:(piece + 1) * 512], in_=pB4[piece][:, :],
                    func=AF.Silu, scale=sCol[:, 3:4]), waits=[fl, scg]))
            ummh = []
            for piece in range(4):
                for ic in range(8):
                    ummh.append(sched.emit("tensor", lambda e, piece=piece, ic=ic, par=par:
                                           e.matmul(
                        pB4[piece][:, :], lhsT=sWuF[par][:, ic, :],
                        rhs=sXF[:, ic, piece * 512:(piece + 1) * 512],
                        start=(ic == 0), stop=(ic == 7)),
                        waits=[twq, evg[piece]]))
            fl2 = sched.flush()
            mm_free = fl2
            evu = []
            for piece in range(4):
                evu.append(sched.emit("scalar", lambda e, piece=piece: e.activation(
                    out=sUp[:, piece * 512:(piece + 1) * 512], in_=pB4[piece][:, :],
                    func=AF.Identity, scale=sCol[:, 4:5]), waits=[fl2, scu]))
            if _DEBUG_D1 and sl == 0:
                sched.dma("sync", dbgS, sCol[:, 0:8].rearrange("p c -> p c")
                          if False else sCol[:], waits=[scg, scu], sem="stM")
                sched.dma("sync", dbgG, sSilu[:, 0:64], waits=evg + evu, sem="stM")
                sched.dma("sync", dbgU, sUp[:, 0:64], waits=evg + evu, sem="stM")
                sched.dma("sync", dbgX, sXF[:, 0, 0:64], waits=[txq], sem="stM")
            pp = sPp[par]
            tp = sched.emit("vector", lambda e, pp=pp: e.tensor_tensor(
                out=pp[:], in0=sSilu[:], in1=sUp[:], op=ALU.mult),
                waits=evu + evg + [ppfree[par]])
            ev_free = tp
            tmx = sched.emit("vector", lambda e, sl=sl, pp=pp: e.tensor_reduce(
                out=sMW[:, sl:sl + 1], in_=pp[:], axis=AX.X, op=ALU.max), waits=[tp])
            tmn = sched.emit("vector", lambda e, sl=sl, pp=pp: e.tensor_reduce(
                out=sMW[:, 22 + sl:23 + sl], in_=pp[:], axis=AX.X, op=ALU.min,
                negate=True), waits=[tp, tmx])
            mtoks.append(tmn)
            tst = sched.dma("sync", po[sl * 128:(sl + 1) * 128, :], pp[:],
                            waits=[tp], sem=f"sp{par}")
            ppfree[par] = tst

        tg = sched.emit("scalar", lambda e: e.activation(out=sM1[:], in_=sMW[:, 0:1],
                                                         func=AF.Identity), waits=mtoks)
        tf_ = sched.emit("vector", lambda e: e.tensor_reduce(
            out=sM1[:, 0:1], in_=sMW[:], axis=AX.X, op=ALU.max), waits=[tg])
        tg2 = sched.emit("scalar", lambda e: e.activation(out=sMW[:, 0:1], in_=sM1[:],
                                                          func=AF.Identity), waits=[tf_])
        tdr0 = sched.emit("tensor", lambda e: e.drain(), waits=[])
        ttp = sched.emit("tensor", lambda e: e.transpose(pR[0:1, 0:128], sM1[:, 0:1], sIdf[:]),
                         waits=[tg2, ("ld", sched.cnt["ld"]), tdr0])
        tdr = sched.emit("tensor", lambda e: e.drain(), waits=[])
        tm2 = sched.emit("vector", lambda e: e.tensor_reduce(
            out=sM1r[0:1, 0:1], in_=pR[0:1, 0:128], axis=AX.X, op=ALU.max), waits=[tdr])
        sched.dma("sync", m2o, sM1r[0:1, 0:1], waits=[tm2], sem="stM")
        sched.barrier("gpsimd", [("stM", sched.cnt["stM"]), ("sp0", sched.cnt["sp0"]),
                                 ("sp1", sched.cnt["sp1"])])
        block = stack.enter_context(nc.Block())
        sched.build(block)
    return nc


def build_D2():
    nc = bass.Bass("TRN2", target_bir_lowering=False)
    NT = 16
    po = nc.dram_tensor("po", [2816, 2048], BF16, kind="ExternalInput").ap()
    m2row = nc.dram_tensor("m2row", [1, 8], F32, kind="ExternalInput").ap()
    clipd = nc.dram_tensor("clipd", [1, 1], F32, kind="ExternalInput").ap()
    wd = nc.dram_tensor("wd", [1024, 2816], F32, kind="ExternalInput").ap()
    x3 = nc.dram_tensor("x3", [2048, 1024], F32, kind="ExternalInput").ap()
    onesb = nc.dram_tensor("onesb", [1, 128], BF16, kind="ExternalInput").ap()
    idf = nc.dram_tensor("idf", [128, 128], F32, kind="ExternalInput").ap()
    x4o = nc.dram_tensor("x4o", [2048, 1024], F32, kind="ExternalOutput").ap()
    x4r = x4o.rearrange("(t p) d -> p t d", p=128)

    with ExitStack() as stack:
        sched = Sched(nc, stack)
        sched.serial = _SERIAL
        sb = lambda n, s, d=F32: stack.enter_context(nc.sbuf_tensor(n, s, d))
        ps = lambda n, s, d=F32: stack.enter_context(nc.psum_tensor(n, s, d))
        sWdF = sb("sWdF", [128, 22, 1024], BF16)
        sHq = sb("sHq", [128, 22, 1024], BF16)
        sWs = [sb("sWsA", [128, 2816]), sb("sWsB", [128, 2816])]
        sWqT = sb("sWqT", [128, 2816], BF16)
        sPs = [sb("sPsA", [128, 1024], BF16), sb("sPsB", [128, 1024], BF16)]
        sQt = sb("sQt", [128, 2816])
        sQi = sb("sQi", [128, 2816], mybir.dt.int32)
        sCol = sb("sCol", [128, 6])
        sSd = sb("sSd", [128, 8])
        sRow = sb("sRow", [1, 1024])
        sRowP = sb("sRowP", [128, 1024], BF16)
        sRsb = sb("sRsb", [128, 1024])
        sXr = [sb("sXrA", [128, 1024]), sb("sXrB", [128, 1024])]
        sOut = [sb("sOutA", [128, 1024]), sb("sOutB", [128, 1024])]
        sM1 = sb("sM1", [128, 1]); sM1r = sb("sM1r", [1, 128])
        sIdf = sb("sIdf", [128, 128])
        sE0 = sb("sE0", [128, 128], BF16)
        sBn6 = sb("sBn6", [128, 12]); sBn2 = sb("sBn2", [128, 2]); sScr = sb("sScr", [128, 6])
        pA = ps("pA", [128, 512]); pB = ps("pB", [128, 512])
        pR = ps("pR", [128, 512]); pF = ps("pF", [128, 512])

        sched.dma("sync", sIdf[:], idf, sem="ld")
        z0 = sched.emit("gpsimd", lambda e: e.memset(sE0[:], 0.0), sem="zz")
        te = sched.dma("sync", sE0[0:1, :], onesb, sem="ld", waits=[z0])
        zM = sched.emit("gpsimd", lambda e: e.memset(sCol[:, 0:1], MAGIC), sem="zz")
        zR = sched.emit("gpsimd", lambda e: e.memset(sRowP[:], 0.0), sem="zz")
        sched.dma("sync", sM1r[0:1, 0:8], m2row, sem="ld")
        sched.dma("sync", sM1[0:1, 0:1], clipd, sem="ld")
        sched.set_flush(sE0[:], sE0[:], pF)
        rs_col, s_col, tcols = _emit_scalar_prep(nc, sched, sb, ps, sM1r[0:1, 0:8],
                                                 sM1[0:1, 0:1], sE0, pR, 128)

        wfree = [None, None]
        wq_toks = []
        trp_toks = []
        for j in range(8):
            par = j % 2
            tl = sched.dma("sync", sWs[par][:], wd[j * 128:(j + 1) * 128, :],
                           sem=f"w{par}", waits=[wfree[par]])
            am = sched.emit("vector", lambda e, par=par: e.tensor_reduce(
                out=sCol[:, 1:2], in_=sWs[par][:], axis=AX.X, op=ALU.max,
                apply_absolute_value=True), waits=[tl, wq_toks[-1] if wq_toks else None])
            rc = sched.emit("vector", lambda e: e.reciprocal(sCol[:, 2:3], sCol[:, 1:2]),
                            waits=[am])
            rc2 = sched.emit("vector", lambda e: e.tensor_scalar(
                out=sCol[:, 2:3], in0=sCol[:, 2:3], scalar1=127.0, scalar2=None,
                op0=ALU.mult), waits=[rc])
            sw = sched.emit("vector", lambda e, j=j: e.tensor_scalar(
                out=sSd[:, j:j + 1], in0=sCol[:, 1:2], scalar1=1.0 / 127.0, scalar2=None,
                op0=ALU.mult), waits=[rc2])
            q1 = sched.emit("scalar", lambda e, par=par: e.activation(
                out=sQt[:], in_=sWs[par][:], func=AF.Identity, scale=sCol[:, 2:3]),
                waits=[rc2, wq_toks[-1] if wq_toks else None])
            q2 = sched.emit("vector", lambda e: e.tensor_copy(sQi[:], sQt[:]), waits=[q1])
            q3 = sched.emit("vector", lambda e: e.tensor_scalar(
                out=sWqT[:], in0=sQi[:], scalar1=-128.0, scalar2=127.0,
                op0=ALU.max, op1=ALU.min), waits=[q2, trp_toks[-1] if trp_toks else None])
            wq_toks.append(q3)
            ttr = sched.dma("sync", sWdF[:, :, j * 128:(j + 1) * 128], sWqT[:],
                            waits=[q3], transpose=True, sem="tw")
            trp_toks.append(ttr)
            wfree[par] = ttr
        tw_all = ("tw", sched.cnt["tw"])

        rowp = []
        sched.emit("tensor", lambda e: e.drain(), waits=[])
        for j in range(8):
            ttp = sched.emit("tensor", lambda e, j=j: e.transpose(
                pR[0:1, 0:128], sSd[:, j:j + 1], sIdf[:]),
                waits=[("ld", sched.cnt["ld"]),
                       wq_toks[-1] if j == 0 else None, rowp[-1] if rowp else None])
            td = sched.emit("tensor", lambda e: e.drain(), waits=[])
            cp = sched.emit("vector", lambda e, j=j: e.tensor_copy(
                sRow[0:1, j * 128:(j + 1) * 128], pR[0:1, 0:128]), waits=[td])
            rowp.append(cp)
        trow = sched.emit("vector", lambda e: e.tensor_scalar(
            out=sRow[:], in0=sRow[:], scalar1=s_col[0:1, 0:1], scalar2=None,
            op0=ALU.mult), waits=rowp + [tcols])
        trp = sched.emit("vector", lambda e: e.tensor_copy(sRowP[0:1, :], sRow[:]),
                         waits=[trow, zR])
        rmm = sched.emit("tensor", lambda e: e.matmul(pR[:, 0:512], lhsT=sE0[:],
                                                      rhs=sRowP[:, 0:512], start=True,
                                                      stop=True), waits=[trp, te])
        rmm2 = sched.emit("tensor", lambda e: e.matmul(pA[:, 0:512], lhsT=sE0[:],
                                                       rhs=sRowP[:, 512:1024], start=True,
                                                       stop=True), waits=[trp, te])
        flr = sched.flush()
        tre = sched.emit("vector", lambda e: e.tensor_copy(sRsb[:, 0:512], pR[:, 0:512]),
                         waits=[flr])
        tre2 = sched.emit("vector", lambda e: e.tensor_copy(sRsb[:, 512:1024], pA[:, 0:512]),
                          waits=[flr, tre])

        out_free = [None, None]
        xr_free = [None, None]
        prev_bn = (None, None)
        hq_free = None
        bank_free = {0: None, 1: None}
        for h2 in range(2):
            pfree = [None, None]
            hq_toks = []
            for c in range(22):
                par = c % 2
                tl = sched.dma("sync", sPs[par][:],
                               po[c * 128:(c + 1) * 128, h2 * 1024:(h2 + 1) * 1024],
                               sem=f"p{par}", waits=[pfree[par]])
                a1 = sched.emit("scalar", lambda e, par=par: e.activation(
                    out=sQt[:, 0:1024], in_=sPs[par][:], func=AF.Identity, scale=rs_col),
                    waits=[tl, tcols, hq_toks[-1] if hq_toks else hq_free])
                a2 = sched.emit("vector", lambda e: e.tensor_copy(
                    sQi[:, 0:1024], sQt[:, 0:1024]), waits=[a1])
                a3 = sched.emit("vector", lambda e, c=c: e.tensor_scalar(
                    out=sHq[:, c, :], in0=sQi[:, 0:1024], scalar1=-128.0, scalar2=127.0,
                    op0=ALU.max, op1=ALU.min), waits=[a2])
                pfree[par] = a3
                hq_toks.append(a3)
            hq_all = hq_toks[-1]
            for tt_ in range(8):
                t = h2 * 8 + tt_
                buf = sOut[t % 2]
                for half, bank in ((0, pA), (1, pB)):
                    dd = [hq_all, tw_all, tre2] + ([bank_free[half]] if bank_free[half]
                                                   else [])
                    for c in range(22):
                        sched.emit("tensor", lambda e, c=c, tt_=tt_, half=half, bank=bank:
                                   e.matmul(bank[:, :],
                                            lhsT=sHq[:, c, tt_ * 128:(tt_ + 1) * 128],
                                            rhs=sWdF[:, c, half * 512:(half + 1) * 512],
                                            start=(c == 0), stop=(c == 21)), waits=dd)
                tfl = sched.flush()
                xrb = sXr[t % 2]
                txr = sched.dma("sync", xrb[:],
                                x3.rearrange("(t p) d -> p t d", p=128)[:, t, :],
                                sem=f"xr{t % 2}", waits=[xr_free[t % 2]])
                tr_toks = []
                for half, bank in ((0, pA), (1, pB)):
                    tr = sched.emit("vector", lambda e, half=half, bank=bank, buf=buf:
                                    e.tensor_tensor(out=buf[:, half * 512:(half + 1) * 512],
                                                    in0=bank[:, :],
                                                    in1=sRsb[:, half * 512:(half + 1) * 512],
                                                    op=ALU.mult),
                                    waits=[tfl] + ([out_free[t % 2]] if out_free[t % 2]
                                                   else []))
                    bank_free[half] = tr
                    tr2 = sched.emit("vector", lambda e, half=half, buf=buf, xrb=xrb:
                                     e.tensor_tensor(out=buf[:, half * 512:(half + 1) * 512],
                                                     in0=buf[:, half * 512:(half + 1) * 512],
                                                     in1=xrb[:, half * 512:(half + 1) * 512],
                                                     op=ALU.add), waits=[tr, txr])
                    tr_toks.append(tr2)
                xr_free[t % 2] = tr_toks[-1]
                tln, tbn = emit_layernorm(sched, buf[:], buf[:], sBn6, sBn2, sScr, 1e-5,
                                          deps=tr_toks, prev=prev_bn)
                prev_bn = tbn
                tst = sched.dma("sync", x4r[:, t, :], buf[:], waits=[tln],
                                sem="stA" if t % 2 == 0 else "stB")
                out_free[t % 2] = tst
            hq_free = ("tensor", sched.cnt["tensor"])

        sched.barrier("gpsimd", [("stA", sched.cnt["stA"]), ("stB", sched.cnt["stB"])])
        block = stack.enter_context(nc.Block())
        sched.build(block)
    return nc



_NCS = {}


def _get(name, builder):
    if name not in _NCS:
        _NCS[name] = builder()
    return _NCS[name]


def kernel(**inputs):
    inp = {k: np.asarray(v) for k, v in inputs.items()}
    x = np.ascontiguousarray(inp["L0_attn_x"], np.float32)
    cores = list(range(8))
    idf = np.eye(128, dtype=np.float32)
    onesb = np.ones((1, 128), BFDT)
    onesr = np.ones((1, 128), np.float32)

    nc = _get("A1", build_A1)
    im = []
    for c in cores:
        b, hg = c // 2, c % 2
        cols = slice(hg * 512, (hg + 1) * 512)
        im.append(host_A1_inputs(x[b], inp["L0_attn_Wq"][:, cols], inp["L0_attn_Wk"][:, cols],
                                 inp["L0_attn_Wv"][:, cols], inp["L0_attn_Wg"][:, cols],
                                 inp["L0_attn_g_attn"]))
    rA1 = run_bass_kernel_spmd(nc, im, core_ids=cores).results

    nc = _get("A2", build_A2)
    im = []
    for c in cores:
        b, hf = c // 2, c % 2
        oT_shard = np.concatenate([rA1[2 * b]["oTo"], rA1[2 * b + 1]["oTo"]], axis=0)
        oT_shard = oT_shard[:, hf * 2048:(hf + 1) * 2048]
        im.append(host_A2_inputs(x[b, hf * 2048:(hf + 1) * 2048], oT_shard,
                                 inp["L0_attn_Wo"], inp["L0_attn_w1"], inp["L0_attn_w2"],
                                 inp["L0_attn_w3"], inp["L0_attn_g_ffn"]))
    rA2 = run_bass_kernel_spmd(nc, im, core_ids=cores).results

    nc = _get("B", build_B)
    im = [host_B_inputs(rA2[c]["x1o"], inp["L0_ffn_G1_R"], inp["L0_ffn_G2_R"],
                        inp["L0_ffn_G1_I"], inp["L0_ffn_G2_I"],
                        inp["L0_ffn_bias_real"], inp["L0_ffn_bias_imag"]) for c in cores]
    rB = run_bass_kernel_spmd(nc, im, core_ids=cores).results

    nc = _get("C", build_C)
    im = []
    for c in cores:
        b, hf = c // 2, c % 2
        own = rB[c]["x2"]
        other = rB[2 * b + (1 - hf)]["x2"]
        rot = np.concatenate([own, other], axis=0)
        im.append(host_C_inputs(rot, inp["L1_attn_Wq"], inp["L1_attn_bq"],
                                inp["L1_attn_Wk"], inp["L1_attn_bk"],
                                inp["L1_attn_Wv"], inp["L1_attn_bv"],
                                inp["L1_attn_Wo"], inp["L1_attn_bo"]))
    rC = run_bass_kernel_spmd(nc, im, core_ids=cores).results

    nc = _get("Cm", build_Cm)
    im = [{"x3": rC[c]["x3o"], "idf": idf} for c in cores]
    rCm = run_bass_kernel_spmd(nc, im, core_ids=cores).results
    m1row = np.concatenate([rCm[c]["m1o"] for c in cores], axis=1)

    nc = _get("D1", build_D1)
    im = [{"x3": rC[c]["x3o"], "m1row": m1row,
           "clipx": np.asarray(inp["L1_ffn_clip_x"], np.float32).reshape(1, 1),
           "wg": np.ascontiguousarray(inp["L1_ffn_w_gate"], np.float32),
           "wu": np.ascontiguousarray(inp["L1_ffn_w_up"], np.float32),
           "onesb": onesb, "idf": idf} for c in cores]
    rD1 = run_bass_kernel_spmd(nc, im, core_ids=cores).results
    m2row = np.concatenate([rD1[c]["m2o"] for c in cores], axis=1)

    nc = _get("D2", build_D2)
    im = [{"po": rD1[c]["po"], "m2row": m2row,
           "clipd": np.asarray(inp["L1_ffn_clip_down"], np.float32).reshape(1, 1),
           "wd": np.ascontiguousarray(inp["L1_ffn_w_down"], np.float32),
           "x3": rC[c]["x3o"], "onesb": onesb, "idf": idf} for c in cores]
    rD2 = run_bass_kernel_spmd(nc, im, core_ids=cores).results

    out = np.zeros((4, 4096, 1024), np.float32)
    for c in cores:
        b, hf = c // 2, c % 2
        out[b, hf * 2048:(hf + 1) * 2048, :] = rD2[c]["x4o"]
    return out
